# version 18
# baseline (speedup 1.0000x reference)
import numpy as np
from contextlib import ExitStack

import concourse.bass as bass
import concourse.mybir as mybir
from concourse.bass_utils import run_bass_kernel_spmd


F32 = mybir.dt.float32
BF16 = mybir.dt.bfloat16
I32 = mybir.dt.int32
AF = mybir.ActivationFunctionType
ALU = mybir.AluOpType
AX = mybir.AxisListType

P = 128
D = 1024
NCH = D // P
H = 16
HD = 64
L0_HID = 3072
L1_HID = 4096
LN_EPS = 1e-5

ENGS = ("tensor", "vector", "scalar", "gpsimd", "sync")


class Builder:
    def __init__(self, T=2048, last_stage=99, dbg=(), p2_sub=9):
        self.T = T
        self.last_stage = last_stage
        self.p2_sub = p2_sub
        self.dbg_names = set(dbg)
        self.nc = bass.Bass("TRN2", target_bir_lowering=False, debug=False)
        self.streams = {e: [] for e in ENGS}
        self.cnt = {}
        self.sem = {}
        self.stack = ExitStack()

    def mksem(self, name):
        self.sem[name] = self.stack.enter_context(self.nc.semaphore(name))
        self.cnt[name] = 0

    def sbuf(self, name, shape, dtype):
        return self.nc.alloc_sbuf_tensor("sb_" + name, shape, dtype)

    def const(self, val):
        if val in self.consts:
            return
        i = len(self.consts)
        assert i < 16
        ap = self.cpool[:, i:i + 1]
        self.consts[val] = ap
        self.emit("vector", lambda e, ap=ap, val=val: e.memset(ap, val),
                  inc="ve")
        self.nc.const_aps.register(F32, val, ap)

    OWN_SEM = {"vector": "ve", "scalar": "se", "gpsimd": "ge"}

    def emit(self, eng, fn, inc=None, amt=1):
        if inc is None:
            inc = self.OWN_SEM.get(eng)
        if inc is not None:
            self.cnt[inc] += amt
            sem = self.sem[inc]

            def wrapped(e, fn=fn, sem=sem, amt=amt):
                fn(e).then_inc(sem, amt)
            self.streams[eng].append(wrapped)
            return self.cnt[inc]
        self.streams[eng].append(fn)
        return None

    def wait(self, eng, sname, val):
        sem = self.sem[sname]
        self.streams[eng].append(lambda e, sem=sem, val=val: e.wait_ge(sem, val))

    def wait_all(self, eng, sname):
        self.wait(eng, sname, self.cnt[sname])

    def drain(self, eng):
        own = self.OWN_SEM.get(eng)
        if own is not None:
            self.wait(eng, own, self.cnt[own])
        else:
            self.streams[eng].append(lambda e: e.drain())

    def milestone(self, eng, sname):
        if self.OWN_SEM.get(eng) == sname and self.cnt[sname] > 0:
            self.wait(eng, sname, self.cnt[sname])
            return self.cnt[sname]
        return self.emit(eng, lambda e: e.drain(), inc=sname)

    def dma(self, queue, out_ap, in_ap, sname, slow_ok=False):
        self.cnt[sname] += 16
        sem = self.sem[sname]

        def fn(e, out_ap=out_ap, in_ap=in_ap, sem=sem):
            e.dma_start(out_ap, in_ap,
                        allow_slow_non_contiguous=slow_ok).then_inc(sem, 16)
        self.streams[queue].append(fn)
        return self.cnt[sname]

    def finalize(self):
        nc = self.nc
        with nc.Block() as block:
            for eng in ENGS:
                dec = getattr(block, eng)
                stream = self.streams[eng]

                def body(e, stream=stream):
                    for fn in stream:
                        fn(e)
                dec(body)
        self.stack.close()
        return nc

    def dbg_dump(self, name, tiles, src_sems=()):
        if name not in self.dbg_names:
            return
        aps = [(t[:] if not isinstance(t, bass.AP) else t) for t in tiles]
        n = len(aps)
        cols = aps[0].shape[-1]
        d = self.nc.dram_tensor("dbg_" + name, [n * P, cols], F32,
                                kind="ExternalOutput").ap()
        for s in src_sems:
            self.wait_all("sync", s)
            self.wait_all("gpsimd", s)
        for i, ap in enumerate(aps):
            if ap.dtype == F32:
                self.dma("sync", d[i * P:(i + 1) * P, :], ap, "dbg2")
            else:
                self.dma("gpsimd", d[i * P:(i + 1) * P, :], ap, "dbg")
        for en in ("vector", "scalar", "tensor", "sync", "gpsimd"):
            self.wait(en, "dbg", self.cnt["dbg"])
            self.wait(en, "dbg2", self.cnt["dbg2"])

    def bank(self, i, cols=512, off=0, p0=0, p1=P):
        return self.psum[p0:p1, 512 * i + off: 512 * i + off + cols]

    def use_bank(self, i):
        sname, cntv = self.bank_free[i]
        if cntv > 0:
            self.wait("tensor", sname, cntv)

    def release_bank(self, i, sname, cntv):
        self.bank_free[i] = (sname, cntv)

    def build(self):
        nc, T = self.nc, self.T
        TT = min(512, T)
        NTT = T // TT
        self.NTT, self.TT = NTT, TT
        self.NTC = T // P

        for s in ("sy", "gd", "gv", "bro", "oA", "oB", "wA", "wB",
                  "sA", "sB", "dA", "dB", "eA", "eB", "gA", "gB", "sb2",
                  "dbg", "dbg2", "te", "ve", "se", "ge", "so", "xq",
                  "pA", "pB"):
            self.mksem(s)

        dt = {}
        dt["x"] = nc.dram_tensor("x", [D, T], F32, kind="ExternalInput").ap()
        dt["bias6"] = nc.dram_tensor("bias6", [D, 6], F32, kind="ExternalInput").ap()
        for w in ("wqT", "wkT", "wvT", "woT"):
            dt[w] = nc.dram_tensor(w, [D, D], F32, kind="ExternalInput").ap()
        dt["wg0T"] = nc.dram_tensor("wg0T", [D, L0_HID], I32, kind="ExternalInput").ap()
        dt["wu0T"] = nc.dram_tensor("wu0T", [D, L0_HID], I32, kind="ExternalInput").ap()
        dt["wd0T"] = nc.dram_tensor("wd0T", [L0_HID, D], I32, kind="ExternalInput").ap()
        dt["wqkvT"] = nc.dram_tensor("wqkvT", [D, 3 * D], F32, kind="ExternalInput").ap()
        dt["woutT"] = nc.dram_tensor("woutT", [D, D], F32, kind="ExternalInput").ap()
        dt["wg1T"] = nc.dram_tensor("wg1T", [D, L1_HID], F32, kind="ExternalInput").ap()
        dt["wu1T"] = nc.dram_tensor("wu1T", [D, L1_HID], F32, kind="ExternalInput").ap()
        dt["wd1T"] = nc.dram_tensor("wd1T", [L1_HID, D], F32, kind="ExternalInput").ap()
        self.dt = dt
        self.out = nc.dram_tensor("out", [D, T], F32, kind="ExternalOutput").ap()
        self.x1_sp = nc.dram_tensor("x1_sp", [D, T], F32, kind="Internal").ap()
        self.xq_sp = nc.dram_tensor("xq_sp", [D, T], BF16, kind="Internal").ap()
        self.s1_sp = nc.dram_tensor("s1_sp", [D, T], BF16, kind="Internal").ap()
        self.wgq_sc = nc.dram_tensor("wgq_sc", [D, L1_HID], BF16, kind="Internal").ap()
        self.wuq_sc = nc.dram_tensor("wuq_sc", [D, L1_HID], BF16, kind="Internal").ap()
        self.wdq_sc = nc.dram_tensor("wdq_sc", [L1_HID, D], BF16, kind="Internal").ap()

        sb = self.sbuf
        self.XN_O = 0
        self.W1_O = 16384
        self.QT_O = 32768
        self.EX_O = 49152
        self.WVS_O = 65536
        self.RG_O = 69632
        self.BF = sb("BF", [P, 71936], BF16)
        self.FW = sb("FW", [P, 8192], F32)
        self.MU = sb("MU", [P, max(T, 2048)], F32)
        self.RS = sb("RS", [P, max(T, 512)], F32)
        self.SXR = sb("SXR", [P, T], F32)
        self.SCR = sb("SCR", [P, 512], F32)
        self.STAT = sb("STAT", [P, 1024], F32)
        self.SMALL = sb("SMALL", [P, 16], F32)
        self.cpool = sb("cpool", [P, 16], F32)
        self.consts = {}
        self.ones = sb("ones", [P, 128], BF16)
        self.maskT2 = sb("maskT2", [P, 2 * P], BF16)
        self.bias6 = sb("bias6c", [P, 6 * NCH], F32)
        self.Sst = [sb(f"Sst{i}", [P, 8 * HD], BF16) for i in range(2)]
        self.IDF = sb("IDF", [P, P], F32)
        self.ONEC = sb("ONEC", [1, P], F32)
        self.psum = self.stack.enter_context(nc.psum_tensor("ps", [P, 4096], F32))
        self.bank_free = [("te", 0)] * 8

        self.xn = [self.bf(self.XN_O + i * T, T) for i in range(NCH)]
        if "upd" in self.dbg_names:
            self.dbg_upd = nc.dram_tensor("dbg_upd", [D, T], F32,
                                          kind="ExternalOutput").ap()
        self.win_te = {}
        self.win_ve = {}

        self.marks = {}

        def mark(nm):
            self.marks[nm] = {e: len(self.streams[e]) for e in ENGS}
        self.P0_setup()
        mark("P1")
        self.P1_preln0()
        if self.last_stage >= 2:
            mark("P2")
            self.P2_l0_attn()
        if self.last_stage >= 3:
            mark("P3")
            self.P3_innerln_wo()
        if self.last_stage >= 4:
            mark("P4")
            self.P4_l0_ffn()
        self.p5_tasks = []
        if self.last_stage >= 6:
            mark("P6")
            self.P6_preln1()
        if self.last_stage >= 7:
            if self.last_stage >= 5:
                self.p5_tasks = self.make_p5_tasks()
            mark("P7")
            self.P7_qkv()
        if self.last_stage >= 8:
            mark("P8")
            self.P8_l1_attn()
            mark("P8b")
            self.P8b_wout()
        if self.last_stage >= 5:
            if self.last_stage < 7:
                self.p5_tasks = self.make_p5_tasks()
            mark("P5")
            self.P5_wquant()
        if self.last_stage >= 9:
            mark("P9")
            self.P9_l1_ffn()
        mark("END")
        if self.last_stage < 9 and self.last_stage >= 4:
            for s in ("sy", "ve", "se", "so"):
                self.wait_all("sync", s)
            for i in range(NCH):
                self.dma("sync", self.out[i * P:(i + 1) * P, :],
                         self.x1_sp[i * P:(i + 1) * P, :], "so")
        elif self.last_stage < 4:
            self.wait_all("sync", "ve")
            for i in range(NCH):
                self.dma("sync", self.out[i * P:(i + 1) * P, :],
                         self.MU[:, 0:T], "so")
        self.wait_all("sync", "so")
        self.wait_all("sync", "dbg")
        self.wait_all("sync", "dbg2")
        self.wait_all("sync", "sy")
        self.wait_all("sync", "oA")
        self.wait_all("sync", "oB")
        return self.finalize()

    def bf(self, off, n):
        return self.BF[:, off:off + n]

    def bias_ap(self, j, m):
        return self.bias6[:, j * NCH + m: j * NCH + m + 1]

    def P0_setup(self):
        dt = self.dt
        self.const(LN_EPS)
        self.const(0.125)
        self.const(1.0 / 127.0)
        self.const(128.5)
        self.const(-128.0)
        self.emit("vector", lambda e: e.memset(self.ones[:], 1.0))
        self.emit("gpsimd", lambda e: e.memset(self.maskT2[:], 1.0))
        self.emit("gpsimd", lambda e: e.memset(self.Sst[0][:], 0.0))
        self.emit("gpsimd", lambda e: e.memset(self.Sst[1][:], 0.0))
        self.drain("gpsimd")
        self.emit("gpsimd", lambda e: e.affine_select(
            self.maskT2[:], self.maskT2[:], pattern=[[0, 2], [1, 128]],
            compare_op=ALU.is_ge, fill=0.0, base=0, channel_multiplier=-1),
            inc="ge")
        ld = self.dma("sync", self.bias6[:].rearrange("p (j m) -> p j m", m=NCH),
                      dt["bias6"].rearrange("(m p) j -> p j m", p=P), "sy",
                      slow_ok=True)
        self.wait("vector", "sy", ld)
        idf = self.IDF[:]
        self.emit("gpsimd", lambda e: e.memset(idf, 1.0))
        self.drain("gpsimd")
        self.emit("gpsimd", lambda e: e.affine_select(
            idf, idf, pattern=[[-1, 128]], compare_op=ALU.is_equal,
            fill=0.0, base=0, channel_multiplier=1), inc="ge")
        self.emit("vector", lambda e: e.memset(self.ONEC[:], 1.0), inc="ve")
        v_t = self.milestone("vector", "ve")
        self.wait("tensor", "ve", v_t)
        self.wait("tensor", "ge", self.cnt["ge"])
        te_t = self.emit("tensor", lambda e: e.transpose(
            self.psum[0:16, 0:128], self.bias6[:, NCH:3 * NCH], idf),
            inc="te")
        self.wait("vector", "te", te_t)
        bT = self.STAT[0:16, 0:128]
        v_t = self.emit("vector", lambda e, bT=bT: e.tensor_copy(
            bT, self.psum[0:16, 0:128]), inc="ve")
        self.wait("gpsimd", "ve", v_t)
        self.brow = self.bf(self.RG_O, 2048)
        for j in range(2):
            self.dma("gpsimd",
                     self.brow[0:1, j * D:(j + 1) * D].rearrange(
                         "o (m f) -> o m f", m=NCH),
                     bT[j * NCH:(j + 1) * NCH, :], "bro")
        self.wait_all("vector", "bro")
        self.wait_all("scalar", "bro")

    def ln_stats_from_windows(self, src_dram, write_xn):
        T, TT, NTT = self.T, self.TT, self.NTT
        sq = self.bf(self.QT_O, 512)[:, 0:self.TT]
        wins = [[self.FW[:, i * TT:(i + 1) * TT] for i in range(NCH)],
                [self.FW[:, 4096 + i * TT:4096 + (i + 1) * TT] for i in range(NCH)]]
        ld_t = {}
        self.wait_all("sync", "ve")
        self.wait_all("sync", "te")

        def issue(tt):
            w = wins[tt % 2]
            sm = "wA" if tt % 2 == 0 else "wB"
            for i in range(NCH):
                t = self.dma("sync", w[i],
                             src_dram[i * P:(i + 1) * P, tt * TT:(tt + 1) * TT],
                             sm)
            ld_t[tt] = (sm, t)
        for tt in range(min(2, NTT)):
            issue(tt)
        sq8 = [self.bf(self.QT_O + k * 512, 512)[:, 0:TT] for k in range(NCH)]
        for tt in range(NTT):
            w = wins[tt % 2]
            c0 = tt * TT
            self.wait("vector", ld_t[tt][0], ld_t[tt][1])
            if tt > 0:
                self.wait("vector", "te", self.win_te[tt - 1])
            self.use_bank(6)
            self.use_bank(7)
            for k in range(NCH):
                if write_xn:
                    self.emit("vector", lambda e, k=k, c0=c0, w=w: e.tensor_copy(
                        self.xn[k][:, c0:c0 + TT], w[k]), inc="ve")
                self.emit("vector", lambda e, k=k, w=w: e.tensor_tensor(
                    sq8[k], w[k], w[k], ALU.mult), inc="ve")
            v_t = self.milestone("vector", "ve")
            self.wait("tensor", "ve", v_t)
            for k in range(NCH):
                last = (k == NCH - 1)
                self.emit("tensor", lambda e, k=k, c0=c0, last=last: e.matmul(
                    self.bank(6, TT), self.ones[:, 0:128],
                    self.xn[k][:, c0:c0 + TT],
                    start=(k == 0), stop=last), inc="te" if last else None)
                self.emit("tensor", lambda e, k=k, last=last: e.matmul(
                    self.bank(7, TT), self.ones[:, 0:128], sq8[k],
                    start=(k == 0), stop=last), inc="te" if last else None)
            te_t = self.cnt["te"]
            self.win_te[tt] = te_t
            if tt + 2 < NTT:
                self.wait("sync", "te", te_t)
                issue(tt + 2)
            self._ln_tail(te_t, c0, TT)
        self.drain("vector")

    def ln_stats_from_tiles(self, tiles):
        T, TT, NTT = self.T, self.TT, self.NTT
        sq8 = [self.bf(self.QT_O + k * 512, 512)[:, 0:self.TT]
               for k in range(NCH)]
        prev_te = 0
        for tt in range(NTT):
            c0 = tt * TT
            self.use_bank(6)
            self.use_bank(7)
            if prev_te > 0:
                self.wait("vector", "te", prev_te)
            for k in range(NCH):
                self.emit("vector", lambda e, k=k, c0=c0: e.tensor_tensor(
                    sq8[k], tiles[k][:, c0:c0 + TT], tiles[k][:, c0:c0 + TT],
                    ALU.mult), inc="ve")
            v_t = self.milestone("vector", "ve")
            self.wait("tensor", "ve", v_t)
            for k in range(NCH):
                last = (k == NCH - 1)
                self.emit("tensor", lambda e, k=k, c0=c0, last=last: e.matmul(
                    self.bank(6, TT), self.ones[:, 0:128],
                    tiles[k][:, c0:c0 + TT],
                    start=(k == 0), stop=last), inc="te" if last else None)
                self.emit("tensor", lambda e, k=k, last=last: e.matmul(
                    self.bank(7, TT), self.ones[:, 0:128], sq8[k],
                    start=(k == 0), stop=last), inc="te" if last else None)
            prev_te = self.cnt["te"]
            self._ln_tail(prev_te, c0, TT)
        self.drain("vector")

    def _ln_tail(self, te_t, c0, TTc):
        st = self.STAT
        mu = self.MU[:, c0:c0 + TTc]
        rstd = self.RS[:, c0:c0 + TTc]
        self.wait("vector", "te", te_t)
        self.emit("vector", lambda e, mu=mu: e.tensor_scalar_mul(
            mu, self.bank(6, TTc), 1.0 / D), inc="ve")
        self.emit("vector", lambda e: e.tensor_scalar_mul(
            st[:, 0:TTc], self.bank(7, TTc), 1.0 / D), inc="ve")
        self.drain("vector")
        self.emit("vector", lambda e, mu=mu: e.tensor_tensor(
            st[:, TTc:2 * TTc], mu, mu, ALU.mult), inc="ve")
        self.drain("vector")
        self.emit("vector", lambda e: e.tensor_tensor(
            st[:, 0:TTc], st[:, 0:TTc], st[:, TTc:2 * TTc], ALU.subtract),
            inc="ve")
        v_t = self.milestone("vector", "ve")
        self.release_bank(6, "ve", v_t)
        self.release_bank(7, "ve", v_t)
        self.wait("scalar", "ve", v_t)
        s_t = self.emit("scalar", lambda e: e.activation(
            st[:, TTc:2 * TTc], st[:, 0:TTc], AF.Sqrt, bias=LN_EPS), inc="se")
        self.wait("vector", "se", s_t)
        self.emit("vector", lambda e, rstd=rstd: e.reciprocal(
            rstd, st[:, TTc:2 * TTc]), inc="ve")
        self.drain("vector")

    def ln_apply_windows(self, src_dram, dst_tiles, absmax=False):
        T, TT, NTT = self.T, self.TT, self.NTT
        wins = [[self.FW[:, i * TT:(i + 1) * TT] for i in range(NCH)],
                [self.FW[:, 4096 + i * TT:4096 + (i + 1) * TT] for i in range(NCH)]]
        scr = self.SCR[:, 0:TT]
        if absmax:
            self.emit("vector", lambda e: e.memset(self.SXR[:], 0.0), inc="ve")
            self.drain("vector")
        ld_t = {}
        self.wait_all("sync", "ve")
        self.wait_all("sync", "te")

        def issue(tt):
            w = wins[tt % 2]
            sm = "wA" if tt % 2 == 0 else "wB"
            for i in range(NCH):
                t = self.dma("sync", w[i],
                             src_dram[i * P:(i + 1) * P, tt * TT:(tt + 1) * TT],
                             sm)
            ld_t[tt] = (sm, t)
        for tt in range(min(2, NTT)):
            issue(tt)
        for tt in range(NTT):
            w = wins[tt % 2]
            c0 = tt * TT
            self.wait("vector", ld_t[tt][0], ld_t[tt][1])
            for i in range(NCH):
                self.emit("vector", lambda e, i=i, w=w, c0=c0: e.tensor_tensor(
                    scr, w[i], self.MU[:, c0:c0 + TT], ALU.subtract), inc="ve")
                self.drain("vector")
                self.emit("vector", lambda e, c0=c0: e.tensor_tensor(
                    scr, scr, self.RS[:, c0:c0 + TT], ALU.mult), inc="ve")
                self.drain("vector")
                self.emit("vector", lambda e, i=i, c0=c0: e.tensor_copy(
                    dst_tiles[i][:, c0:c0 + TT], scr), inc="ve")
                if absmax:
                    self.drain("vector")
                    self.emit("vector", lambda e: e.scalar_tensor_tensor(
                        scr, scr, -1.0, scr, ALU.mult, ALU.max), inc="ve")
                    self.drain("vector")
                    self.emit("vector", lambda e, c0=c0: e.tensor_tensor(
                        self.SXR[:, c0:c0 + TT], self.SXR[:, c0:c0 + TT],
                        scr, ALU.max), inc="ve")
                self.drain("vector")
            self.win_ve[tt] = self.cnt["ve"]
            if tt + 2 < NTT:
                self.wait("sync", "ve", self.cnt["ve"])
                issue(tt + 2)
        if absmax:
            self.bcast_ptok_max()
        self.drain("vector")

    def bcast_ptok_max(self):
        T, TT, NTT = self.T, self.TT, self.NTT
        NG = T // P
        v_t = self.milestone("vector", "ve")
        self.wait("tensor", "ve", v_t)
        for g in range(NG):
            self.use_bank(6)
            te_t = self.emit("tensor", lambda e, g=g: e.transpose(
                self.bank(6, 128), self.SXR[:, g * P:(g + 1) * P], self.IDF[:]),
                inc="te")
            self.wait("vector", "te", te_t)
            v_t = self.emit("vector", lambda e, g=g: e.reduce_max(
                self.STAT[:, g:g + 1], self.bank(6, 128), axis=AX.X), inc="ve")
            self.drain("vector")
            self.release_bank(6, "ve", v_t)
        v_t = self.milestone("vector", "ve")
        self.wait("tensor", "ve", v_t)
        self.use_bank(6)
        te_t = self.emit("tensor", lambda e: e.transpose(
            self.psum[0:NG, 512 * 6:512 * 6 + P],
            self.STAT[:, 0:NG], self.IDF[:]), inc="te")
        self.wait("vector", "te", te_t)
        v_t = self.emit("vector", lambda e: e.tensor_copy(
            self.STAT[0:NG, 256:256 + P],
            self.psum[0:NG, 512 * 6:512 * 6 + P]), inc="ve")
        self.drain("vector")
        self.release_bank(6, "ve", v_t)
        self.wait("sync", "ve", v_t)
        d_t = self.dma("sync",
                       self.SXR[0:1, 0:T].rearrange("o (g f) -> o g f", g=NG),
                       self.STAT[0:NG, 256:256 + P], "sb2")
        self.wait("vector", "sb2", d_t)
        v_t = self.emit("vector", lambda e: e.tensor_single_scalar(
            self.SXR[0:1, 0:T], self.SXR[0:1, 0:T], 1e-5, ALU.max), inc="ve")
        self.drain("vector")
        v_t = self.milestone("vector", "ve")
        self.wait("tensor", "ve", v_t)
        for tt in range(NTT):
            c0 = tt * TT
            self.use_bank(7)
            te_t = self.emit("tensor", lambda e, c0=c0: e.matmul(
                self.bank(7, TT), self.ONEC[0:1, :],
                self.SXR[0:1, c0:c0 + TT], start=True, stop=True), inc="te")
            self.wait("vector", "te", te_t)
            v_t = self.emit("vector", lambda e, c0=c0: e.tensor_copy(
                self.SXR[:, c0:c0 + TT], self.bank(7, TT)), inc="ve")
            self.drain("vector")
            self.release_bank(7, "ve", v_t)

    def P1_preln0(self):
        self.ln_stats_from_windows(self.dt["x"], write_xn=True)
        self.ln_apply_windows(self.dt["x"], self.xn)
        self.dbg_dump("xn", list(self.xn), src_sems=("ve",))

    def proj_fm(self, lhsT_fn, rhs_tiles, rhs_c0, N, out_writer, banks,
                nk=NCH, m_range=None, bias_row=None):
        if m_range is None:
            m_range = range(NCH)
        bi = 0
        for m in m_range:
            b = banks[bi % len(banks)]
            bi += 1
            self.use_bank(b)
            for k in range(nk):
                last = (k == nk - 1) and bias_row is None
                self.emit("tensor",
                          lambda e, k=k, m=m, b=b, last=last: e.matmul(
                              self.bank(b, N), lhsT_fn(k, m),
                              rhs_tiles[k][:, rhs_c0:rhs_c0 + N],
                              start=(k == 0), stop=last),
                          inc="te" if last else None)
            if bias_row is not None:
                self.emit("tensor", lambda e, m=m, b=b: e.matmul(
                    self.bank(b, N), bias_row(m), self.ones[0:1, 0:N],
                    start=False, stop=True), inc="te")
            te_t = self.cnt["te"]
            sem, cnt = out_writer(m, self.bank(b, N), te_t)
            self.release_bank(b, sem, cnt)

    def elu1_writer(self, dst_tile_fn, bias_j):
        scr = self.SCR[:, 0:512]

        def w(m, ps, te_t):
            TTc = ps.shape[-1]
            self.wait("vector", "te", te_t)
            pv = getattr(self, "_elu_prev_v", 0)
            if pv > 0:
                self.wait("vector", "se", self.cnt["se"])
            bap = self.bias_ap(bias_j, m)
            self.emit("vector", lambda e, bap=bap: e.tensor_scalar(
                scr[:, 0:TTc], ps, bap, 0.0, ALU.add, ALU.min), inc="ve")
            self.emit("vector", lambda e, bap=bap: e.tensor_scalar(
                self.STAT[:, 0:TTc], ps, bap, 0.0, ALU.add, ALU.max),
                inc="ve")
            v_m = self.milestone("vector", "ve")
            self.wait("scalar", "ve", v_m)
            s_t = self.emit("scalar", lambda e: e.activation(
                scr[:, 0:TTc], scr[:, 0:TTc], AF.Exp), inc="se")
            self.wait("vector", "se", s_t)
            v_t = self.emit("vector", lambda e: e.tensor_tensor(
                dst_tile_fn(m)[:, 0:TTc], scr[:, 0:TTc], self.STAT[:, 0:TTc],
                ALU.add), inc="ve")
            self.drain("vector")
            self._elu_prev_v = v_t
            return ("ve", v_t)
        return w

    def P2_l0_attn(self):
        self._elu_prev_v = self.cnt["ve"]
        self.wait("tensor", "ve", self.cnt["ve"])
        self.wait("tensor", "se", self.cnt["se"])
        for q_ in ("gpsimd", "sync"):
            self.wait(q_, "te", self.cnt["te"])
            self.wait(q_, "ve", self.cnt["ve"])
            self.wait(q_, "se", self.cnt["se"])
        T, TT, NTT = self.T, self.TT, self.NTT
        dt = self.dt
        wq = self.bf(self.W1_O, 8192)
        wk = self.bf(self.W1_O + 8192, 8192)
        for name, dst in (("wqT", wq), ("wkT", wk)):
            ldw = self.dma("gpsimd", dst.rearrange("p (k m) -> p k m", k=NCH),
                           dt[name].rearrange("(k p) m -> p k m", p=P), "gd")
        wvs = self.bf(self.WVS_O, 4096)
        q2 = self.bf(self.QT_O, 4096)
        k2f = self.bf(self.QT_O + 4096, 4096)
        k2t = self.bf(self.QT_O + 8192, 4096)
        v2 = self.bf(self.QT_O + 12288, 4096)
        ab = [self.bf(self.RG_O + 2048, 256), self.bf(self.RG_O + 2048, 256)]
        o_t = [self.bf(self.EX_O + p * T, T) for p in range(NCH)]
        wt = lambda w, k, m: w[:, k * 1024 + m * 128: k * 1024 + (m + 1) * 128]

        self.wait("tensor", "gd", self.cnt["gd"])
        self.wait_all("tensor", "bro")
        nchunks = TT // P
        wvs_cons = 0
        for tt in range(NTT):
            c0 = tt * TT
            self.proj_fm(lambda k, m: wt(wq, k, m), self.xn, c0, TT,
                         self.elu1_writer(
                             lambda m: q2[:, m * TT:(m + 1) * TT], 0),
                         banks=[4, 5])
            self.proj_fm(lambda k, m: wt(wk, k, m), self.xn, c0, TT,
                         self.elu1_writer(
                             lambda m: k2f[:, m * TT:(m + 1) * TT], 1),
                         banks=[4, 5])
            for tj in range(nchunks):
                for n in range(2):
                    b = [4, 5][(tj * 2 + n) % 2]
                    self.use_bank(b)
                    for k in range(NCH):
                        self.emit("tensor", lambda e, k=k, tj=tj, n=n, b=b,
                                  c0=c0:
                                  e.matmul(self.bank(b, 512),
                                           self.xn[k][:, c0 + tj * P:c0 + (tj + 1) * P],
                                           wk[:, k * 1024 + n * 512:
                                              k * 1024 + (n + 1) * 512],
                                           start=(k == 0), stop=False))
                    self.emit("tensor", lambda e, tj=tj, n=n, b=b: e.matmul(
                        self.bank(b, 512), self.ones[0:1, 0:128],
                        self.brow[0:1, n * 512:(n + 1) * 512],
                        start=False, stop=True), inc="te")
                    te_t = self.cnt["te"]
                    scr = self.SCR[:, 0:512]
                    self.wait("vector", "te", te_t)
                    self.wait("vector", "se", self.cnt["se"])
                    self.emit("vector", lambda e, b=b: e.tensor_single_scalar(
                        scr, self.bank(b, 512), 0.0, ALU.min), inc="ve")
                    self.emit("vector", lambda e, b=b: e.tensor_single_scalar(
                        self.STAT[:, 0:512], self.bank(b, 512), 0.0, ALU.max),
                        inc="ve")
                    v_m = self.milestone("vector", "ve")
                    self.wait("scalar", "ve", v_m)
                    s_t = self.emit("scalar", lambda e: e.activation(
                        scr, scr, AF.Exp), inc="se")
                    self.wait("vector", "se", s_t)
                    v_t = self.emit("vector", lambda e, tj=tj, n=n: e.tensor_tensor(
                        k2t[:, tj * 1024 + n * 512: tj * 1024 + (n + 1) * 512],
                        scr, self.STAT[:, 0:512], ALU.add), inc="ve")
                    self.drain("vector")
                    self._elu_prev_v = v_t
                    self.release_bank(b, "ve", v_t)
            for n in range(2):
                if wvs_cons > 0:
                    self.wait("gpsimd", "te", wvs_cons)
                ldv = self.dma("gpsimd",
                               wvs.rearrange("p (k m) -> p k m", k=NCH),
                               dt["wvT"][:, n * 512:(n + 1) * 512].rearrange(
                                   "(k p) m -> p k m", p=P), "gv")
                self.wait("tensor", "gv", ldv)
                for tj in range(nchunks):
                    b = [4, 5][tj % 2]
                    self.use_bank(b)
                    for k in range(NCH):
                        self.emit("tensor", lambda e, k=k, tj=tj, b=b, c0=c0:
                                  e.matmul(self.bank(b, 512),
                                           self.xn[k][:, c0 + tj * P:c0 + (tj + 1) * P],
                                           wvs[:, k * 512:(k + 1) * 512],
                                           start=(k == 0), stop=False))
                    self.emit("tensor", lambda e, n=n, b=b: e.matmul(
                        self.bank(b, 512), self.ones[0:1, 0:128],
                        self.brow[0:1, 1024 + n * 512:1024 + (n + 1) * 512],
                        start=False, stop=True), inc="te")
                    te_t = self.cnt["te"]
                    if tj == nchunks - 1:
                        wvs_cons = te_t
                    self.wait("vector", "te", te_t)
                    v_t = self.emit("vector", lambda e, tj=tj, n=n, b=b:
                                    e.tensor_copy(
                                        v2[:, tj * 1024 + n * 512:
                                           tj * 1024 + (n + 1) * 512],
                                        self.bank(b, 512)), inc="ve")
                    self.drain("vector")
                    self.release_bank(b, "ve", v_t)
            if tt == 0:
                self.dbg_dump("q0f", [q2[:, m * TT:(m + 1) * TT]
                                      for m in range(NCH)], src_sems=("ve",))
                self.dbg_dump("k0f", [k2f[:, m * TT:(m + 1) * TT]
                                      for m in range(NCH)], src_sems=("ve",))
                self.dbg_dump("k0t", [k2t[:, j * 1024:(j + 1) * 1024]
                                      for j in range(nchunks)],
                              src_sems=("ve",))
                self.dbg_dump("v0t", [v2[:, j * 1024:(j + 1) * 1024]
                                      for j in range(nchunks)],
                              src_sems=("ve",))
            if self.p2_sub < 2:
                continue
            for cl in range(nchunks):
                c = tt * nchunks + cl
                tl = cl * P
                Sold = self.Sst[c % 2]
                Snew = self.Sst[(c + 1) % 2]
                self.use_bank(6)
                for p in range(NCH):
                    sb0 = [0, 1][p % 2]
                    sb1 = [2, 3][p % 2]
                    self.use_bank(sb0)
                    self.use_bank(sb1)
                    for lane in range(2):
                        r0, r1 = lane * 64, lane * 64 + 64
                        bsl = sb0 if lane == 0 else sb1
                        self.emit("tensor", lambda e, p=p, tl=tl, bsl=bsl,
                                  r0=r0, r1=r1: e.matmul(
                                      self.bank(bsl, 128),
                                      k2f[r0:r1, p * TT + tl:p * TT + tl + P],
                                      q2[r0:r1, p * TT + tl:p * TT + tl + P],
                                      start=True, stop=True,
                                      tile_position=(r0, 0)), inc="te")
                    te_t = self.cnt["te"]
                    abt = ab[p % 2]
                    self.wait("vector", "te", te_t)
                    self.emit("vector", lambda e, sb0=sb0, abt=abt:
                              e.tensor_tensor(abt[:, 0:128],
                                              self.bank(sb0, 128),
                                              self.maskT2[:, 0:128],
                                              ALU.mult), inc="ve")
                    self.drain("vector")
                    v_t = self.emit("vector", lambda e, sb1=sb1, abt=abt:
                                    e.tensor_tensor(abt[:, 128:256],
                                                    self.bank(sb1, 128),
                                                    self.maskT2[:, 128:256],
                                                    ALU.mult), inc="ve")
                    self.drain("vector")
                    self.release_bank(sb0, "ve", v_t)
                    self.release_bank(sb1, "ve", v_t)
                    if self.p2_sub < 3:
                        continue
                    ob = 4 + p // 4
                    oc = (p % 4) * 128
                    if oc == 0:
                        self.use_bank(ob)
                    self.wait("tensor", "ve", v_t)
                    for lane in range(2):
                        r0 = lane * 64
                        self.emit("tensor", lambda e, p=p, tl=tl, ob=ob, oc=oc,
                                  abt=abt, r0=r0, lane=lane: e.matmul(
                                      self.bank(ob, 128, off=oc,
                                                p0=r0, p1=r0 + 64),
                                      v2[:, (tl // P) * 1024 + p * 128 + r0:
                                         (tl // P) * 1024 + p * 128 + r0 + 64],
                                      abt[:, lane * 128:lane * 128 + 128],
                                      start=True, stop=False,
                                      tile_position=(0, r0)))
                        self.emit("tensor", lambda e, p=p, tl=tl, ob=ob, oc=oc,
                                  r0=r0, Sold=Sold: e.matmul(
                                      self.bank(ob, 128, off=oc,
                                                p0=r0, p1=r0 + 64),
                                      Sold[r0:r0 + 64, p * 64:p * 64 + 64],
                                      q2[r0:r0 + 64,
                                         p * TT + tl:p * TT + tl + P],
                                      start=False, stop=True,
                                      tile_position=(r0, r0)), inc="te")
                    if self.p2_sub < 4:
                        continue
                    for lane in range(2):
                        r0 = lane * 64
                        self.emit("tensor", lambda e, p=p, tl=tl, r0=r0, c=c:
                                  e.matmul(
                                      self.bank(6, 64, off=p * 64,
                                                p0=r0, p1=r0 + 64),
                                      k2t[:, (tl // P) * 1024 + p * 128 + r0:
                                          (tl // P) * 1024 + p * 128 + r0 + 64],
                                      v2[:, (tl // P) * 1024 + p * 128 + r0:
                                         (tl // P) * 1024 + p * 128 + r0 + 64],
                                      start=(p == 0),
                                      stop=(p == NCH - 1),
                                      tile_position=(0, r0)), inc="te")
                te_t = self.cnt["te"]
                if self.p2_sub < 3:
                    continue
                self.wait("vector", "te", te_t)
                for p in range(NCH):
                    ob = 4 + p // 4
                    oc = (p % 4) * 128
                    v_t = self.emit("vector", lambda e, p=p, ob=ob, oc=oc, c=c:
                                    e.tensor_copy(
                                        o_t[p][:, c * P:(c + 1) * P],
                                        self.bank(ob, 128, off=oc)), inc="ve")
                v_t = self.milestone("vector", "ve")
                self.release_bank(4, "ve", v_t)
                self.release_bank(5, "ve", v_t)
                if self.p2_sub < 4:
                    continue
                self.emit("vector", lambda e, Snew=Snew, Sold=Sold:
                          e.tensor_tensor(Snew[:], Sold[:],
                                          self.bank(6, 512), ALU.add),
                          inc="ve")
                v_t = self.milestone("vector", "ve")
                self.release_bank(6, "ve", v_t)
                self.wait("tensor", "ve", v_t)
        self.o_t = o_t
        self.dbg_dump("o0", list(o_t), src_sems=("ve",))

    def P3_innerln_wo(self):
        self._elu_prev_v = self.cnt["ve"]
        self.wait("tensor", "ve", self.cnt["ve"])
        self.wait("tensor", "se", self.cnt["se"])
        for q_ in ("gpsimd", "sync"):
            self.wait(q_, "te", self.cnt["te"])
            self.wait(q_, "ve", self.cnt["ve"])
            self.wait(q_, "se", self.cnt["se"])
        T, TT, NTT = self.T, self.TT, self.NTT
        dt = self.dt
        o_t = self.o_t
        self.ln_stats_from_tiles(o_t)
        oln = [self.bf(self.W1_O + i * T, T) for i in range(NCH)]
        scr = self.SCR[:, 0:TT]
        for tt in range(NTT):
            c0 = tt * TT
            for i in range(NCH):
                self.emit("vector", lambda e, i=i, c0=c0: e.tensor_tensor(
                    scr, o_t[i][:, c0:c0 + TT], self.MU[:, c0:c0 + TT],
                    ALU.subtract), inc="ve")
                self.drain("vector")
                self.emit("vector", lambda e, c0=c0: e.tensor_tensor(
                    scr, scr, self.RS[:, c0:c0 + TT], ALU.mult), inc="ve")
                self.drain("vector")
                self.emit("vector", lambda e, i=i, c0=c0: e.tensor_scalar(
                    oln[i][:, c0:c0 + TT], scr, self.bias_ap(4, i),
                    self.bias_ap(5, i), ALU.mult, ALU.add), inc="ve")
                self.drain("vector")
        v_t = self.milestone("vector", "ve")
        self.wait("gpsimd", "te", self.cnt["te"])
        wo = self.bf(self.QT_O, 8192)
        ldw = self.dma("gpsimd", wo.rearrange("p (k m) -> p k m", k=NCH),
                       dt["woT"].rearrange("(k p) m -> p k m", p=P), "gd")
        self.wait("tensor", "gd", ldw)
        self.wait("tensor", "ve", v_t)
        s_t = [self.bf(self.EX_O + i * T, T) for i in range(NCH)]

        def s_writer_tt(c0):
            def w(m, ps, te_t):
                self.wait("vector", "te", te_t)
                v_t = self.emit("vector", lambda e, m=m: e.tensor_scalar_add(
                    s_t[m][:, c0:c0 + TT], ps, self.bias_ap(3, m)), inc="ve")
                self.drain("vector")
                return ("ve", v_t)
            return w
        for tt in range(NTT):
            c0 = tt * TT
            self.proj_fm(lambda k, m: wo[:, k * 1024 + m * 128:
                                         k * 1024 + (m + 1) * 128],
                         oln, c0, TT, s_writer_tt(c0), banks=[4, 5])
        self.s_t = s_t
        self.dbg_dump("s0", list(s_t), src_sems=("ve",))

    def P4_l0_ffn(self):
        self._elu_prev_v = self.cnt["ve"]
        self.wait("tensor", "ve", self.cnt["ve"])
        self.wait("tensor", "se", self.cnt["se"])
        for q_ in ("gpsimd", "sync"):
            self.wait(q_, "te", self.cnt["te"])
            self.wait(q_, "ve", self.cnt["ve"])
            self.wait(q_, "se", self.cnt["se"])
        T, TT, NTT = self.T, self.TT, self.NTT
        dt = self.dt
        s_t = self.s_t
        h_t = [self.bf(self.W1_O + i * TT, TT) for i in range(L0_HID // P)]
        g_s = [self.bf(self.QT_O + i * 4096, 4096) for i in range(2)]
        u_s = [self.bf(self.QT_O + 8192 + i * 4096, 4096) for i in range(2)]
        wd_s = [self.bf(self.W1_O + 12288, 4096), self.bf(self.WVS_O, 4096)]
        xw = self.FW[:, 0:TT]
        NS = L0_HID // 512
        g_cons = [0, 0]
        u_cons = [0, 0]
        wd_cons = [0, 0]
        last_down_te = 0
        prev_evac_v = 0
        for tt in range(NTT):
            c0 = tt * TT
            if last_down_te > 0:
                self.wait("vector", "te", last_down_te)
            for hs in range(NS):
                par = hs % 2
                if g_cons[par] > 0:
                    self.wait("gpsimd", "se", g_cons[par])
                    self.wait("gpsimd", "ve", u_cons[par])
                smp = "gA" if par == 0 else "gB"
                ldg = self.dma("gpsimd",
                               g_s[par].rearrange("p (k m) -> p k m", k=NCH),
                               dt["wg0T"][:, hs * 512:(hs + 1) * 512].rearrange(
                                   "(k p) m -> p k m", p=P), smp)
                ldu = self.dma("gpsimd",
                               u_s[par].rearrange("p (k m) -> p k m", k=NCH),
                               dt["wu0T"][:, hs * 512:(hs + 1) * 512].rearrange(
                                   "(k p) m -> p k m", p=P), smp)
                self.wait("tensor", smp, ldu)
                for mh in range(4):
                    hh = hs * 4 + mh
                    bg, bu = [0, 1][mh % 2], [2, 3][mh % 2]
                    self.use_bank(bg)
                    self.use_bank(bu)
                    for k in range(NCH):
                        last = (k == NCH - 1)
                        self.emit("tensor", lambda e, k=k, mh=mh, bg=bg,
                                  par=par, c0=c0, last=last: e.matmul(
                                      self.bank(bg, TT),
                                      g_s[par][:, k * 512 + mh * 128:
                                               k * 512 + (mh + 1) * 128],
                                      self.xn[k][:, c0:c0 + TT],
                                      start=(k == 0), stop=last),
                                  inc="te" if last else None)
                        self.emit("tensor", lambda e, k=k, mh=mh, bu=bu,
                                  par=par, c0=c0, last=last: e.matmul(
                                      self.bank(bu, TT),
                                      u_s[par][:, k * 512 + mh * 128:
                                               k * 512 + (mh + 1) * 128],
                                      self.xn[k][:, c0:c0 + TT],
                                      start=(k == 0), stop=last),
                                  inc="te" if last else None)
                    te_t = self.cnt["te"]
                    scr = self.SCR[:, 0:TT]
                    self.wait("scalar", "te", te_t)
                    if prev_evac_v > 0:
                        self.wait("scalar", "ve", prev_evac_v)
                    s_tt = self.emit("scalar", lambda e, bg=bg: e.activation(
                        scr, self.bank(bg, TT), AF.Silu, scale=1.0 / 127.0),
                        inc="se")
                    g_cons[par] = s_tt
                    self.wait("vector", "se", s_tt)
                    v_t = self.emit("vector", lambda e, hh=hh, bu=bu:
                                    e.scalar_tensor_tensor(
                                        h_t[hh], self.bank(bu, TT),
                                        1.0 / 127.0, scr,
                                        ALU.mult, ALU.mult), inc="ve")
                    u_cons[par] = v_t
                    prev_evac_v = v_t
                    self.drain("vector")
                    self.release_bank(bg, "se", s_tt)
                    self.release_bank(bu, "ve", v_t)
            h_done = self.cnt["ve"]
            self.wait("tensor", "ve", h_done)
            NSD = 6
            dc_te = [0] * NCH
            for ds in range(NSD):
                par = ds % 2
                if wd_cons[par] > 0:
                    self.wait("gpsimd", "te", wd_cons[par])
                smp = "dA" if par == 0 else "dB"
                ldd = self.dma("gpsimd",
                               wd_s[par].rearrange("p (k m) -> p k m", k=4),
                               dt["wd0T"][ds * 512:(ds + 1) * 512, :].rearrange(
                                   "(k p) m -> p k m", p=P), smp)
                self.wait("tensor", smp, ldd)
                for dc in range(NCH):
                    b = dc
                    if ds == 0:
                        self.use_bank(b)
                    for k in range(4):
                        hk = ds * 4 + k
                        last = (ds == NSD - 1 and k == 3)
                        strip_tail = (dc == NCH - 1 and k == 3)
                        self.emit("tensor", lambda e, k=k, dc=dc, b=b, par=par,
                                  hk=hk, ds=ds, last=last: e.matmul(
                                      self.bank(b, TT),
                                      wd_s[par][:, k * 1024 + dc * 128:
                                                k * 1024 + (dc + 1) * 128],
                                      h_t[hk],
                                      start=(ds == 0 and k == 0), stop=last),
                                  inc="te" if (last or strip_tail) else None)
                    if ds == NSD - 1:
                        dc_te[dc] = self.cnt["te"]
                wd_cons[par] = self.cnt["te"]
            last_down_te = self.cnt["te"]
            prev_v = getattr(self, "_p4_prev_v", 0)
            prev_st = getattr(self, "_p4_prev_st", 0)
            for dc in range(NCH):
                b = dc
                scr = self.SCR[:, 0:TT]
                if prev_v > 0:
                    self.wait("sync", "ve", prev_v)
                lx = self.dma("sync", xw,
                              dt["x"][dc * P:(dc + 1) * P, c0:c0 + TT], "sy")
                self.wait("scalar", "te", dc_te[dc])
                if prev_v > 0:
                    self.wait("scalar", "ve", prev_v)
                s_tt = self.emit("scalar", lambda e, b=b: e.activation(
                    scr, self.bank(b, TT), AF.Silu, scale=1.0 / 127.0),
                    inc="se")
                self.wait("vector", "se", s_tt)
                if prev_st > 0:
                    self.wait("vector", "so", prev_st)
                self.emit("vector", lambda e, dc=dc, c0=c0: e.tensor_tensor(
                    self.STAT[:, 0:TT], scr, s_t[dc][:, c0:c0 + TT], ALU.mult),
                    inc="ve")
                self.wait("vector", "sy", lx)
                self.drain("vector")
                v_t = self.emit("vector", lambda e: e.tensor_tensor(
                    self.STAT[:, 0:TT], self.STAT[:, 0:TT], xw, ALU.add),
                    inc="ve")
                self.drain("vector")
                prev_v = v_t
                self.release_bank(b, "ve", v_t)
                self.wait("sync", "ve", v_t)
                prev_st = self.dma("sync",
                                   self.x1_sp[dc * P:(dc + 1) * P, c0:c0 + TT],
                                   self.STAT[:, 0:TT], "so")
                self._p4_prev_v = prev_v
                self._p4_prev_st = prev_st
            prev_evac_v = prev_v
        self.drain("vector")

    def make_p5_tasks(self):
        dt = self.dt
        specs = [("wg1T", D, L1_HID), ("wu1T", D, L1_HID),
                 ("wd1T", L1_HID, D)]
        fw = [self.FW[:, 5632:6656], self.FW[:, 6656:7680]]
        self.p5_acc = [self.SMALL[:, 2 + j:3 + j] for j in range(3)]
        red = self.SMALL[:, 5:6]
        tl = []
        for j, (name, A, B) in enumerate(specs):
            srcd = dt[name]
            for a in range(A // P):
                for u in range(B // 1024):
                    tl.append((j, srcd[a * P:(a + 1) * P,
                                       u * 1024:(u + 1) * 1024]))
        for j in range(3):
            self.emit("vector", lambda e, j=j: e.memset(self.p5_acc[j], 0.0),
                      inc="ve")
        self.drain("vector")
        st = {}

        def issue(i):
            self.wait("sync", "ve", self.cnt["ve"])
            sm = "pA" if i % 2 == 0 else "pB"
            self.wait("sync", sm, self.cnt[sm])
            st[i] = (sm, self.dma("sync", fw[i % 2], tl[i][1], sm))
        issue(0)
        issue(1)

        def task(i):
            j = tl[i][0]
            self.wait("vector", *st.pop(i))
            self.emit("vector", lambda e, i=i: e.reduce_sum(
                red, fw[i % 2], axis=AX.X, apply_absolute_value=True),
                inc="ve")
            self.drain("vector")
            self.emit("vector", lambda e, j=j: e.tensor_tensor(
                self.p5_acc[j], self.p5_acc[j], red, ALU.add), inc="ve")
            self.drain("vector")
            if i + 2 < len(tl):
                issue(i + 2)
        return [lambda i=i: task(i) for i in range(len(tl))]

    def P5_wquant(self):
        for en in ("vector", "scalar"):
            self.wait(en, "so", self.cnt["so"])
        self._elu_prev_v = self.cnt["ve"]
        self.wait("tensor", "ve", self.cnt["ve"])
        self.wait("tensor", "se", self.cnt["se"])
        for q_ in ("gpsimd", "sync"):
            self.wait(q_, "te", self.cnt["te"])
            self.wait(q_, "ve", self.cnt["ve"])
            self.wait(q_, "se", self.cnt["se"])
        self.wait("sync", "xq", self.cnt["xq"])
        if self.last_stage >= 9:
            t = 0
            for i in range(NCH):
                t = self.dma("sync", self.bf(self.XN_O + i * self.T, self.T),
                             self.xq_sp[i * P:(i + 1) * P, :], "wA")
            self._xq_reload_t = t
        while self.p5_tasks:
            self.p5_tasks.pop(0)()
        dt = self.dt
        specs = [("wg1T", self.wgq_sc, D, L1_HID, 0),
                 ("wu1T", self.wuq_sc, D, L1_HID, 1),
                 ("wd1T", self.wdq_sc, L1_HID, D, 2)]
        fw = [self.FW[:, 0:2048], self.FW[:, 2048:4096]]
        self.mclip = self.SMALL[:, 8:11]
        self.thr = self.SMALL[:, 11:14]
        ub = [self.bf(self.QT_O + i * 2048, 2048) for i in range(2)]
        db = [self.bf(self.QT_O + 4096 + i * 2048, 2048) for i in range(2)]
        qb = [self.bf(self.QT_O + 8192 + i * 2048, 2048) for i in range(2)]
        self.wait_all("sync", "ve")
        self.wait_all("sync", "te")
        for name, dst, A, B, j in specs:
            srcd = dt[name]
            tiles = []
            if B >= 2048:
                for a in range(A // P):
                    for u in range(B // 2048):
                        tiles.append((src_ := srcd[a * P:(a + 1) * P,
                                                   u * 2048:(u + 1) * 2048],
                                      dst[a * P:(a + 1) * P,
                                          u * 2048:(u + 1) * 2048], 2048))
            else:
                for a in range(A // P):
                    tiles.append((srcd[a * P:(a + 1) * P, :],
                                  dst[a * P:(a + 1) * P, :], B))
            n = len(tiles)
            lds = {}
            cons = {}
            v_t = self.milestone("vector", "ve")
            self.wait("tensor", "ve", v_t)
            self.use_bank(6)
            te_t = self.emit("tensor", lambda e, j=j: e.transpose(
                self.psum[0:1, 3072:3072 + P], self.p5_acc[j],
                self.IDF[:]), inc="te")
            self.wait("vector", "te", te_t)
            self.emit("vector", lambda e: e.reduce_sum(
                self.STAT[0:1, 8:9], self.psum[0:1, 3072:3072 + P],
                axis=AX.X), inc="ve")
            self.drain("vector")
            self.emit("vector", lambda e, A=A, B=B: e.tensor_scalar(
                self.STAT[0:1, 8:9], self.STAT[0:1, 8:9], 1.0 / (A * B), 1e-5,
                ALU.mult, ALU.max), inc="ve")
            self.drain("vector")
            v_t = self.emit("vector", lambda e: e.tensor_scalar_mul(
                self.STAT[0:1, 9:10], self.STAT[0:1, 8:9], 0.5), inc="ve")
            self.drain("vector")
            self.release_bank(6, "ve", v_t)
            v_t = self.milestone("vector", "ve")
            self.wait("tensor", "ve", v_t)
            self.use_bank(7)
            te_t = self.emit("tensor", lambda e: e.matmul(
                self.psum[:, 3584:3586], self.ONEC[0:1, :],
                self.STAT[0:1, 8:10], start=True, stop=True), inc="te")
            mc = self.mclip[:, j:j + 1]
            th = self.thr[:, j:j + 1]
            self.wait("vector", "te", te_t)
            self.emit("vector", lambda e, mc=mc: e.tensor_copy(
                mc, self.psum[:, 3584:3585]), inc="ve")
            self.drain("vector")
            v_t = self.emit("vector", lambda e, th=th: e.tensor_copy(
                th, self.psum[:, 3585:3586]), inc="ve")
            self.drain("vector")
            self.release_bank(7, "ve", v_t)
            lds = {}
            sts = {}
            self.wait("sync", "ve", self.cnt["ve"])
            self.wait("sync", "se", self.cnt["se"])
            self.wait("vector", "oA", self.cnt["oA"])
            self.wait("vector", "oB", self.cnt["oB"])
            self.wait("scalar", "oA", self.cnt["oA"])
            self.wait("scalar", "oB", self.cnt["oB"])
            self.wait("sync", "wA", self.cnt["wA"])
            self.wait("sync", "wB", self.cnt["wB"])
            for i in range(min(2, n)):
                sm = "wA" if i % 2 == 0 else "wB"
                lds[i] = (sm, self.dma("sync", fw[i % 2][:, 0:tiles[i][2]],
                                       tiles[i][0], sm))
            for i in range(n):
                par = i % 2
                cc = tiles[i][2]
                self.wait("scalar", *lds[i])
                if i >= 2:
                    self.wait("scalar", "ve", cons[i - 2])
                s_tt = self.emit("scalar", lambda e, par=par, cc=cc:
                                 e.activation(ub[par][:, 0:cc],
                                              fw[par][:, 0:cc], AF.Sign),
                                 inc="se")
                self.wait("vector", *lds[i])
                self.emit("vector", lambda e, par=par, cc=cc:
                          e.scalar_tensor_tensor(db[par][:, 0:cc],
                                                 fw[par][:, 0:cc], -1.0,
                                                 fw[par][:, 0:cc],
                                                 ALU.mult, ALU.max), inc="ve")
                self.drain("vector")
                self.emit("vector", lambda e, par=par, cc=cc, th=th:
                          e.tensor_single_scalar(db[par][:, 0:cc],
                                                 db[par][:, 0:cc], th,
                                                 ALU.is_ge), inc="ve")
                self.wait("vector", "se", s_tt)
                if i >= 2:
                    self.wait("vector", sts[i - 2][0], sts[i - 2][1])
                self.drain("vector")
                v_t = self.emit("vector", lambda e, par=par, cc=cc:
                                e.tensor_tensor(qb[par][:, 0:cc],
                                                ub[par][:, 0:cc],
                                                db[par][:, 0:cc], ALU.mult),
                                inc="ve")
                cons[i] = v_t
                self.drain("vector")
                self.wait("gpsimd", "ve", v_t)
                smo = "oA" if par == 0 else "oB"
                sts[i] = (smo, self.dma("gpsimd", tiles[i][1],
                                        qb[par][:, 0:cc], smo))
                if i + 2 < n:
                    sm = "wA" if i % 2 == 0 else "wB"
                    self.wait("sync", "ve", v_t)
                    lds[i + 2] = (sm, self.dma(
                        "sync", fw[par][:, 0:tiles[i + 2][2]],
                        tiles[i + 2][0], sm))
        self.drain("vector")
        self.p5_store_done = (self.cnt["oA"], self.cnt["oB"])

    def P6_preln1(self):
        self.wait_all("sync", "so")
        self._elu_prev_v = self.cnt["ve"]
        self.wait("tensor", "ve", self.cnt["ve"])
        self.wait("tensor", "se", self.cnt["se"])
        for q_ in ("gpsimd", "sync"):
            self.wait(q_, "te", self.cnt["te"])
            self.wait(q_, "ve", self.cnt["ve"])
            self.wait(q_, "se", self.cnt["se"])
        self.win_te = {}
        self.win_ve = {}
        self.ln_stats_from_windows(self.x1_sp, write_xn=True)
        self.ln_apply_windows(self.x1_sp, self.xn, absmax=True)
        self.dbg_dump("xn1", list(self.xn), src_sems=("ve",))

    def P7_qkv(self):
        for en in ("vector", "scalar", "tensor"):
            self.wait(en, "oA", self.cnt["oA"])
            self.wait(en, "oB", self.cnt["oB"])
        self._elu_prev_v = self.cnt["ve"]
        self.wait("tensor", "ve", self.cnt["ve"])
        self.wait("tensor", "se", self.cnt["se"])
        for q_ in ("gpsimd", "sync"):
            self.wait(q_, "te", self.cnt["te"])
            self.wait(q_, "ve", self.cnt["ve"])
            self.wait(q_, "se", self.cnt["se"])
        T, TT, NTT = self.T, self.TT, self.NTT
        self.xq_tasks = (self.make_xq_tasks()
                         if self.last_stage >= 9 else [])
        dt = self.dt
        Q = [self.bf(self.QT_O + i * T, T) for i in range(NCH)]
        K = [self.bf(self.W1_O + i * T, T) for i in range(NCH)]
        Vp = self.bf(self.EX_O, 16640)
        ws = self.bf(self.RG_O, 2048)
        NSC = 256
        nst = 3 * D // NSC
        st_cons = 0
        for sidx in range(nst):
            cbase = sidx * NSC
            for _ in range(5):
                if self.p5_tasks:
                    self.p5_tasks.pop(0)()
            if st_cons > 0:
                self.wait("gpsimd", "te", st_cons)
            lds = self.dma("gpsimd", ws.rearrange("p (k m) -> p k m", k=NCH),
                           dt["wqkvT"][:, cbase:cbase + NSC].rearrange(
                               "(k p) m -> p k m", p=P), "gv")
            self.wait("tensor", "gv", lds)
            kind = cbase // D
            moff = (cbase % D)
            if kind < 2:
                dstt = Q if kind == 0 else K
                for mm in range(NSC // P):
                    m = moff // P + mm
                    b = [4, 5][mm % 2]
                    for tt in range(NTT):
                        c0 = tt * TT
                        self.use_bank(b)
                        for k in range(NCH):
                            last = (k == NCH - 1)
                            self.emit("tensor", lambda e, k=k, mm=mm, b=b,
                                      c0=c0, last=last: e.matmul(
                                          self.bank(b, TT),
                                          ws[:, k * NSC + mm * P:
                                             k * NSC + (mm + 1) * P],
                                          self.xn[k][:, c0:c0 + TT],
                                          start=(k == 0), stop=last),
                                      inc="te" if last else None)
                        te_t = self.cnt["te"]
                        if tt == NTT - 1:
                            st_cons = te_t
                        self.wait("vector", "te", te_t)
                        v_t = self.emit("vector", lambda e, m=m, c0=c0,
                                        dstt=dstt, b=b: e.tensor_copy(
                                            dstt[m][:, c0:c0 + TT],
                                            self.bank(b, TT)), inc="ve")
                        self.drain("vector")
                        self.release_bank(b, "ve", v_t)
            else:
                for tj in range(self.NTC):
                    b = [4, 5][tj % 2]
                    self.use_bank(b)
                    for k in range(NCH):
                        last = (k == NCH - 1)
                        self.emit("tensor", lambda e, k=k, tj=tj, b=b,
                                  last=last: e.matmul(
                                      self.bank(b, NSC),
                                      self.xn[k][:, tj * P:(tj + 1) * P],
                                      ws[:, k * NSC:(k + 1) * NSC],
                                      start=(k == 0), stop=last),
                                  inc="te" if last else None)
                    te_t = self.cnt["te"]
                    if tj == self.NTC - 1:
                        st_cons = te_t
                    self.wait("vector", "te", te_t)
                    h0 = moff // HD
                    v_t = self.emit("vector", lambda e, tj=tj, h0=h0, b=b:
                                    e.tensor_copy(
                                        Vp[:, tj * 1040 + h0 * 65:]
                                        [:, 0:4 * 65].rearrange(
                                            "p (h d) -> p h d", h=4)
                                        [:, :, 0:64],
                                        self.bank(b, NSC).rearrange(
                                            "p (h d) -> p h d", h=4)),
                                    inc="ve")
                    self.drain("vector")
                    self.release_bank(b, "ve", v_t)
        self.emit("vector", lambda e: e.memset(
            Vp.rearrange("p (t h d) -> p t h d", t=16, h=16)[:, :, :, 64:65],
            1.0), inc="ve")
        self.drain("vector")
        self.Q, self.K, self.Vp = Q, K, Vp
        self.dbg_dump("q1", list(Q), src_sems=("ve",))
        self.dbg_dump("k1", list(K), src_sems=("ve",))

    def make_xq_tasks(self):
        T, TT, NTT = self.T, self.TT, self.NTT
        win = [self.FW[:, i * TT:(i + 1) * TT] for i in range(NCH)]
        scr = self.FW[:, 4096:4096 + TT]
        tq_ = self.FW[:, 4608:4608 + TT]
        rec = self.FW[:, 5120:5120 + TT]
        xr = self.bf(self.WVS_O + 768, TT)
        xg = self.bf(self.WVS_O + 1280, TT)
        st = {"w": 0}

        def issue(tt):
            self.wait("sync", "ve", self.cnt["ve"])
            self.wait("sync", "xq", self.cnt["xq"])
            t = 0
            for i in range(NCH):
                t = self.dma("sync", win[i],
                             self.x1_sp[i * P:(i + 1) * P,
                                        tt * TT:(tt + 1) * TT], "wA")
            st["w"] = t

        issue(0)

        def task(tt, i):
            c0 = tt * TT
            w = win[i]
            if i == 0:
                self.wait("vector", "wA", st["w"])
                self.emit("vector", lambda e: e.reciprocal(
                    rec, self.SXR[:, c0:c0 + TT]), inc="ve")
                self.drain("vector")
            self.emit("vector", lambda e: e.tensor_tensor(
                scr, w, self.MU[:, c0:c0 + TT], ALU.subtract), inc="ve")
            self.drain("vector")
            self.emit("vector", lambda e: e.tensor_tensor(
                scr, scr, self.RS[:, c0:c0 + TT], ALU.mult), inc="ve")
            self.drain("vector")
            self.emit("vector", lambda e: e.scalar_tensor_tensor(
                tq_, scr, 127.0, rec, ALU.mult, ALU.mult), inc="ve")
            self.drain("vector")
            self.emit("vector", lambda e: e.tensor_scalar_add(
                tq_, tq_, 128.5), inc="ve")
            self.drain("vector")
            self.emit("vector", lambda e: e.tensor_scalar(
                xr, tq_, 128.0, 128.0, ALU.is_ge, ALU.mult), inc="ve")
            self.drain("vector")
            for sbit in (64.0, 32.0, 16.0, 8.0, 4.0, 2.0, 1.0):
                self.emit("vector", lambda e, sbit=sbit:
                          e.scalar_tensor_tensor(
                              xg, xr, sbit, tq_, ALU.add, ALU.is_le),
                          inc="ve")
                self.drain("vector")
                self.emit("vector", lambda e, sbit=sbit:
                          e.scalar_tensor_tensor(
                              xr, xg, sbit, xr, ALU.mult, ALU.add),
                          inc="ve")
                self.drain("vector")
            self.emit("vector", lambda e: e.tensor_scalar_add(
                w, xr, -128.0), inc="ve")
            v_t = self.milestone("vector", "ve")
            self.wait("gpsimd", "ve", v_t)
            self.dma("gpsimd", self.xq_sp[i * P:(i + 1) * P, c0:c0 + TT],
                     w, "xq")
            if i == NCH - 1 and tt + 1 < NTT:
                issue(tt + 1)

        return [lambda tt=tt, i=i: task(tt, i)
                for tt in range(NTT) for i in range(NCH)]

    def P8_l1_attn(self):
        self._elu_prev_v = self.cnt["ve"]
        self.wait("tensor", "ve", self.cnt["ve"])
        self.wait("tensor", "se", self.cnt["se"])
        for q_ in ("gpsimd", "sync"):
            self.wait(q_, "te", self.cnt["te"])
            self.wait(q_, "ve", self.cnt["ve"])
            self.wait(q_, "se", self.cnt["se"])
        T, TT, NTT = self.T, self.TT, self.NTT
        Q, K, Vp = self.Q, self.K, self.Vp
        o1 = [self.bf(self.XN_O + i * T, T) for i in range(NCH)]
        ring = [self.bf(self.RG_O + r * 512, 512)[:, 0:TT] for r in range(4)]
        o1tmp = self.bf(self.WVS_O + 256, 512)
        zrow = self.STAT[0:1, 0:TT]
        o1u = self.SCR[0:64, 0:TT]
        NJC = self.NTC
        for ic in range(NTT):
            i0 = ic * TT
            for p in range(NCH):
                avb = [4, 5]
                self.use_bank(4)
                self.use_bank(5)
                ring_v = [0, 0, 0, 0]
                av_info = {}

                def emit_av(j, p=p):
                    for lane in range(2):
                        rsl, s_tt = av_info.pop((j, lane))
                        self.wait("tensor", "se", s_tt)
                        self.emit("tensor", lambda e, j=j, rsl=rsl,
                                  lane=lane, p=p: e.matmul(
                                      self.bank(avb[lane], TT,
                                                p0=0, p1=65),
                                      Vp[:, j * 1040 + (2 * p + lane) * 65:
                                         j * 1040 + (2 * p + lane) * 65 + 65],
                                      ring[rsl],
                                      start=(j == 0), stop=(j == NJC - 1)),
                                  inc="te")
                        ring_v[rsl] = self.cnt["te"]
                for jc in range(NJC):
                    for lane in range(2):
                        bsl = [0, 1][jc % 2] if lane == 0 else [2, 3][jc % 2]
                        self.use_bank(bsl)
                        r0 = lane * 64
                        self.emit("tensor", lambda e, p=p, jc=jc, r0=r0,
                                  bsl=bsl, i0=i0: e.matmul(
                                      self.bank(bsl, TT),
                                      K[p][r0:r0 + 64, jc * P:(jc + 1) * P],
                                      Q[p][r0:r0 + 64, i0:i0 + TT],
                                      start=True, stop=True,
                                      tile_position=(r0, 0)), inc="te")
                        te_t = self.cnt["te"]
                        rsl = (jc % 2) * 2 + lane
                        if ring_v[rsl] > 0:
                            self.wait("scalar", "te", ring_v[rsl])
                        self.wait("scalar", "te", te_t)
                        s_tt = self.emit("scalar", lambda e, bsl=bsl, rsl=rsl:
                                         e.activation(ring[rsl],
                                                      self.bank(bsl, TT),
                                                      AF.Exp, scale=0.125),
                                         inc="se")
                        self.release_bank(bsl, "se", s_tt)
                        av_info[(jc, lane)] = (rsl, s_tt)
                    if jc > 0:
                        emit_av(jc - 1)
                emit_av(NJC - 1)
                te_t = self.cnt["te"]
                for lane in range(2):
                    b = avb[lane]
                    zbank = 6 + lane
                    self.wait("vector", "te", te_t)
                    self.emit("vector", lambda e, b=b: e.reciprocal(
                        zrow, self.bank(b, TT, p0=64, p1=65)), inc="ve")
                    v_t = self.milestone("vector", "ve")
                    self.wait("tensor", "ve", v_t)
                    self.use_bank(zbank)
                    tz = self.emit("tensor", lambda e, zbank=zbank: e.matmul(
                        self.bank(zbank, TT, p0=0, p1=64),
                        self.ONEC[0:1, 0:64], zrow,
                        start=True, stop=True), inc="te")
                    pv = getattr(self, "_p8_prev_o1u", 0)
                    self.wait("scalar", "te", te_t)
                    if pv > 0:
                        self.wait("scalar", "ve", pv)
                    s_t = self.emit("scalar", lambda e, b=b: e.copy(
                        o1u, self.bank(b, TT, p0=0, p1=64)), inc="se")
                    if lane == 0:
                        dst = o1[p][0:64, i0:i0 + TT]
                    else:
                        dst = o1tmp[0:64, 0:TT]
                        self.wait("vector", "sb2",
                                  getattr(self, "p8_o1dma", 0))
                    self.wait("vector", "te", tz)
                    self.wait("vector", "se", s_t)
                    v_t = self.emit("vector", lambda e, dst=dst, zbank=zbank:
                                    e.tensor_tensor(
                                        dst, o1u,
                                        self.bank(zbank, TT, p0=0, p1=64),
                                        ALU.mult), inc="ve")
                    self.drain("vector")
                    self._p8_prev_o1u = v_t
                    self.release_bank(b, "ve", v_t)
                    self.release_bank(zbank, "ve", v_t)
                    if lane == 1:
                        self.wait("sync", "ve", v_t)
                        self.p8_o1dma = self.dma(
                            "sync", o1[p][64:128, i0:i0 + TT],
                            o1tmp[0:64, 0:TT], "sb2")
                if self.xq_tasks:
                    self.xq_tasks.pop(0)()
                if self.p5_tasks:
                    self.p5_tasks.pop(0)()
        while self.xq_tasks:
            self.xq_tasks.pop(0)()
        self.o1 = o1
        self.dbg_dump("o1", list(o1), src_sems=("ve", "sy"))

    def P8b_wout(self):
        self._elu_prev_v = self.cnt["ve"]
        self.wait("tensor", "ve", self.cnt["ve"])
        self.wait("tensor", "se", self.cnt["se"])
        for q_ in ("gpsimd", "sync"):
            self.wait(q_, "te", self.cnt["te"])
            self.wait(q_, "ve", self.cnt["ve"])
            self.wait(q_, "se", self.cnt["se"])
        T, TT, NTT = self.T, self.TT, self.NTT
        dt = self.dt
        o1 = self.o1
        wo = self.bf(self.QT_O, 8192)
        ldw = self.dma("gpsimd", wo.rearrange("p (k m) -> p k m", k=NCH),
                       dt["woutT"].rearrange("(k p) m -> p k m", p=P), "gd")
        self.wait("tensor", "gd", ldw)
        self.wait_all("tensor", "sb2")
        s1 = [self.bf(self.W1_O + i * T, T) for i in range(NCH)]

        def w_writer(c0):
            def w(m, ps, te_t):
                self.wait("vector", "te", te_t)
                v_t = self.emit("vector", lambda e, m=m: e.tensor_copy(
                    s1[m][:, c0:c0 + TT], ps), inc="ve")
                self.drain("vector")
                self.wait("sync", "ve", v_t)
                self.dma("sync", self.s1_sp[m * P:(m + 1) * P, c0:c0 + TT],
                         s1[m][:, c0:c0 + TT], "wB")
                return ("ve", v_t)
            return w
        for tt in range(NTT):
            c0 = tt * TT
            self.proj_fm(lambda k, m: wo[:, k * 1024 + m * 128:
                                         k * 1024 + (m + 1) * 128],
                         o1, c0, TT, w_writer(c0), banks=[4, 5])
        self.s1 = s1
        self.dbg_dump("s1", list(s1), src_sems=("ve",))

    def P9_l1_ffn(self):
        self.wait_all("sync", "so")
        self._elu_prev_v = self.cnt["ve"]
        self.wait("tensor", "ve", self.cnt["ve"])
        self.wait("tensor", "se", self.cnt["se"])
        for q_ in ("gpsimd", "sync"):
            self.wait(q_, "te", self.cnt["te"])
            self.wait(q_, "ve", self.cnt["ve"])
            self.wait(q_, "se", self.cnt["se"])
        T, TT, NTT = self.T, self.TT, self.NTT
        dt = self.dt
        xq = [self.bf(self.XN_O + i * T, T) for i in range(NCH)]
        self.wait("sync", "wB", self.cnt["wB"])
        t = self._xq_reload_t
        self.wait("tensor", "wA", t)
        self.wait("gpsimd", "wA", t)
        self.dbg_dump("xq", list(xq), src_sems=("wA",))
        TQ = 256
        NTQ = T // TQ
        NH1 = L1_HID // P
        HQ = NH1 * TQ
        hpt = [self.FW[:, i * TQ:(i + 1) * TQ] for i in range(NH1)]
        hqb = [[self.bf(self.EX_O + h * HQ + i * TQ, TQ) for i in range(NH1)]
               for h in range(2)]
        g_s = [self.bf(self.QT_O + i * 4096, 4096) for i in range(2)]
        u_s = [self.bf(self.QT_O + 8192 + i * 4096, 4096) for i in range(2)]
        d_s = [self.bf(self.W1_O + i * 8192, 8192) for i in range(2)]
        QW = min(2048, NH1 * TQ)
        rbufs = [self.bf(self.RG_O, QW), self.bf(self.WVS_O + 2048, QW)]
        gbuf = self.bf(self.WVS_O, QW)
        s1w = self.bf(self.RG_O + 2048, TQ)
        cg = self.STAT[:, 0:TQ]
        cu = self.STAT[:, 256:512]
        cdb = [self.STAT[:, 512:768],
               self.MU[:, 256:512] if T >= 512 else self.STAT[:, 512:768]]
        rec_h = self.STAT[:, 768:768 + TQ]
        mxh = self.RS[:, 0:TQ]
        mrow = self.MU[0:1, 0:TQ]
        mrow2 = self.MU[0:1, 1024:1024 + TQ]
        rfrow = self.MU[0:1, 1280:1280 + TQ]
        scrq = self.SCR[:, 0:TQ]
        scr2 = self.SCR[:, 256:512]
        sl_g = [self.SCR[:, 0:TQ], self.MU[:, 512:768]]
        sl_s = [self.SCR[:, 256:512], self.MU[:, 768:1024]]
        vab = self.MU[:, 1536:1536 + TQ]
        ug2 = [self.MU[:, 1792:1792 + TQ], self.RS[:, TQ:2 * TQ]]
        cu0 = self.STAT[0:1, 256:256 + TQ]
        NS = L1_HID // 512
        g_cons = [0, 0]
        d_cons = [0, 0]
        hq_done = [0, 0]
        dmm_done = [0, 0]
        rb_free = [0, 0]
        pa, pb = getattr(self, "p5_store_done", (0, 0))
        self.wait("sync", "oA", pa)
        self.wait("sync", "oB", pb)

        def emit_down_strip(tqp, ds):
            par = ds % 2
            smp = "eA" if par == 0 else "eB"
            hqp = hqb[tqp % 2]
            if d_cons[par] > 0:
                self.wait("sync", "te", d_cons[par])
            ldd = self.dma("sync",
                           d_s[par].rearrange("p (k m) -> p k m", k=NCH),
                           self.wdq_sc[ds * 1024:(ds + 1) * 1024, :].rearrange(
                               "(k p) m -> p k m", p=P), smp)
            if ds == 0:
                self.wait("tensor", "se", hq_done[tqp % 2])
            self.wait("tensor", smp, ldd)
            for dc in range(NCH):
                b = 4 + dc // 2
                off = (dc % 2) * TQ
                if ds == 0 and dc % 2 == 0:
                    self.use_bank(b)
                for k in range(NCH):
                    hk = ds * NCH + k
                    first = (ds == 0 and k == 0 and dc % 2 == 0)
                    last = (ds == 3 and k == NCH - 1 and dc % 2 == 1)
                    self.emit("tensor", lambda e, k=k, dc=dc, b=b, off=off,
                              par=par, hk=hk, first=first, last=last,
                              hqp=hqp: e.matmul(
                                  self.bank(b, TQ, off=off),
                                  d_s[par].rearrange(
                                      "p (k m) -> p k m", k=NCH)[:, k, :]
                                  [:, dc * 128:(dc + 1) * 128],
                                  hqp[hk],
                                  start=first, stop=last),
                              inc="te" if k == NCH - 1 else None)
            d_cons[par] = self.cnt["te"]
            if ds == 3:
                dmm_done[tqp % 2] = self.cnt["te"]

        def emit_down_evac(tqp):
            c0 = tqp * TQ
            cdq = cdb[tqp % 2]
            te_t = self.cnt["te"]
            prev_fin = self.cnt["ve"]
            for dc in range(NCH):
                b = 4 + dc // 2
                off = (dc % 2) * TQ
                self.wait("sync", "ve", prev_fin)
                lx = self.dma("sync", scr2,
                              self.x1_sp[dc * P:(dc + 1) * P, c0:c0 + TQ],
                              "sy")
                ls = self.dma("sync", s1w,
                              self.s1_sp[dc * P:(dc + 1) * P, c0:c0 + TQ],
                              "sy")
                self.wait("vector", "te", te_t)
                self.emit("vector", lambda e, b=b, off=off, cdq=cdq:
                          e.tensor_tensor(
                              scrq, self.bank(b, TQ, off=off), cdq, ALU.mult),
                          inc="ve")
                v_t = self.milestone("vector", "ve")
                if dc % 2 == 1:
                    self.release_bank(b, "ve", v_t)
                self.wait("scalar", "ve", v_t)
                self.wait("scalar", "so", self.cnt["so"])
                s_tt = self.emit("scalar", lambda e: e.activation(
                    scrq, scrq, AF.Silu), inc="se")
                self.wait("vector", "se", s_tt)
                self.wait("vector", "sy", ls)
                self.emit("vector", lambda e: e.tensor_tensor(
                    scrq, scrq, s1w, ALU.mult), inc="ve")
                self.drain("vector")
                if "upd" in self.dbg_names:
                    v_t = self.milestone("vector", "ve")
                    self.wait("sync", "ve", v_t)
                    dt_ = self.dma("sync",
                                   self.dbg_upd[dc * P:(dc + 1) * P,
                                                c0:c0 + TQ], scrq, "dbg2")
                    self.wait("vector", "dbg2", dt_)
                v_t = self.emit("vector", lambda e: e.tensor_tensor(
                    scrq, scrq, scr2, ALU.add), inc="ve")
                self.drain("vector")
                prev_fin = v_t
                self.wait("sync", "ve", v_t)
                self.dma("sync", self.out[dc * P:(dc + 1) * P, c0:c0 + TQ],
                         scrq, "so")
                self.wait("vector", "so", self.cnt["so"])

        self._p9_vp = [0, 0]
        for tq in range(NTQ):
            c0 = tq * TQ
            if tq == 0:
                self.wait("sync", "te", self.cnt["te"])
            self.emit("vector", lambda e, c0=c0: e.tensor_scalar(
                cg, self.SXR[:, c0:c0 + TQ], self.mclip[:, 0:1], 1.0 / 127.0,
                ALU.mult, ALU.mult), inc="ve")
            self.emit("vector", lambda e, c0=c0: e.tensor_scalar(
                cu0, self.SXR[0:1, c0:c0 + TQ], self.mclip[0:1, 1:2],
                1.0 / 127.0, ALU.mult, ALU.mult), inc="ve")
            self.drain("vector")
            self.emit("vector", lambda e: e.memset(mxh, 0.0), inc="ve")
            self.drain("vector")
            pend = {}

            def flush_h(hh):
                s_tt = pend.pop(hh)
                self.drain("vector")
                self.wait("vector", "se", s_tt)
                vp = self.emit("vector", lambda e, hh=hh:
                               e.tensor_tensor(
                                   hpt[hh], ug2[hh % 2],
                                   sl_s[hh % 2], ALU.mult), inc="ve")
                self._p9_vp[hh % 2] = vp
                self.drain("vector")
                self.emit("vector", lambda e, hh=hh: e.scalar_tensor_tensor(
                    vab, hpt[hh], -1.0, hpt[hh], ALU.mult, ALU.max),
                    inc="ve")
                self.drain("vector")
                self.emit("vector", lambda e: e.tensor_tensor(
                    mxh, mxh, vab, ALU.max), inc="ve")
                self.drain("vector")
            for hs in range(NS):
                par = hs % 2
                smp = "sA" if par == 0 else "sB"
                if g_cons[par] > 0:
                    self.wait("sync", "te", g_cons[par])
                ldg = self.dma("sync",
                               g_s[par].rearrange("p (k m) -> p k m", k=NCH),
                               self.wgq_sc[:, hs * 512:(hs + 1) * 512].rearrange(
                                   "(k p) m -> p k m", p=P), smp)
                ldu = self.dma("sync",
                               u_s[par].rearrange("p (k m) -> p k m", k=NCH),
                               self.wuq_sc[:, hs * 512:(hs + 1) * 512].rearrange(
                                   "(k p) m -> p k m", p=P), smp)
                self.wait("tensor", smp, ldu)
                for mh in range(4):
                    hh = hs * 4 + mh
                    bg, bu = [0, 1][mh % 2], [2, 3][mh % 2]
                    self.use_bank(bg)
                    self.use_bank(bu)
                    for k in range(NCH):
                        last = (k == NCH - 1)
                        self.emit("tensor", lambda e, k=k, mh=mh, bg=bg,
                                  par=par, c0=c0, last=last: e.matmul(
                                      self.bank(bg, TQ),
                                      g_s[par][:, k * 512 + mh * 128:
                                               k * 512 + (mh + 1) * 128],
                                      xq[k][:, c0:c0 + TQ],
                                      start=(k == 0), stop=last),
                                  inc="te" if last else None)
                        self.emit("tensor", lambda e, k=k, mh=mh, bu=bu,
                                  par=par, c0=c0, last=last: e.matmul(
                                      self.bank(bu, TQ),
                                      u_s[par][:, k * 512 + mh * 128:
                                               k * 512 + (mh + 1) * 128],
                                      xq[k][:, c0:c0 + TQ],
                                      start=(k == 0), stop=last),
                                  inc="te" if last else None)
                    te_t = self.cnt["te"]
                    self.wait("scalar", "te", te_t)
                    if self._p9_vp[hh % 2] > 0:
                        self.wait("scalar", "ve", self._p9_vp[hh % 2])
                    cp = self.emit("scalar", lambda e, hh=hh, bu=bu: e.copy(
                        ug2[hh % 2], self.bank(bu, TQ)), inc="se")
                    self.release_bank(bu, "se", cp)
                    self.wait("vector", "te", te_t)
                    vg = self.emit("vector", lambda e, bg=bg, hh=hh:
                                   e.tensor_tensor(
                                       sl_g[hh % 2], self.bank(bg, TQ),
                                       cg, ALU.mult), inc="ve")
                    self.release_bank(bg, "ve", vg)
                    self.wait("scalar", "ve", vg)
                    s_tt = self.emit("scalar", lambda e, hh=hh: e.activation(
                        sl_s[hh % 2], sl_g[hh % 2], AF.Silu), inc="se")
                    pend[hh] = s_tt
                    if (hh - 1) in pend:
                        flush_h(hh - 1)
                g_cons[par] = self.cnt["te"]
                if tq > 0 and hs % 2 == 1:
                    emit_down_strip(tq - 1, hs // 2)
            for hh in sorted(pend):
                flush_h(hh)
            v_t = self.milestone("vector", "ve")
            self.wait("tensor", "ve", v_t)
            for g in range(2):
                self.use_bank(0)
                te_t = self.emit("tensor", lambda e, g=g: e.transpose(
                    self.bank(0, 128), mxh[:, g * P:(g + 1) * P], self.IDF[:]),
                    inc="te")
                self.wait("vector", "te", te_t)
                v_t = self.emit("vector", lambda e, g=g: e.reduce_max(
                    self.SMALL[:, g:g + 1], self.bank(0, 128), axis=AX.X),
                    inc="ve")
                self.drain("vector")
                self.release_bank(0, "ve", v_t)
            v_t = self.milestone("vector", "ve")
            self.wait("tensor", "ve", v_t)
            self.use_bank(1)
            te_t = self.emit("tensor", lambda e: e.transpose(
                self.psum[0:2, 512:512 + P], self.SMALL[:, 0:2],
                self.IDF[:]), inc="te")
            self.wait("vector", "te", te_t)
            v_t = self.emit("vector", lambda e: e.tensor_copy(
                self.SCR[0:2, 0:P], self.psum[0:2, 512:512 + P]),
                inc="ve")
            self.drain("vector")
            self.release_bank(1, "ve", v_t)
            self.wait("sync", "ve", v_t)
            d_t = self.dma("sync",
                           mrow.rearrange("o (g f) -> o g f", g=2),
                           self.SCR[0:2, 0:P], "sb2")
            self.wait("vector", "sb2", d_t)
            self.emit("vector", lambda e: e.tensor_tensor(
                mrow2, mrow, cu0, ALU.mult), inc="ve")
            self.drain("vector")
            self.emit("vector", lambda e: e.tensor_single_scalar(
                mrow2, mrow2, 1e-5, ALU.max), inc="ve")
            self.drain("vector")
            self.emit("vector", lambda e: e.reciprocal(rfrow, mrow2),
                      inc="ve")
            self.drain("vector")
            self.emit("vector", lambda e: e.tensor_tensor(
                rfrow, rfrow, cu0, ALU.mult), inc="ve")
            v_t = self.milestone("vector", "ve")
            self.wait("tensor", "ve", v_t)
            self.use_bank(0)
            te_t = self.emit("tensor", lambda e: e.matmul(
                self.bank(0, 2 * TQ), self.ONEC[0:1, :],
                self.MU[0:1, 1024:1024 + 2 * TQ],
                start=True, stop=True), inc="te")
            self.wait("vector", "te", te_t)
            cdq = cdb[tq % 2]
            self.emit("vector", lambda e, cdq=cdq: e.tensor_scalar(
                cdq, self.bank(0, TQ), self.mclip[:, 2:3], 1.0 / 127.0,
                ALU.mult, ALU.mult), inc="ve")
            self.drain("vector")
            v_t = self.emit("vector", lambda e: e.tensor_copy(
                rec_h, self.bank(0, TQ, off=TQ)), inc="ve")
            self.drain("vector")
            self.release_bank(0, "ve", v_t)
            if tq > 0:
                emit_down_evac(tq - 1)
            for i in range(NH1):
                self.emit("vector", lambda e, i=i: e.scalar_tensor_tensor(
                    hpt[i], hpt[i], 127.0, rec_h, ALU.mult, ALU.mult),
                    inc="ve")
            self.drain("vector")
            NQC = NH1 * TQ // QW
            self.wait("scalar", "ve", self.cnt["ve"])
            s_add = {}
            for q in range(NQC):
                s_add[q] = self.emit("scalar", lambda e, q=q: e.activation(
                    self.FW[:, q * QW:(q + 1) * QW],
                    self.FW[:, q * QW:(q + 1) * QW],
                    AF.Identity, bias=128.5), inc="se")
            if dmm_done[tq % 2] > 0:
                self.wait("scalar", "te", dmm_done[tq % 2])
            for q in range(NQC):
                u = self.FW[:, q * QW:(q + 1) * QW]
                hqq = self.bf(self.EX_O + (tq % 2) * HQ + q * QW, QW)
                rb = rbufs[q % 2]
                if rb_free[q % 2] > 0:
                    self.wait("vector", "se", rb_free[q % 2])
                self.wait("vector", "se", s_add[q])
                self.emit("vector", lambda e, u=u, rb=rb: e.tensor_scalar(
                    rb, u, 128.0, 128.0, ALU.is_ge, ALU.mult), inc="ve")
                self.drain("vector")
                for sbit in (64.0, 32.0, 16.0, 8.0, 4.0, 2.0, 1.0):
                    self.emit("vector", lambda e, sbit=sbit, u=u, rb=rb:
                              e.scalar_tensor_tensor(
                                  gbuf, rb, sbit, u, ALU.add, ALU.is_le),
                              inc="ve")
                    self.drain("vector")
                    self.emit("vector", lambda e, sbit=sbit, rb=rb:
                              e.scalar_tensor_tensor(
                                  rb, gbuf, sbit, rb, ALU.mult, ALU.add),
                              inc="ve")
                    self.drain("vector")
                self.wait("scalar", "ve", self.cnt["ve"])
                s_f = self.emit("scalar", lambda e, hqq=hqq, rb=rb:
                                e.activation(hqq, rb, AF.Identity,
                                             bias=-128.0), inc="se")
                rb_free[q % 2] = s_f
            hq_done[tq % 2] = self.cnt["se"]
            if tq == 2:
                self.dbg_dump("hq0", [h[:, 0:TQ] for h in hqb[0]],
                              src_sems=("se",))
        for ds in range(4):
            emit_down_strip(NTQ - 1, ds)
        emit_down_evac(NTQ - 1)
        self.drain("vector")



def _prep_inputs(inputs):
    x = np.asarray(inputs["L0_attn_query"])
    n_cores = x.shape[0]
    bias6 = np.stack([
        np.asarray(inputs["L0_attn_bq"]),
        np.asarray(inputs["L0_attn_bk"]),
        np.asarray(inputs["L0_attn_bv"]),
        np.asarray(inputs["L0_attn_bo"]),
        np.asarray(inputs["L0_attn_ln_g"]),
        np.asarray(inputs["L0_attn_ln_b"]),
    ], axis=1).astype(np.float32)
    tr = lambda a: np.ascontiguousarray(np.asarray(a).T)
    shared = {
        "bias6": np.ascontiguousarray(bias6),
        "wqT": tr(inputs["L0_attn_Wq"]),
        "wkT": tr(inputs["L0_attn_Wk"]),
        "wvT": tr(inputs["L0_attn_Wv"]),
        "woT": tr(inputs["L0_attn_Wo"]),
        "wg0T": tr(inputs["L0_ffn_w_gate"]),
        "wu0T": tr(inputs["L0_ffn_w_up"]),
        "wd0T": tr(inputs["L0_ffn_w_down"]),
        "wqkvT": tr(inputs["L1_attn_w_qkv"]),
        "woutT": tr(inputs["L1_attn_w_out"]),
        "wg1T": tr(inputs["L1_ffn_w_gate"]),
        "wu1T": tr(inputs["L1_ffn_w_up"]),
        "wd1T": tr(inputs["L1_ffn_w_down"]),
    }
    in_maps = []
    for c in range(n_cores):
        m = dict(shared)
        m["x"] = np.ascontiguousarray(x[c].T)
        in_maps.append(m)
    return in_maps


def kernel(**inputs):
    in_maps = _prep_inputs(inputs)
    b = Builder(T=2048, last_stage=99)
    nc = b.build()
    res = run_bass_kernel_spmd(nc, in_maps, core_ids=list(range(8)))
    outs = [np.ascontiguousarray(r["out"].T) for r in res.results]
    return np.stack(outs, axis=0).astype(np.float32)


# version 19
# speedup vs baseline: 1.0117x; 1.0117x over previous
import numpy as np
from contextlib import ExitStack

import concourse.bass as bass
import concourse.mybir as mybir
from concourse.bass_utils import run_bass_kernel_spmd


F32 = mybir.dt.float32
BF16 = mybir.dt.bfloat16
I32 = mybir.dt.int32
AF = mybir.ActivationFunctionType
ALU = mybir.AluOpType
AX = mybir.AxisListType

P = 128
D = 1024
NCH = D // P
H = 16
HD = 64
L0_HID = 3072
L1_HID = 4096
LN_EPS = 1e-5

ENGS = ("tensor", "vector", "scalar", "gpsimd", "sync")


class Builder:
    def __init__(self, T=2048, last_stage=99, dbg=(), p2_sub=9):
        self.T = T
        self.last_stage = last_stage
        self.p2_sub = p2_sub
        self.dbg_names = set(dbg)
        self.nc = bass.Bass("TRN2", target_bir_lowering=False, debug=False)
        self.streams = {e: [] for e in ENGS}
        self.cnt = {}
        self.sem = {}
        self.stack = ExitStack()

    def mksem(self, name):
        self.sem[name] = self.stack.enter_context(self.nc.semaphore(name))
        self.cnt[name] = 0

    def sbuf(self, name, shape, dtype):
        return self.nc.alloc_sbuf_tensor("sb_" + name, shape, dtype)

    def const(self, val):
        if val in self.consts:
            return
        i = len(self.consts)
        assert i < 16
        ap = self.cpool[:, i:i + 1]
        self.consts[val] = ap
        self.emit("vector", lambda e, ap=ap, val=val: e.memset(ap, val),
                  inc="ve")
        self.nc.const_aps.register(F32, val, ap)

    OWN_SEM = {"vector": "ve", "scalar": "se", "gpsimd": "ge"}

    def emit(self, eng, fn, inc=None, amt=1):
        if inc is None:
            inc = self.OWN_SEM.get(eng)
        if inc is not None:
            self.cnt[inc] += amt
            sem = self.sem[inc]

            def wrapped(e, fn=fn, sem=sem, amt=amt):
                fn(e).then_inc(sem, amt)
            self.streams[eng].append(wrapped)
            return self.cnt[inc]
        self.streams[eng].append(fn)
        return None

    def wait(self, eng, sname, val):
        sem = self.sem[sname]
        self.streams[eng].append(lambda e, sem=sem, val=val: e.wait_ge(sem, val))

    def wait_all(self, eng, sname):
        self.wait(eng, sname, self.cnt[sname])

    def drain(self, eng):
        own = self.OWN_SEM.get(eng)
        if own is not None:
            self.wait(eng, own, self.cnt[own])
        else:
            self.streams[eng].append(lambda e: e.drain())

    def milestone(self, eng, sname):
        if self.OWN_SEM.get(eng) == sname and self.cnt[sname] > 0:
            self.wait(eng, sname, self.cnt[sname])
            return self.cnt[sname]
        return self.emit(eng, lambda e: e.drain(), inc=sname)

    def dma(self, queue, out_ap, in_ap, sname, slow_ok=False):
        self.cnt[sname] += 16
        sem = self.sem[sname]

        def fn(e, out_ap=out_ap, in_ap=in_ap, sem=sem):
            e.dma_start(out_ap, in_ap,
                        allow_slow_non_contiguous=slow_ok).then_inc(sem, 16)
        self.streams[queue].append(fn)
        return self.cnt[sname]

    def finalize(self):
        nc = self.nc
        with nc.Block() as block:
            for eng in ENGS:
                dec = getattr(block, eng)
                stream = self.streams[eng]

                def body(e, stream=stream):
                    for fn in stream:
                        fn(e)
                dec(body)
        self.stack.close()
        return nc

    def dbg_dump(self, name, tiles, src_sems=()):
        if name not in self.dbg_names:
            return
        aps = [(t[:] if not isinstance(t, bass.AP) else t) for t in tiles]
        n = len(aps)
        cols = aps[0].shape[-1]
        d = self.nc.dram_tensor("dbg_" + name, [n * P, cols], F32,
                                kind="ExternalOutput").ap()
        for s in src_sems:
            self.wait_all("sync", s)
            self.wait_all("gpsimd", s)
        for i, ap in enumerate(aps):
            if ap.dtype == F32:
                self.dma("sync", d[i * P:(i + 1) * P, :], ap, "dbg2")
            else:
                self.dma("gpsimd", d[i * P:(i + 1) * P, :], ap, "dbg")
        for en in ("vector", "scalar", "tensor", "sync", "gpsimd"):
            self.wait(en, "dbg", self.cnt["dbg"])
            self.wait(en, "dbg2", self.cnt["dbg2"])

    def bank(self, i, cols=512, off=0, p0=0, p1=P):
        return self.psum[p0:p1, 512 * i + off: 512 * i + off + cols]

    def use_bank(self, i):
        sname, cntv = self.bank_free[i]
        if cntv > 0:
            self.wait("tensor", sname, cntv)

    def release_bank(self, i, sname, cntv):
        self.bank_free[i] = (sname, cntv)

    def build(self):
        nc, T = self.nc, self.T
        TT = min(512, T)
        NTT = T // TT
        self.NTT, self.TT = NTT, TT
        self.NTC = T // P

        for s in ("sy", "gd", "gv", "bro", "oA", "oB", "wA", "wB",
                  "sA", "sB", "dA", "dB", "eA", "eB", "gA", "gB", "sb2",
                  "dbg", "dbg2", "te", "ve", "se", "ge", "so", "xq",
                  "pA", "pB"):
            self.mksem(s)

        dt = {}
        dt["x"] = nc.dram_tensor("x", [D, T], F32, kind="ExternalInput").ap()
        dt["bias6"] = nc.dram_tensor("bias6", [D, 6], F32, kind="ExternalInput").ap()
        for w in ("wqT", "wkT", "wvT", "woT"):
            dt[w] = nc.dram_tensor(w, [D, D], F32, kind="ExternalInput").ap()
        dt["wg0T"] = nc.dram_tensor("wg0T", [D, L0_HID], I32, kind="ExternalInput").ap()
        dt["wu0T"] = nc.dram_tensor("wu0T", [D, L0_HID], I32, kind="ExternalInput").ap()
        dt["wd0T"] = nc.dram_tensor("wd0T", [L0_HID, D], I32, kind="ExternalInput").ap()
        dt["wqkvT"] = nc.dram_tensor("wqkvT", [D, 3 * D], F32, kind="ExternalInput").ap()
        dt["woutT"] = nc.dram_tensor("woutT", [D, D], F32, kind="ExternalInput").ap()
        dt["wg1T"] = nc.dram_tensor("wg1T", [D, L1_HID], F32, kind="ExternalInput").ap()
        dt["wu1T"] = nc.dram_tensor("wu1T", [D, L1_HID], F32, kind="ExternalInput").ap()
        dt["wd1T"] = nc.dram_tensor("wd1T", [L1_HID, D], F32, kind="ExternalInput").ap()
        self.dt = dt
        self.out = nc.dram_tensor("out", [D, T], F32, kind="ExternalOutput").ap()
        self.x1_sp = nc.dram_tensor("x1_sp", [D, T], F32, kind="Internal").ap()
        self.xq_sp = nc.dram_tensor("xq_sp", [D, T], BF16, kind="Internal").ap()
        self.s1_sp = nc.dram_tensor("s1_sp", [D, T], BF16, kind="Internal").ap()
        self.wgq_sc = nc.dram_tensor("wgq_sc", [D, L1_HID], BF16, kind="Internal").ap()
        self.wuq_sc = nc.dram_tensor("wuq_sc", [D, L1_HID], BF16, kind="Internal").ap()
        self.wdq_sc = nc.dram_tensor("wdq_sc", [L1_HID, D], BF16, kind="Internal").ap()

        sb = self.sbuf
        self.XN_O = 0
        self.W1_O = 16384
        self.QT_O = 32768
        self.EX_O = 49152
        self.WVS_O = 65536
        self.RG_O = 69632
        self.BF = sb("BF", [P, 71936], BF16)
        self.FW = sb("FW", [P, 8192], F32)
        self.MU = sb("MU", [P, max(T, 2048)], F32)
        self.RS = sb("RS", [P, max(T, 512)], F32)
        self.SXR = sb("SXR", [P, T], F32)
        self.SCR = sb("SCR", [P, 512], F32)
        self.STAT = sb("STAT", [P, 1024], F32)
        self.SMALL = sb("SMALL", [P, 16], F32)
        self.cpool = sb("cpool", [P, 16], F32)
        self.consts = {}
        self.ones = sb("ones", [P, 128], BF16)
        self.maskT2 = sb("maskT2", [P, 2 * P], BF16)
        self.bias6 = sb("bias6c", [P, 6 * NCH], F32)
        self.Sst = [sb(f"Sst{i}", [P, 8 * HD], BF16) for i in range(2)]
        self.IDF = sb("IDF", [P, P], F32)
        self.ONEC = sb("ONEC", [1, P], F32)
        self.psum = self.stack.enter_context(nc.psum_tensor("ps", [P, 4096], F32))
        self.bank_free = [("te", 0)] * 8

        self.xn = [self.bf(self.XN_O + i * T, T) for i in range(NCH)]
        if "upd" in self.dbg_names:
            self.dbg_upd = nc.dram_tensor("dbg_upd", [D, T], F32,
                                          kind="ExternalOutput").ap()
        self.win_te = {}
        self.win_ve = {}

        self.marks = {}

        def mark(nm):
            self.marks[nm] = {e: len(self.streams[e]) for e in ENGS}
        self.P0_setup()
        mark("P1")
        self.P1_preln0()
        if self.last_stage >= 2:
            mark("P2")
            self.P2_l0_attn()
        if self.last_stage >= 3:
            mark("P3")
            self.P3_innerln_wo()
        if self.last_stage >= 4:
            mark("P4")
            self.P4_l0_ffn()
        self.p5_tasks = []
        if self.last_stage >= 6:
            mark("P6")
            self.P6_preln1()
        if self.last_stage >= 7:
            if self.last_stage >= 5:
                self.p5_tasks = self.make_p5_tasks()
            mark("P7")
            self.P7_qkv()
        if self.last_stage >= 8:
            mark("P8")
            self.P8_l1_attn()
            mark("P8b")
            self.P8b_wout()
        if self.last_stage >= 5:
            if self.last_stage < 7:
                self.p5_tasks = self.make_p5_tasks()
            mark("P5")
            self.P5_wquant()
        if self.last_stage >= 9:
            mark("P9")
            self.P9_l1_ffn()
        mark("END")
        if self.last_stage < 9 and self.last_stage >= 4:
            for s in ("sy", "ve", "se", "so"):
                self.wait_all("sync", s)
            for i in range(NCH):
                self.dma("sync", self.out[i * P:(i + 1) * P, :],
                         self.x1_sp[i * P:(i + 1) * P, :], "so")
        elif self.last_stage < 4:
            self.wait_all("sync", "ve")
            for i in range(NCH):
                self.dma("sync", self.out[i * P:(i + 1) * P, :],
                         self.MU[:, 0:T], "so")
        self.wait_all("sync", "so")
        self.wait_all("sync", "dbg")
        self.wait_all("sync", "dbg2")
        self.wait_all("sync", "sy")
        self.wait_all("sync", "oA")
        self.wait_all("sync", "oB")
        return self.finalize()

    def bf(self, off, n):
        return self.BF[:, off:off + n]

    def bias_ap(self, j, m):
        return self.bias6[:, j * NCH + m: j * NCH + m + 1]

    def P0_setup(self):
        dt = self.dt
        self.const(LN_EPS)
        self.const(0.125)
        self.const(1.0 / 127.0)
        self.const(128.5)
        self.const(-128.0)
        self.emit("vector", lambda e: e.memset(self.ones[:], 1.0))
        self.emit("gpsimd", lambda e: e.memset(self.maskT2[:], 1.0))
        self.emit("gpsimd", lambda e: e.memset(self.Sst[0][:], 0.0))
        self.emit("gpsimd", lambda e: e.memset(self.Sst[1][:], 0.0))
        self.drain("gpsimd")
        self.emit("gpsimd", lambda e: e.affine_select(
            self.maskT2[:], self.maskT2[:], pattern=[[0, 2], [1, 128]],
            compare_op=ALU.is_ge, fill=0.0, base=0, channel_multiplier=-1),
            inc="ge")
        ld = self.dma("sync", self.bias6[:].rearrange("p (j m) -> p j m", m=NCH),
                      dt["bias6"].rearrange("(m p) j -> p j m", p=P), "sy",
                      slow_ok=True)
        self.wait("vector", "sy", ld)
        idf = self.IDF[:]
        self.emit("gpsimd", lambda e: e.memset(idf, 1.0))
        self.drain("gpsimd")
        self.emit("gpsimd", lambda e: e.affine_select(
            idf, idf, pattern=[[-1, 128]], compare_op=ALU.is_equal,
            fill=0.0, base=0, channel_multiplier=1), inc="ge")
        self.emit("vector", lambda e: e.memset(self.ONEC[:], 1.0), inc="ve")
        v_t = self.milestone("vector", "ve")
        self.wait("tensor", "ve", v_t)
        self.wait("tensor", "ge", self.cnt["ge"])
        te_t = self.emit("tensor", lambda e: e.transpose(
            self.psum[0:16, 0:128], self.bias6[:, NCH:3 * NCH], idf),
            inc="te")
        self.wait("vector", "te", te_t)
        bT = self.STAT[0:16, 0:128]
        v_t = self.emit("vector", lambda e, bT=bT: e.tensor_copy(
            bT, self.psum[0:16, 0:128]), inc="ve")
        self.wait("gpsimd", "ve", v_t)
        self.brow = self.bf(self.RG_O, 2048)
        for j in range(2):
            self.dma("gpsimd",
                     self.brow[0:1, j * D:(j + 1) * D].rearrange(
                         "o (m f) -> o m f", m=NCH),
                     bT[j * NCH:(j + 1) * NCH, :], "bro")
        self.wait_all("vector", "bro")
        self.wait_all("scalar", "bro")

    def ln_stats_from_windows(self, src_dram, write_xn):
        T, TT, NTT = self.T, self.TT, self.NTT
        sq = self.bf(self.QT_O, 512)[:, 0:self.TT]
        wins = [[self.FW[:, i * TT:(i + 1) * TT] for i in range(NCH)],
                [self.FW[:, 4096 + i * TT:4096 + (i + 1) * TT] for i in range(NCH)]]
        ld_t = {}
        self.wait_all("sync", "ve")
        self.wait_all("sync", "te")

        def issue(tt):
            w = wins[tt % 2]
            sm = "wA" if tt % 2 == 0 else "wB"
            for i in range(NCH):
                t = self.dma("sync", w[i],
                             src_dram[i * P:(i + 1) * P, tt * TT:(tt + 1) * TT],
                             sm)
            ld_t[tt] = (sm, t)
        for tt in range(min(2, NTT)):
            issue(tt)
        sq8 = [self.bf(self.QT_O + k * 512, 512)[:, 0:TT] for k in range(NCH)]
        for tt in range(NTT):
            w = wins[tt % 2]
            c0 = tt * TT
            self.wait("vector", ld_t[tt][0], ld_t[tt][1])
            if tt > 0:
                self.wait("vector", "te", self.win_te[tt - 1])
            self.use_bank(6)
            self.use_bank(7)
            for k in range(NCH):
                if write_xn:
                    self.emit("vector", lambda e, k=k, c0=c0, w=w: e.tensor_copy(
                        self.xn[k][:, c0:c0 + TT], w[k]), inc="ve")
                self.emit("vector", lambda e, k=k, w=w: e.tensor_tensor(
                    sq8[k], w[k], w[k], ALU.mult), inc="ve")
            v_t = self.milestone("vector", "ve")
            self.wait("tensor", "ve", v_t)
            for k in range(NCH):
                last = (k == NCH - 1)
                self.emit("tensor", lambda e, k=k, c0=c0, last=last: e.matmul(
                    self.bank(6, TT), self.ones[:, 0:128],
                    self.xn[k][:, c0:c0 + TT],
                    start=(k == 0), stop=last), inc="te" if last else None)
                self.emit("tensor", lambda e, k=k, last=last: e.matmul(
                    self.bank(7, TT), self.ones[:, 0:128], sq8[k],
                    start=(k == 0), stop=last), inc="te" if last else None)
            te_t = self.cnt["te"]
            self.win_te[tt] = te_t
            if tt + 2 < NTT:
                self.wait("sync", "te", te_t)
                issue(tt + 2)
            self._ln_tail(te_t, c0, TT)
        self.drain("vector")

    def ln_stats_from_tiles(self, tiles):
        T, TT, NTT = self.T, self.TT, self.NTT
        sq8 = [self.bf(self.QT_O + k * 512, 512)[:, 0:self.TT]
               for k in range(NCH)]
        prev_te = 0
        for tt in range(NTT):
            c0 = tt * TT
            self.use_bank(6)
            self.use_bank(7)
            if prev_te > 0:
                self.wait("vector", "te", prev_te)
            for k in range(NCH):
                self.emit("vector", lambda e, k=k, c0=c0: e.tensor_tensor(
                    sq8[k], tiles[k][:, c0:c0 + TT], tiles[k][:, c0:c0 + TT],
                    ALU.mult), inc="ve")
            v_t = self.milestone("vector", "ve")
            self.wait("tensor", "ve", v_t)
            for k in range(NCH):
                last = (k == NCH - 1)
                self.emit("tensor", lambda e, k=k, c0=c0, last=last: e.matmul(
                    self.bank(6, TT), self.ones[:, 0:128],
                    tiles[k][:, c0:c0 + TT],
                    start=(k == 0), stop=last), inc="te" if last else None)
                self.emit("tensor", lambda e, k=k, last=last: e.matmul(
                    self.bank(7, TT), self.ones[:, 0:128], sq8[k],
                    start=(k == 0), stop=last), inc="te" if last else None)
            prev_te = self.cnt["te"]
            self._ln_tail(prev_te, c0, TT)
        self.drain("vector")

    def _ln_tail(self, te_t, c0, TTc):
        st = self.STAT
        mu = self.MU[:, c0:c0 + TTc]
        rstd = self.RS[:, c0:c0 + TTc]
        self.wait("vector", "te", te_t)
        self.emit("vector", lambda e, mu=mu: e.tensor_scalar_mul(
            mu, self.bank(6, TTc), 1.0 / D), inc="ve")
        self.emit("vector", lambda e: e.tensor_scalar_mul(
            st[:, 0:TTc], self.bank(7, TTc), 1.0 / D), inc="ve")
        self.drain("vector")
        self.emit("vector", lambda e, mu=mu: e.tensor_tensor(
            st[:, TTc:2 * TTc], mu, mu, ALU.mult), inc="ve")
        self.drain("vector")
        self.emit("vector", lambda e: e.tensor_tensor(
            st[:, 0:TTc], st[:, 0:TTc], st[:, TTc:2 * TTc], ALU.subtract),
            inc="ve")
        v_t = self.milestone("vector", "ve")
        self.release_bank(6, "ve", v_t)
        self.release_bank(7, "ve", v_t)
        self.wait("scalar", "ve", v_t)
        s_t = self.emit("scalar", lambda e: e.activation(
            st[:, TTc:2 * TTc], st[:, 0:TTc], AF.Sqrt, bias=LN_EPS), inc="se")
        self.wait("vector", "se", s_t)
        self.emit("vector", lambda e, rstd=rstd: e.reciprocal(
            rstd, st[:, TTc:2 * TTc]), inc="ve")
        self.drain("vector")

    def ln_apply_windows(self, src_dram, dst_tiles, absmax=False):
        T, TT, NTT = self.T, self.TT, self.NTT
        wins = [[self.FW[:, i * TT:(i + 1) * TT] for i in range(NCH)],
                [self.FW[:, 4096 + i * TT:4096 + (i + 1) * TT] for i in range(NCH)]]
        scr = self.SCR[:, 0:TT]
        if absmax:
            self.emit("vector", lambda e: e.memset(self.SXR[:], 0.0), inc="ve")
            self.drain("vector")
        ld_t = {}
        self.wait_all("sync", "ve")
        self.wait_all("sync", "te")

        def issue(tt):
            w = wins[tt % 2]
            sm = "wA" if tt % 2 == 0 else "wB"
            for i in range(NCH):
                t = self.dma("sync", w[i],
                             src_dram[i * P:(i + 1) * P, tt * TT:(tt + 1) * TT],
                             sm)
            ld_t[tt] = (sm, t)
        for tt in range(min(2, NTT)):
            issue(tt)
        for tt in range(NTT):
            w = wins[tt % 2]
            c0 = tt * TT
            self.wait("vector", ld_t[tt][0], ld_t[tt][1])
            for i in range(NCH):
                self.emit("vector", lambda e, i=i, w=w, c0=c0: e.tensor_tensor(
                    scr, w[i], self.MU[:, c0:c0 + TT], ALU.subtract), inc="ve")
                self.drain("vector")
                self.emit("vector", lambda e, c0=c0: e.tensor_tensor(
                    scr, scr, self.RS[:, c0:c0 + TT], ALU.mult), inc="ve")
                self.drain("vector")
                self.emit("vector", lambda e, i=i, c0=c0: e.tensor_copy(
                    dst_tiles[i][:, c0:c0 + TT], scr), inc="ve")
                if absmax:
                    self.drain("vector")
                    self.emit("vector", lambda e: e.scalar_tensor_tensor(
                        scr, scr, -1.0, scr, ALU.mult, ALU.max), inc="ve")
                    self.drain("vector")
                    self.emit("vector", lambda e, c0=c0: e.tensor_tensor(
                        self.SXR[:, c0:c0 + TT], self.SXR[:, c0:c0 + TT],
                        scr, ALU.max), inc="ve")
                self.drain("vector")
            self.win_ve[tt] = self.cnt["ve"]
            if tt + 2 < NTT:
                self.wait("sync", "ve", self.cnt["ve"])
                issue(tt + 2)
        if absmax:
            self.bcast_ptok_max()
        self.drain("vector")

    def bcast_ptok_max(self):
        T, TT, NTT = self.T, self.TT, self.NTT
        NG = T // P
        v_t = self.milestone("vector", "ve")
        self.wait("tensor", "ve", v_t)
        for g in range(NG):
            self.use_bank(6)
            te_t = self.emit("tensor", lambda e, g=g: e.transpose(
                self.bank(6, 128), self.SXR[:, g * P:(g + 1) * P], self.IDF[:]),
                inc="te")
            self.wait("vector", "te", te_t)
            v_t = self.emit("vector", lambda e, g=g: e.reduce_max(
                self.STAT[:, g:g + 1], self.bank(6, 128), axis=AX.X), inc="ve")
            self.drain("vector")
            self.release_bank(6, "ve", v_t)
        v_t = self.milestone("vector", "ve")
        self.wait("tensor", "ve", v_t)
        self.use_bank(6)
        te_t = self.emit("tensor", lambda e: e.transpose(
            self.psum[0:NG, 512 * 6:512 * 6 + P],
            self.STAT[:, 0:NG], self.IDF[:]), inc="te")
        self.wait("vector", "te", te_t)
        v_t = self.emit("vector", lambda e: e.tensor_copy(
            self.STAT[0:NG, 256:256 + P],
            self.psum[0:NG, 512 * 6:512 * 6 + P]), inc="ve")
        self.drain("vector")
        self.release_bank(6, "ve", v_t)
        self.wait("sync", "ve", v_t)
        d_t = self.dma("sync",
                       self.SXR[0:1, 0:T].rearrange("o (g f) -> o g f", g=NG),
                       self.STAT[0:NG, 256:256 + P], "sb2")
        self.wait("vector", "sb2", d_t)
        v_t = self.emit("vector", lambda e: e.tensor_single_scalar(
            self.SXR[0:1, 0:T], self.SXR[0:1, 0:T], 1e-5, ALU.max), inc="ve")
        self.drain("vector")
        v_t = self.milestone("vector", "ve")
        self.wait("tensor", "ve", v_t)
        for tt in range(NTT):
            c0 = tt * TT
            self.use_bank(7)
            te_t = self.emit("tensor", lambda e, c0=c0: e.matmul(
                self.bank(7, TT), self.ONEC[0:1, :],
                self.SXR[0:1, c0:c0 + TT], start=True, stop=True), inc="te")
            self.wait("vector", "te", te_t)
            v_t = self.emit("vector", lambda e, c0=c0: e.tensor_copy(
                self.SXR[:, c0:c0 + TT], self.bank(7, TT)), inc="ve")
            self.drain("vector")
            self.release_bank(7, "ve", v_t)

    def P1_preln0(self):
        self.ln_stats_from_windows(self.dt["x"], write_xn=True)
        self.ln_apply_windows(self.dt["x"], self.xn)
        self.dbg_dump("xn", list(self.xn), src_sems=("ve",))

    def proj_fm(self, lhsT_fn, rhs_tiles, rhs_c0, N, out_writer, banks,
                nk=NCH, m_range=None, bias_row=None):
        if m_range is None:
            m_range = range(NCH)
        bi = 0
        for m in m_range:
            b = banks[bi % len(banks)]
            bi += 1
            self.use_bank(b)
            for k in range(nk):
                last = (k == nk - 1) and bias_row is None
                self.emit("tensor",
                          lambda e, k=k, m=m, b=b, last=last: e.matmul(
                              self.bank(b, N), lhsT_fn(k, m),
                              rhs_tiles[k][:, rhs_c0:rhs_c0 + N],
                              start=(k == 0), stop=last),
                          inc="te" if last else None)
            if bias_row is not None:
                self.emit("tensor", lambda e, m=m, b=b: e.matmul(
                    self.bank(b, N), bias_row(m), self.ones[0:1, 0:N],
                    start=False, stop=True), inc="te")
            te_t = self.cnt["te"]
            sem, cnt = out_writer(m, self.bank(b, N), te_t)
            self.release_bank(b, sem, cnt)

    def elu1_writer(self, dst_tile_fn, bias_j):
        scr = self.SCR[:, 0:512]

        def w(m, ps, te_t):
            TTc = ps.shape[-1]
            self.wait("vector", "te", te_t)
            pv = getattr(self, "_elu_prev_v", 0)
            if pv > 0:
                self.wait("vector", "se", self.cnt["se"])
            bap = self.bias_ap(bias_j, m)
            self.emit("vector", lambda e, bap=bap: e.tensor_scalar(
                scr[:, 0:TTc], ps, bap, 0.0, ALU.add, ALU.min), inc="ve")
            self.emit("vector", lambda e, bap=bap: e.tensor_scalar(
                self.STAT[:, 0:TTc], ps, bap, 0.0, ALU.add, ALU.max),
                inc="ve")
            v_m = self.milestone("vector", "ve")
            self.wait("scalar", "ve", v_m)
            s_t = self.emit("scalar", lambda e: e.activation(
                scr[:, 0:TTc], scr[:, 0:TTc], AF.Exp), inc="se")
            self.wait("vector", "se", s_t)
            v_t = self.emit("vector", lambda e: e.tensor_tensor(
                dst_tile_fn(m)[:, 0:TTc], scr[:, 0:TTc], self.STAT[:, 0:TTc],
                ALU.add), inc="ve")
            self.drain("vector")
            self._elu_prev_v = v_t
            return ("ve", v_t)
        return w

    def P2_l0_attn(self):
        self._elu_prev_v = self.cnt["ve"]
        self.wait("tensor", "ve", self.cnt["ve"])
        self.wait("tensor", "se", self.cnt["se"])
        for q_ in ("gpsimd", "sync"):
            self.wait(q_, "te", self.cnt["te"])
            self.wait(q_, "ve", self.cnt["ve"])
            self.wait(q_, "se", self.cnt["se"])
        T, TT, NTT = self.T, self.TT, self.NTT
        dt = self.dt
        wq = self.bf(self.W1_O, 8192)
        wk = self.bf(self.W1_O + 8192, 8192)
        for name, dst in (("wqT", wq), ("wkT", wk)):
            ldw = self.dma("gpsimd", dst.rearrange("p (k m) -> p k m", k=NCH),
                           dt[name].rearrange("(k p) m -> p k m", p=P), "gd")
        wvs = self.bf(self.WVS_O, 4096)
        q2 = self.bf(self.QT_O, 4096)
        k2f = self.bf(self.QT_O + 4096, 4096)
        k2t = self.bf(self.QT_O + 8192, 4096)
        v2 = self.bf(self.QT_O + 12288, 4096)
        ab = [self.bf(self.RG_O + 2048, 256), self.bf(self.RG_O + 2048, 256)]
        o_t = [self.bf(self.EX_O + p * T, T) for p in range(NCH)]
        wt = lambda w, k, m: w[:, k * 1024 + m * 128: k * 1024 + (m + 1) * 128]

        self.wait("tensor", "gd", self.cnt["gd"])
        self.wait_all("tensor", "bro")
        nchunks = TT // P
        wvs_cons = 0
        for tt in range(NTT):
            c0 = tt * TT
            self.proj_fm(lambda k, m: wt(wq, k, m), self.xn, c0, TT,
                         self.elu1_writer(
                             lambda m: q2[:, m * TT:(m + 1) * TT], 0),
                         banks=[4, 5])
            self.proj_fm(lambda k, m: wt(wk, k, m), self.xn, c0, TT,
                         self.elu1_writer(
                             lambda m: k2f[:, m * TT:(m + 1) * TT], 1),
                         banks=[4, 5])
            for tj in range(nchunks):
                for n in range(2):
                    b = [4, 5][(tj * 2 + n) % 2]
                    self.use_bank(b)
                    for k in range(NCH):
                        self.emit("tensor", lambda e, k=k, tj=tj, n=n, b=b,
                                  c0=c0:
                                  e.matmul(self.bank(b, 512),
                                           self.xn[k][:, c0 + tj * P:c0 + (tj + 1) * P],
                                           wk[:, k * 1024 + n * 512:
                                              k * 1024 + (n + 1) * 512],
                                           start=(k == 0), stop=False))
                    self.emit("tensor", lambda e, tj=tj, n=n, b=b: e.matmul(
                        self.bank(b, 512), self.ones[0:1, 0:128],
                        self.brow[0:1, n * 512:(n + 1) * 512],
                        start=False, stop=True), inc="te")
                    te_t = self.cnt["te"]
                    scr = self.SCR[:, 0:512]
                    self.wait("vector", "te", te_t)
                    self.wait("vector", "se", self.cnt["se"])
                    self.emit("vector", lambda e, b=b: e.tensor_single_scalar(
                        scr, self.bank(b, 512), 0.0, ALU.min), inc="ve")
                    self.emit("vector", lambda e, b=b: e.tensor_single_scalar(
                        self.STAT[:, 0:512], self.bank(b, 512), 0.0, ALU.max),
                        inc="ve")
                    v_m = self.milestone("vector", "ve")
                    self.wait("scalar", "ve", v_m)
                    s_t = self.emit("scalar", lambda e: e.activation(
                        scr, scr, AF.Exp), inc="se")
                    self.wait("vector", "se", s_t)
                    v_t = self.emit("vector", lambda e, tj=tj, n=n: e.tensor_tensor(
                        k2t[:, tj * 1024 + n * 512: tj * 1024 + (n + 1) * 512],
                        scr, self.STAT[:, 0:512], ALU.add), inc="ve")
                    self.drain("vector")
                    self._elu_prev_v = v_t
                    self.release_bank(b, "ve", v_t)
            for n in range(2):
                if wvs_cons > 0:
                    self.wait("gpsimd", "te", wvs_cons)
                ldv = self.dma("gpsimd",
                               wvs.rearrange("p (k m) -> p k m", k=NCH),
                               dt["wvT"][:, n * 512:(n + 1) * 512].rearrange(
                                   "(k p) m -> p k m", p=P), "gv")
                self.wait("tensor", "gv", ldv)
                for tj in range(nchunks):
                    b = [4, 5][tj % 2]
                    self.use_bank(b)
                    for k in range(NCH):
                        self.emit("tensor", lambda e, k=k, tj=tj, b=b, c0=c0:
                                  e.matmul(self.bank(b, 512),
                                           self.xn[k][:, c0 + tj * P:c0 + (tj + 1) * P],
                                           wvs[:, k * 512:(k + 1) * 512],
                                           start=(k == 0), stop=False))
                    self.emit("tensor", lambda e, n=n, b=b: e.matmul(
                        self.bank(b, 512), self.ones[0:1, 0:128],
                        self.brow[0:1, 1024 + n * 512:1024 + (n + 1) * 512],
                        start=False, stop=True), inc="te")
                    te_t = self.cnt["te"]
                    if tj == nchunks - 1:
                        wvs_cons = te_t
                    self.wait("vector", "te", te_t)
                    v_t = self.emit("vector", lambda e, tj=tj, n=n, b=b:
                                    e.tensor_copy(
                                        v2[:, tj * 1024 + n * 512:
                                           tj * 1024 + (n + 1) * 512],
                                        self.bank(b, 512)), inc="ve")
                    self.drain("vector")
                    self.release_bank(b, "ve", v_t)
            if tt == 0:
                self.dbg_dump("q0f", [q2[:, m * TT:(m + 1) * TT]
                                      for m in range(NCH)], src_sems=("ve",))
                self.dbg_dump("k0f", [k2f[:, m * TT:(m + 1) * TT]
                                      for m in range(NCH)], src_sems=("ve",))
                self.dbg_dump("k0t", [k2t[:, j * 1024:(j + 1) * 1024]
                                      for j in range(nchunks)],
                              src_sems=("ve",))
                self.dbg_dump("v0t", [v2[:, j * 1024:(j + 1) * 1024]
                                      for j in range(nchunks)],
                              src_sems=("ve",))
            if self.p2_sub < 2:
                continue
            for cl in range(nchunks):
                c = tt * nchunks + cl
                tl = cl * P
                Sold = self.Sst[c % 2]
                Snew = self.Sst[(c + 1) % 2]
                self.use_bank(6)
                for p in range(NCH):
                    sb0 = [0, 1][p % 2]
                    sb1 = [2, 3][p % 2]
                    self.use_bank(sb0)
                    self.use_bank(sb1)
                    for lane in range(2):
                        r0, r1 = lane * 64, lane * 64 + 64
                        bsl = sb0 if lane == 0 else sb1
                        self.emit("tensor", lambda e, p=p, tl=tl, bsl=bsl,
                                  r0=r0, r1=r1: e.matmul(
                                      self.bank(bsl, 128),
                                      k2f[r0:r1, p * TT + tl:p * TT + tl + P],
                                      q2[r0:r1, p * TT + tl:p * TT + tl + P],
                                      start=True, stop=True,
                                      tile_position=(r0, 0)), inc="te")
                    te_t = self.cnt["te"]
                    abt = ab[p % 2]
                    self.wait("vector", "te", te_t)
                    self.emit("vector", lambda e, sb0=sb0, abt=abt:
                              e.tensor_tensor(abt[:, 0:128],
                                              self.bank(sb0, 128),
                                              self.maskT2[:, 0:128],
                                              ALU.mult), inc="ve")
                    self.drain("vector")
                    v_t = self.emit("vector", lambda e, sb1=sb1, abt=abt:
                                    e.tensor_tensor(abt[:, 128:256],
                                                    self.bank(sb1, 128),
                                                    self.maskT2[:, 128:256],
                                                    ALU.mult), inc="ve")
                    self.drain("vector")
                    self.release_bank(sb0, "ve", v_t)
                    self.release_bank(sb1, "ve", v_t)
                    if self.p2_sub < 3:
                        continue
                    ob = 4 + p // 4
                    oc = (p % 4) * 128
                    if oc == 0:
                        self.use_bank(ob)
                    self.wait("tensor", "ve", v_t)
                    for lane in range(2):
                        r0 = lane * 64
                        self.emit("tensor", lambda e, p=p, tl=tl, ob=ob, oc=oc,
                                  abt=abt, r0=r0, lane=lane: e.matmul(
                                      self.bank(ob, 128, off=oc,
                                                p0=r0, p1=r0 + 64),
                                      v2[:, (tl // P) * 1024 + p * 128 + r0:
                                         (tl // P) * 1024 + p * 128 + r0 + 64],
                                      abt[:, lane * 128:lane * 128 + 128],
                                      start=True, stop=False,
                                      tile_position=(0, r0)))
                        self.emit("tensor", lambda e, p=p, tl=tl, ob=ob, oc=oc,
                                  r0=r0, Sold=Sold: e.matmul(
                                      self.bank(ob, 128, off=oc,
                                                p0=r0, p1=r0 + 64),
                                      Sold[r0:r0 + 64, p * 64:p * 64 + 64],
                                      q2[r0:r0 + 64,
                                         p * TT + tl:p * TT + tl + P],
                                      start=False, stop=True,
                                      tile_position=(r0, r0)), inc="te")
                    if self.p2_sub < 4:
                        continue
                    for lane in range(2):
                        r0 = lane * 64
                        self.emit("tensor", lambda e, p=p, tl=tl, r0=r0, c=c:
                                  e.matmul(
                                      self.bank(6, 64, off=p * 64,
                                                p0=r0, p1=r0 + 64),
                                      k2t[:, (tl // P) * 1024 + p * 128 + r0:
                                          (tl // P) * 1024 + p * 128 + r0 + 64],
                                      v2[:, (tl // P) * 1024 + p * 128 + r0:
                                         (tl // P) * 1024 + p * 128 + r0 + 64],
                                      start=(p == 0),
                                      stop=(p == NCH - 1),
                                      tile_position=(0, r0)), inc="te")
                te_t = self.cnt["te"]
                if self.p2_sub < 3:
                    continue
                self.wait("vector", "te", te_t)
                for p in range(NCH):
                    ob = 4 + p // 4
                    oc = (p % 4) * 128
                    v_t = self.emit("vector", lambda e, p=p, ob=ob, oc=oc, c=c:
                                    e.tensor_copy(
                                        o_t[p][:, c * P:(c + 1) * P],
                                        self.bank(ob, 128, off=oc)), inc="ve")
                v_t = self.milestone("vector", "ve")
                self.release_bank(4, "ve", v_t)
                self.release_bank(5, "ve", v_t)
                if self.p2_sub < 4:
                    continue
                self.emit("vector", lambda e, Snew=Snew, Sold=Sold:
                          e.tensor_tensor(Snew[:], Sold[:],
                                          self.bank(6, 512), ALU.add),
                          inc="ve")
                v_t = self.milestone("vector", "ve")
                self.release_bank(6, "ve", v_t)
                self.wait("tensor", "ve", v_t)
        self.o_t = o_t
        self.dbg_dump("o0", list(o_t), src_sems=("ve",))

    def P3_innerln_wo(self):
        self._elu_prev_v = self.cnt["ve"]
        self.wait("tensor", "ve", self.cnt["ve"])
        self.wait("tensor", "se", self.cnt["se"])
        for q_ in ("gpsimd", "sync"):
            self.wait(q_, "te", self.cnt["te"])
            self.wait(q_, "ve", self.cnt["ve"])
            self.wait(q_, "se", self.cnt["se"])
        T, TT, NTT = self.T, self.TT, self.NTT
        dt = self.dt
        o_t = self.o_t
        self.ln_stats_from_tiles(o_t)
        oln = [self.bf(self.W1_O + i * T, T) for i in range(NCH)]
        scr = self.SCR[:, 0:TT]
        for tt in range(NTT):
            c0 = tt * TT
            for i in range(NCH):
                self.emit("vector", lambda e, i=i, c0=c0: e.tensor_tensor(
                    scr, o_t[i][:, c0:c0 + TT], self.MU[:, c0:c0 + TT],
                    ALU.subtract), inc="ve")
                self.drain("vector")
                self.emit("vector", lambda e, c0=c0: e.tensor_tensor(
                    scr, scr, self.RS[:, c0:c0 + TT], ALU.mult), inc="ve")
                self.drain("vector")
                self.emit("vector", lambda e, i=i, c0=c0: e.tensor_scalar(
                    oln[i][:, c0:c0 + TT], scr, self.bias_ap(4, i),
                    self.bias_ap(5, i), ALU.mult, ALU.add), inc="ve")
                self.drain("vector")
        v_t = self.milestone("vector", "ve")
        self.wait("gpsimd", "te", self.cnt["te"])
        wo = self.bf(self.QT_O, 8192)
        ldw = self.dma("gpsimd", wo.rearrange("p (k m) -> p k m", k=NCH),
                       dt["woT"].rearrange("(k p) m -> p k m", p=P), "gd")
        self.wait("tensor", "gd", ldw)
        self.wait("tensor", "ve", v_t)
        s_t = [self.bf(self.EX_O + i * T, T) for i in range(NCH)]

        def s_writer_tt(c0):
            def w(m, ps, te_t):
                self.wait("vector", "te", te_t)
                v_t = self.emit("vector", lambda e, m=m: e.tensor_scalar_add(
                    s_t[m][:, c0:c0 + TT], ps, self.bias_ap(3, m)), inc="ve")
                self.drain("vector")
                return ("ve", v_t)
            return w
        for tt in range(NTT):
            c0 = tt * TT
            self.proj_fm(lambda k, m: wo[:, k * 1024 + m * 128:
                                         k * 1024 + (m + 1) * 128],
                         oln, c0, TT, s_writer_tt(c0), banks=[4, 5])
        self.s_t = s_t
        self.dbg_dump("s0", list(s_t), src_sems=("ve",))

    def P4_l0_ffn(self):
        self._elu_prev_v = self.cnt["ve"]
        self.wait("tensor", "ve", self.cnt["ve"])
        self.wait("tensor", "se", self.cnt["se"])
        for q_ in ("gpsimd", "sync"):
            self.wait(q_, "te", self.cnt["te"])
            self.wait(q_, "ve", self.cnt["ve"])
            self.wait(q_, "se", self.cnt["se"])
        T, TT, NTT = self.T, self.TT, self.NTT
        dt = self.dt
        s_t = self.s_t
        h_t = [self.bf(self.W1_O + i * TT, TT) for i in range(L0_HID // P)]
        g_s = [self.bf(self.QT_O + i * 4096, 4096) for i in range(2)]
        u_s = [self.bf(self.QT_O + 8192 + i * 4096, 4096) for i in range(2)]
        wd_s = [self.bf(self.W1_O + 12288, 4096), self.bf(self.WVS_O, 4096)]
        xw = self.FW[:, 0:TT]
        NS = L0_HID // 512
        g_cons = [0, 0]
        u_cons = [0, 0]
        wd_cons = [0, 0]
        last_down_te = 0
        prev_evac_v = 0
        for tt in range(NTT):
            c0 = tt * TT
            if last_down_te > 0:
                self.wait("vector", "te", last_down_te)
            for hs in range(NS):
                par = hs % 2
                if g_cons[par] > 0:
                    self.wait("gpsimd", "se", g_cons[par])
                    self.wait("gpsimd", "ve", u_cons[par])
                smp = "gA" if par == 0 else "gB"
                ldg = self.dma("gpsimd",
                               g_s[par].rearrange("p (k m) -> p k m", k=NCH),
                               dt["wg0T"][:, hs * 512:(hs + 1) * 512].rearrange(
                                   "(k p) m -> p k m", p=P), smp)
                ldu = self.dma("gpsimd",
                               u_s[par].rearrange("p (k m) -> p k m", k=NCH),
                               dt["wu0T"][:, hs * 512:(hs + 1) * 512].rearrange(
                                   "(k p) m -> p k m", p=P), smp)
                self.wait("tensor", smp, ldu)
                for mh in range(4):
                    hh = hs * 4 + mh
                    bg, bu = [0, 1][mh % 2], [2, 3][mh % 2]
                    self.use_bank(bg)
                    self.use_bank(bu)
                    for k in range(NCH):
                        last = (k == NCH - 1)
                        self.emit("tensor", lambda e, k=k, mh=mh, bg=bg,
                                  par=par, c0=c0, last=last: e.matmul(
                                      self.bank(bg, TT),
                                      g_s[par][:, k * 512 + mh * 128:
                                               k * 512 + (mh + 1) * 128],
                                      self.xn[k][:, c0:c0 + TT],
                                      start=(k == 0), stop=last),
                                  inc="te" if last else None)
                        self.emit("tensor", lambda e, k=k, mh=mh, bu=bu,
                                  par=par, c0=c0, last=last: e.matmul(
                                      self.bank(bu, TT),
                                      u_s[par][:, k * 512 + mh * 128:
                                               k * 512 + (mh + 1) * 128],
                                      self.xn[k][:, c0:c0 + TT],
                                      start=(k == 0), stop=last),
                                  inc="te" if last else None)
                    te_t = self.cnt["te"]
                    scr = self.SCR[:, 0:TT]
                    self.wait("scalar", "te", te_t)
                    if prev_evac_v > 0:
                        self.wait("scalar", "ve", prev_evac_v)
                    s_tt = self.emit("scalar", lambda e, bg=bg: e.activation(
                        scr, self.bank(bg, TT), AF.Silu, scale=1.0 / 127.0),
                        inc="se")
                    g_cons[par] = s_tt
                    self.wait("vector", "se", s_tt)
                    v_t = self.emit("vector", lambda e, hh=hh, bu=bu:
                                    e.scalar_tensor_tensor(
                                        h_t[hh], self.bank(bu, TT),
                                        1.0 / 127.0, scr,
                                        ALU.mult, ALU.mult), inc="ve")
                    u_cons[par] = v_t
                    prev_evac_v = v_t
                    self.drain("vector")
                    self.release_bank(bg, "se", s_tt)
                    self.release_bank(bu, "ve", v_t)
            h_done = self.cnt["ve"]
            self.wait("tensor", "ve", h_done)
            NSD = 6
            dc_te = [0] * NCH
            for ds in range(NSD):
                par = ds % 2
                if wd_cons[par] > 0:
                    self.wait("gpsimd", "te", wd_cons[par])
                smp = "dA" if par == 0 else "dB"
                ldd = self.dma("gpsimd",
                               wd_s[par].rearrange("p (k m) -> p k m", k=4),
                               dt["wd0T"][ds * 512:(ds + 1) * 512, :].rearrange(
                                   "(k p) m -> p k m", p=P), smp)
                self.wait("tensor", smp, ldd)
                for dc in range(NCH):
                    b = dc
                    if ds == 0:
                        self.use_bank(b)
                    for k in range(4):
                        hk = ds * 4 + k
                        last = (ds == NSD - 1 and k == 3)
                        strip_tail = (dc == NCH - 1 and k == 3)
                        self.emit("tensor", lambda e, k=k, dc=dc, b=b, par=par,
                                  hk=hk, ds=ds, last=last: e.matmul(
                                      self.bank(b, TT),
                                      wd_s[par][:, k * 1024 + dc * 128:
                                                k * 1024 + (dc + 1) * 128],
                                      h_t[hk],
                                      start=(ds == 0 and k == 0), stop=last),
                                  inc="te" if (last or strip_tail) else None)
                    if ds == NSD - 1:
                        dc_te[dc] = self.cnt["te"]
                wd_cons[par] = self.cnt["te"]
            last_down_te = self.cnt["te"]
            prev_v = getattr(self, "_p4_prev_v", 0)
            prev_st = getattr(self, "_p4_prev_st", 0)
            for dc in range(NCH):
                b = dc
                scr = self.SCR[:, 0:TT]
                if prev_v > 0:
                    self.wait("sync", "ve", prev_v)
                lx = self.dma("sync", xw,
                              dt["x"][dc * P:(dc + 1) * P, c0:c0 + TT], "sy")
                self.wait("scalar", "te", dc_te[dc])
                if prev_v > 0:
                    self.wait("scalar", "ve", prev_v)
                s_tt = self.emit("scalar", lambda e, b=b: e.activation(
                    scr, self.bank(b, TT), AF.Silu, scale=1.0 / 127.0),
                    inc="se")
                self.wait("vector", "se", s_tt)
                if prev_st > 0:
                    self.wait("vector", "so", prev_st)
                self.emit("vector", lambda e, dc=dc, c0=c0: e.tensor_tensor(
                    self.STAT[:, 0:TT], scr, s_t[dc][:, c0:c0 + TT], ALU.mult),
                    inc="ve")
                self.wait("vector", "sy", lx)
                self.drain("vector")
                v_t = self.emit("vector", lambda e: e.tensor_tensor(
                    self.STAT[:, 0:TT], self.STAT[:, 0:TT], xw, ALU.add),
                    inc="ve")
                self.drain("vector")
                prev_v = v_t
                self.release_bank(b, "ve", v_t)
                self.wait("sync", "ve", v_t)
                prev_st = self.dma("sync",
                                   self.x1_sp[dc * P:(dc + 1) * P, c0:c0 + TT],
                                   self.STAT[:, 0:TT], "so")
                self._p4_prev_v = prev_v
                self._p4_prev_st = prev_st
            prev_evac_v = prev_v
        self.drain("vector")

    def make_p5_tasks(self):
        dt = self.dt
        specs = [("wg1T", D, L1_HID), ("wu1T", D, L1_HID),
                 ("wd1T", L1_HID, D)]
        fw = [self.FW[:, 5632:6656], self.FW[:, 6656:7680]]
        self.p5_acc = [self.SMALL[:, 2 + j:3 + j] for j in range(3)]
        red = self.SMALL[:, 5:6]
        tl = []
        for j, (name, A, B) in enumerate(specs):
            srcd = dt[name]
            for a in range(A // P):
                for u in range(B // 1024):
                    tl.append((j, srcd[a * P:(a + 1) * P,
                                       u * 1024:(u + 1) * 1024]))
        for j in range(3):
            self.emit("vector", lambda e, j=j: e.memset(self.p5_acc[j], 0.0),
                      inc="ve")
        self.drain("vector")
        st = {}

        def issue(i):
            self.wait("sync", "ve", self.cnt["ve"])
            sm = "pA" if i % 2 == 0 else "pB"
            self.wait("sync", sm, self.cnt[sm])
            st[i] = (sm, self.dma("sync", fw[i % 2], tl[i][1], sm))
        issue(0)
        issue(1)

        def task(i):
            j = tl[i][0]
            self.wait("vector", *st.pop(i))
            self.emit("vector", lambda e, i=i: e.reduce_sum(
                red, fw[i % 2], axis=AX.X, apply_absolute_value=True),
                inc="ve")
            self.drain("vector")
            self.emit("vector", lambda e, j=j: e.tensor_tensor(
                self.p5_acc[j], self.p5_acc[j], red, ALU.add), inc="ve")
            self.drain("vector")
            if i + 2 < len(tl):
                issue(i + 2)
        return [lambda i=i: task(i) for i in range(len(tl))]

    def P5_wquant(self):
        for en in ("vector", "scalar"):
            self.wait(en, "so", self.cnt["so"])
        self._elu_prev_v = self.cnt["ve"]
        self.wait("tensor", "ve", self.cnt["ve"])
        self.wait("tensor", "se", self.cnt["se"])
        for q_ in ("gpsimd", "sync"):
            self.wait(q_, "te", self.cnt["te"])
            self.wait(q_, "ve", self.cnt["ve"])
            self.wait(q_, "se", self.cnt["se"])
        self.wait("sync", "xq", self.cnt["xq"])
        while self.p5_tasks:
            self.p5_tasks.pop(0)()
        dt = self.dt
        specs = [("wg1T", self.wgq_sc, D, L1_HID, 0),
                 ("wu1T", self.wuq_sc, D, L1_HID, 1),
                 ("wd1T", self.wdq_sc, L1_HID, D, 2)]
        fw = [self.FW[:, 0:2048], self.FW[:, 2048:4096]]
        self.mclip = self.SMALL[:, 8:11]
        self.thr = self.SMALL[:, 11:14]
        ub = [self.bf(self.QT_O + i * 2048, 2048) for i in range(2)]
        db = [self.bf(self.QT_O + 4096 + i * 2048, 2048) for i in range(2)]
        qb = [self.bf(self.QT_O + 8192 + i * 2048, 2048) for i in range(2)]
        self.wait_all("sync", "ve")
        self.wait_all("sync", "te")
        for name, dst, A, B, j in specs:
            srcd = dt[name]
            tiles = []
            if B >= 2048:
                for a in range(A // P):
                    for u in range(B // 2048):
                        tiles.append((src_ := srcd[a * P:(a + 1) * P,
                                                   u * 2048:(u + 1) * 2048],
                                      dst[a * P:(a + 1) * P,
                                          u * 2048:(u + 1) * 2048], 2048))
            else:
                for a in range(A // P):
                    tiles.append((srcd[a * P:(a + 1) * P, :],
                                  dst[a * P:(a + 1) * P, :], B))
            n = len(tiles)
            lds = {}
            cons = {}
            v_t = self.milestone("vector", "ve")
            self.wait("tensor", "ve", v_t)
            self.use_bank(6)
            te_t = self.emit("tensor", lambda e, j=j: e.transpose(
                self.psum[0:1, 3072:3072 + P], self.p5_acc[j],
                self.IDF[:]), inc="te")
            self.wait("vector", "te", te_t)
            self.emit("vector", lambda e: e.reduce_sum(
                self.STAT[0:1, 8:9], self.psum[0:1, 3072:3072 + P],
                axis=AX.X), inc="ve")
            self.drain("vector")
            self.emit("vector", lambda e, A=A, B=B: e.tensor_scalar(
                self.STAT[0:1, 8:9], self.STAT[0:1, 8:9], 1.0 / (A * B), 1e-5,
                ALU.mult, ALU.max), inc="ve")
            self.drain("vector")
            v_t = self.emit("vector", lambda e: e.tensor_scalar_mul(
                self.STAT[0:1, 9:10], self.STAT[0:1, 8:9], 0.5), inc="ve")
            self.drain("vector")
            self.release_bank(6, "ve", v_t)
            v_t = self.milestone("vector", "ve")
            self.wait("tensor", "ve", v_t)
            self.use_bank(7)
            te_t = self.emit("tensor", lambda e: e.matmul(
                self.psum[:, 3584:3586], self.ONEC[0:1, :],
                self.STAT[0:1, 8:10], start=True, stop=True), inc="te")
            mc = self.mclip[:, j:j + 1]
            th = self.thr[:, j:j + 1]
            self.wait("vector", "te", te_t)
            self.emit("vector", lambda e, mc=mc: e.tensor_copy(
                mc, self.psum[:, 3584:3585]), inc="ve")
            self.drain("vector")
            v_t = self.emit("vector", lambda e, th=th: e.tensor_copy(
                th, self.psum[:, 3585:3586]), inc="ve")
            self.drain("vector")
            self.release_bank(7, "ve", v_t)
            lds = {}
            sts = {}
            self.wait("sync", "ve", self.cnt["ve"])
            self.wait("sync", "se", self.cnt["se"])
            self.wait("vector", "oA", self.cnt["oA"])
            self.wait("vector", "oB", self.cnt["oB"])
            self.wait("scalar", "oA", self.cnt["oA"])
            self.wait("scalar", "oB", self.cnt["oB"])
            for i in range(min(2, n)):
                sm = "wA" if i % 2 == 0 else "wB"
                lds[i] = (sm, self.dma("sync", fw[i % 2][:, 0:tiles[i][2]],
                                       tiles[i][0], sm))
            for i in range(n):
                par = i % 2
                cc = tiles[i][2]
                self.wait("scalar", *lds[i])
                if i >= 2:
                    self.wait("scalar", "ve", cons[i - 2])
                s_tt = self.emit("scalar", lambda e, par=par, cc=cc:
                                 e.activation(ub[par][:, 0:cc],
                                              fw[par][:, 0:cc], AF.Sign),
                                 inc="se")
                self.wait("vector", *lds[i])
                self.emit("vector", lambda e, par=par, cc=cc:
                          e.scalar_tensor_tensor(db[par][:, 0:cc],
                                                 fw[par][:, 0:cc], -1.0,
                                                 fw[par][:, 0:cc],
                                                 ALU.mult, ALU.max), inc="ve")
                self.drain("vector")
                self.emit("vector", lambda e, par=par, cc=cc, th=th:
                          e.tensor_single_scalar(db[par][:, 0:cc],
                                                 db[par][:, 0:cc], th,
                                                 ALU.is_ge), inc="ve")
                self.wait("vector", "se", s_tt)
                if i >= 2:
                    self.wait("vector", sts[i - 2][0], sts[i - 2][1])
                self.drain("vector")
                v_t = self.emit("vector", lambda e, par=par, cc=cc:
                                e.tensor_tensor(qb[par][:, 0:cc],
                                                ub[par][:, 0:cc],
                                                db[par][:, 0:cc], ALU.mult),
                                inc="ve")
                cons[i] = v_t
                self.drain("vector")
                self.wait("gpsimd", "ve", v_t)
                smo = "oA" if par == 0 else "oB"
                sts[i] = (smo, self.dma("gpsimd", tiles[i][1],
                                        qb[par][:, 0:cc], smo))
                if i + 2 < n:
                    sm = "wA" if i % 2 == 0 else "wB"
                    self.wait("sync", "ve", v_t)
                    lds[i + 2] = (sm, self.dma(
                        "sync", fw[par][:, 0:tiles[i + 2][2]],
                        tiles[i + 2][0], sm))
        self.drain("vector")
        self.p5_store_done = (self.cnt["oA"], self.cnt["oB"])

    def P6_preln1(self):
        self.wait_all("sync", "so")
        self._elu_prev_v = self.cnt["ve"]
        self.wait("tensor", "ve", self.cnt["ve"])
        self.wait("tensor", "se", self.cnt["se"])
        for q_ in ("gpsimd", "sync"):
            self.wait(q_, "te", self.cnt["te"])
            self.wait(q_, "ve", self.cnt["ve"])
            self.wait(q_, "se", self.cnt["se"])
        self.win_te = {}
        self.win_ve = {}
        self.ln_stats_from_windows(self.x1_sp, write_xn=True)
        self.ln_apply_windows(self.x1_sp, self.xn, absmax=True)
        self.dbg_dump("xn1", list(self.xn), src_sems=("ve",))

    def P7_qkv(self):
        for en in ("vector", "scalar", "tensor"):
            self.wait(en, "oA", self.cnt["oA"])
            self.wait(en, "oB", self.cnt["oB"])
        self._elu_prev_v = self.cnt["ve"]
        self.wait("tensor", "ve", self.cnt["ve"])
        self.wait("tensor", "se", self.cnt["se"])
        for q_ in ("gpsimd", "sync"):
            self.wait(q_, "te", self.cnt["te"])
            self.wait(q_, "ve", self.cnt["ve"])
            self.wait(q_, "se", self.cnt["se"])
        T, TT, NTT = self.T, self.TT, self.NTT
        self.xq_tasks = (self.make_xq_tasks()
                         if self.last_stage >= 9 else [])
        dt = self.dt
        Q = [self.bf(self.QT_O + i * T, T) for i in range(NCH)]
        K = [self.bf(self.W1_O + i * T, T) for i in range(NCH)]
        Vp = self.bf(self.EX_O, 16640)
        ws = self.bf(self.RG_O, 2048)
        NSC = 256
        nst = 3 * D // NSC
        st_cons = 0
        for sidx in range(nst):
            cbase = sidx * NSC
            for _ in range(5):
                if self.p5_tasks:
                    self.p5_tasks.pop(0)()
            if st_cons > 0:
                self.wait("gpsimd", "te", st_cons)
            lds = self.dma("gpsimd", ws.rearrange("p (k m) -> p k m", k=NCH),
                           dt["wqkvT"][:, cbase:cbase + NSC].rearrange(
                               "(k p) m -> p k m", p=P), "gv")
            self.wait("tensor", "gv", lds)
            kind = cbase // D
            moff = (cbase % D)
            if kind < 2:
                dstt = Q if kind == 0 else K
                for mm in range(NSC // P):
                    m = moff // P + mm
                    b = [4, 5][mm % 2]
                    for tt in range(NTT):
                        c0 = tt * TT
                        self.use_bank(b)
                        for k in range(NCH):
                            last = (k == NCH - 1)
                            self.emit("tensor", lambda e, k=k, mm=mm, b=b,
                                      c0=c0, last=last: e.matmul(
                                          self.bank(b, TT),
                                          ws[:, k * NSC + mm * P:
                                             k * NSC + (mm + 1) * P],
                                          self.xn[k][:, c0:c0 + TT],
                                          start=(k == 0), stop=last),
                                      inc="te" if last else None)
                        te_t = self.cnt["te"]
                        if tt == NTT - 1:
                            st_cons = te_t
                        self.wait("vector", "te", te_t)
                        v_t = self.emit("vector", lambda e, m=m, c0=c0,
                                        dstt=dstt, b=b: e.tensor_copy(
                                            dstt[m][:, c0:c0 + TT],
                                            self.bank(b, TT)), inc="ve")
                        self.drain("vector")
                        self.release_bank(b, "ve", v_t)
            else:
                for tj in range(self.NTC):
                    b = [4, 5][tj % 2]
                    self.use_bank(b)
                    for k in range(NCH):
                        last = (k == NCH - 1)
                        self.emit("tensor", lambda e, k=k, tj=tj, b=b,
                                  last=last: e.matmul(
                                      self.bank(b, NSC),
                                      self.xn[k][:, tj * P:(tj + 1) * P],
                                      ws[:, k * NSC:(k + 1) * NSC],
                                      start=(k == 0), stop=last),
                                  inc="te" if last else None)
                    te_t = self.cnt["te"]
                    if tj == self.NTC - 1:
                        st_cons = te_t
                    self.wait("vector", "te", te_t)
                    h0 = moff // HD
                    v_t = self.emit("vector", lambda e, tj=tj, h0=h0, b=b:
                                    e.tensor_copy(
                                        Vp[:, tj * 1040 + h0 * 65:]
                                        [:, 0:4 * 65].rearrange(
                                            "p (h d) -> p h d", h=4)
                                        [:, :, 0:64],
                                        self.bank(b, NSC).rearrange(
                                            "p (h d) -> p h d", h=4)),
                                    inc="ve")
                    self.drain("vector")
                    self.release_bank(b, "ve", v_t)
        self.emit("vector", lambda e: e.memset(
            Vp.rearrange("p (t h d) -> p t h d", t=16, h=16)[:, :, :, 64:65],
            1.0), inc="ve")
        self.drain("vector")
        self.Q, self.K, self.Vp = Q, K, Vp
        self.dbg_dump("q1", list(Q), src_sems=("ve",))
        self.dbg_dump("k1", list(K), src_sems=("ve",))

    def make_xq_tasks(self):
        T, TT, NTT = self.T, self.TT, self.NTT
        win = [self.FW[:, i * TT:(i + 1) * TT] for i in range(NCH)]
        scr = self.FW[:, 4096:4096 + TT]
        tq_ = self.FW[:, 4608:4608 + TT]
        rec = self.FW[:, 5120:5120 + TT]
        xr = self.bf(self.WVS_O + 768, TT)
        xg = self.bf(self.WVS_O + 1280, TT)
        st = {"w": 0}

        def issue(tt):
            self.wait("sync", "ve", self.cnt["ve"])
            self.wait("sync", "xq", self.cnt["xq"])
            t = 0
            for i in range(NCH):
                t = self.dma("sync", win[i],
                             self.x1_sp[i * P:(i + 1) * P,
                                        tt * TT:(tt + 1) * TT], "wA")
            st["w"] = t

        issue(0)

        def task(tt, i):
            c0 = tt * TT
            w = win[i]
            if i == 0:
                self.wait("vector", "wA", st["w"])
                self.emit("vector", lambda e: e.reciprocal(
                    rec, self.SXR[:, c0:c0 + TT]), inc="ve")
                self.drain("vector")
            self.emit("vector", lambda e: e.tensor_tensor(
                scr, w, self.MU[:, c0:c0 + TT], ALU.subtract), inc="ve")
            self.drain("vector")
            self.emit("vector", lambda e: e.tensor_tensor(
                scr, scr, self.RS[:, c0:c0 + TT], ALU.mult), inc="ve")
            self.drain("vector")
            self.emit("vector", lambda e: e.scalar_tensor_tensor(
                tq_, scr, 127.0, rec, ALU.mult, ALU.mult), inc="ve")
            self.drain("vector")
            self.emit("vector", lambda e: e.tensor_scalar_add(
                tq_, tq_, 128.5), inc="ve")
            self.drain("vector")
            self.emit("vector", lambda e: e.tensor_scalar(
                xr, tq_, 128.0, 128.0, ALU.is_ge, ALU.mult), inc="ve")
            self.drain("vector")
            for sbit in (64.0, 32.0, 16.0, 8.0, 4.0, 2.0, 1.0):
                self.emit("vector", lambda e, sbit=sbit:
                          e.scalar_tensor_tensor(
                              xg, xr, sbit, tq_, ALU.add, ALU.is_le),
                          inc="ve")
                self.drain("vector")
                self.emit("vector", lambda e, sbit=sbit:
                          e.scalar_tensor_tensor(
                              xr, xg, sbit, xr, ALU.mult, ALU.add),
                          inc="ve")
                self.drain("vector")
            self.emit("vector", lambda e: e.tensor_scalar_add(
                w, xr, -128.0), inc="ve")
            v_t = self.milestone("vector", "ve")
            self.wait("gpsimd", "ve", v_t)
            self.dma("gpsimd", self.xq_sp[i * P:(i + 1) * P, c0:c0 + TT],
                     w, "xq")
            if i == NCH - 1 and tt + 1 < NTT:
                issue(tt + 1)

        return [lambda tt=tt, i=i: task(tt, i)
                for tt in range(NTT) for i in range(NCH)]

    def P8_l1_attn(self):
        self._elu_prev_v = self.cnt["ve"]
        self.wait("tensor", "ve", self.cnt["ve"])
        self.wait("tensor", "se", self.cnt["se"])
        for q_ in ("gpsimd", "sync"):
            self.wait(q_, "te", self.cnt["te"])
            self.wait(q_, "ve", self.cnt["ve"])
            self.wait(q_, "se", self.cnt["se"])
        T, TT, NTT = self.T, self.TT, self.NTT
        Q, K, Vp = self.Q, self.K, self.Vp
        o1 = [self.bf(self.XN_O + i * T, T) for i in range(NCH)]
        ring = [self.bf(self.RG_O + r * 512, 512)[:, 0:TT] for r in range(4)]
        o1tmp = self.bf(self.WVS_O + 256, 512)
        zrow = self.STAT[0:1, 0:TT]
        o1u = self.SCR[0:64, 0:TT]
        NJC = self.NTC
        for ic in range(NTT):
            i0 = ic * TT
            for p in range(NCH):
                avb = [4, 5]
                self.use_bank(4)
                self.use_bank(5)
                ring_v = [0, 0, 0, 0]
                av_info = {}

                def emit_av(j, p=p):
                    for lane in range(2):
                        rsl, s_tt = av_info.pop((j, lane))
                        self.wait("tensor", "se", s_tt)
                        self.emit("tensor", lambda e, j=j, rsl=rsl,
                                  lane=lane, p=p: e.matmul(
                                      self.bank(avb[lane], TT,
                                                p0=0, p1=65),
                                      Vp[:, j * 1040 + (2 * p + lane) * 65:
                                         j * 1040 + (2 * p + lane) * 65 + 65],
                                      ring[rsl],
                                      start=(j == 0), stop=(j == NJC - 1)),
                                  inc="te")
                        ring_v[rsl] = self.cnt["te"]
                for jc in range(NJC):
                    for lane in range(2):
                        bsl = [0, 1][jc % 2] if lane == 0 else [2, 3][jc % 2]
                        self.use_bank(bsl)
                        r0 = lane * 64
                        self.emit("tensor", lambda e, p=p, jc=jc, r0=r0,
                                  bsl=bsl, i0=i0: e.matmul(
                                      self.bank(bsl, TT),
                                      K[p][r0:r0 + 64, jc * P:(jc + 1) * P],
                                      Q[p][r0:r0 + 64, i0:i0 + TT],
                                      start=True, stop=True,
                                      tile_position=(r0, 0)), inc="te")
                        te_t = self.cnt["te"]
                        rsl = (jc % 2) * 2 + lane
                        if ring_v[rsl] > 0:
                            self.wait("scalar", "te", ring_v[rsl])
                        self.wait("scalar", "te", te_t)
                        s_tt = self.emit("scalar", lambda e, bsl=bsl, rsl=rsl:
                                         e.activation(ring[rsl],
                                                      self.bank(bsl, TT),
                                                      AF.Exp, scale=0.125),
                                         inc="se")
                        self.release_bank(bsl, "se", s_tt)
                        av_info[(jc, lane)] = (rsl, s_tt)
                    if jc > 0:
                        emit_av(jc - 1)
                emit_av(NJC - 1)
                te_t = self.cnt["te"]
                for lane in range(2):
                    b = avb[lane]
                    zbank = 6 + lane
                    self.wait("vector", "te", te_t)
                    self.emit("vector", lambda e, b=b: e.reciprocal(
                        zrow, self.bank(b, TT, p0=64, p1=65)), inc="ve")
                    v_t = self.milestone("vector", "ve")
                    self.wait("tensor", "ve", v_t)
                    self.use_bank(zbank)
                    tz = self.emit("tensor", lambda e, zbank=zbank: e.matmul(
                        self.bank(zbank, TT, p0=0, p1=64),
                        self.ONEC[0:1, 0:64], zrow,
                        start=True, stop=True), inc="te")
                    pv = getattr(self, "_p8_prev_o1u", 0)
                    self.wait("scalar", "te", te_t)
                    if pv > 0:
                        self.wait("scalar", "ve", pv)
                    s_t = self.emit("scalar", lambda e, b=b: e.copy(
                        o1u, self.bank(b, TT, p0=0, p1=64)), inc="se")
                    if lane == 0:
                        dst = o1[p][0:64, i0:i0 + TT]
                    else:
                        dst = o1tmp[0:64, 0:TT]
                        self.wait("vector", "sb2",
                                  getattr(self, "p8_o1dma", 0))
                    self.wait("vector", "te", tz)
                    self.wait("vector", "se", s_t)
                    v_t = self.emit("vector", lambda e, dst=dst, zbank=zbank:
                                    e.tensor_tensor(
                                        dst, o1u,
                                        self.bank(zbank, TT, p0=0, p1=64),
                                        ALU.mult), inc="ve")
                    self.drain("vector")
                    self._p8_prev_o1u = v_t
                    self.release_bank(b, "ve", v_t)
                    self.release_bank(zbank, "ve", v_t)
                    if lane == 1:
                        self.wait("sync", "ve", v_t)
                        self.p8_o1dma = self.dma(
                            "sync", o1[p][64:128, i0:i0 + TT],
                            o1tmp[0:64, 0:TT], "sb2")
                if self.xq_tasks:
                    self.xq_tasks.pop(0)()
                if self.p5_tasks:
                    self.p5_tasks.pop(0)()
        while self.xq_tasks:
            self.xq_tasks.pop(0)()
        self.o1 = o1
        self.dbg_dump("o1", list(o1), src_sems=("ve", "sy"))

    def P8b_wout(self):
        self._elu_prev_v = self.cnt["ve"]
        self.wait("tensor", "ve", self.cnt["ve"])
        self.wait("tensor", "se", self.cnt["se"])
        for q_ in ("gpsimd", "sync"):
            self.wait(q_, "te", self.cnt["te"])
            self.wait(q_, "ve", self.cnt["ve"])
            self.wait(q_, "se", self.cnt["se"])
        T, TT, NTT = self.T, self.TT, self.NTT
        dt = self.dt
        o1 = self.o1
        wo = self.bf(self.QT_O, 8192)
        ldw = self.dma("gpsimd", wo.rearrange("p (k m) -> p k m", k=NCH),
                       dt["woutT"].rearrange("(k p) m -> p k m", p=P), "gd")
        self.wait("tensor", "gd", ldw)
        self.wait_all("tensor", "sb2")
        s1 = [self.bf(self.W1_O + i * T, T) for i in range(NCH)]

        def w_writer(c0):
            def w(m, ps, te_t):
                self.wait("vector", "te", te_t)
                v_t = self.emit("vector", lambda e, m=m: e.tensor_copy(
                    s1[m][:, c0:c0 + TT], ps), inc="ve")
                self.drain("vector")
                self.wait("sync", "ve", v_t)
                self.dma("sync", self.s1_sp[m * P:(m + 1) * P, c0:c0 + TT],
                         s1[m][:, c0:c0 + TT], "wB")
                return ("ve", v_t)
            return w
        for tt in range(NTT):
            c0 = tt * TT
            self.proj_fm(lambda k, m: wo[:, k * 1024 + m * 128:
                                         k * 1024 + (m + 1) * 128],
                         o1, c0, TT, w_writer(c0), banks=[4, 5])
        self.s1 = s1
        self.dbg_dump("s1", list(s1), src_sems=("ve",))

    def P9_l1_ffn(self):
        self.wait_all("sync", "so")
        self._elu_prev_v = self.cnt["ve"]
        self.wait("tensor", "ve", self.cnt["ve"])
        self.wait("tensor", "se", self.cnt["se"])
        for q_ in ("gpsimd", "sync"):
            self.wait(q_, "te", self.cnt["te"])
            self.wait(q_, "ve", self.cnt["ve"])
            self.wait(q_, "se", self.cnt["se"])
        T, TT, NTT = self.T, self.TT, self.NTT
        dt = self.dt
        xq = [self.bf(self.XN_O + i * T, T) for i in range(NCH)]
        self.wait("sync", "xq", self.cnt["xq"])
        self.wait("sync", "wB", self.cnt["wB"])
        t = 0
        for i in range(NCH):
            t = self.dma("sync", xq[i], self.xq_sp[i * P:(i + 1) * P, :], "wA")
        self.wait("tensor", "wA", t)
        self.wait("gpsimd", "wA", t)
        self.dbg_dump("xq", list(xq), src_sems=("wA",))
        TQ = 256
        NTQ = T // TQ
        NH1 = L1_HID // P
        HQ = NH1 * TQ
        hpt = [self.FW[:, i * TQ:(i + 1) * TQ] for i in range(NH1)]
        hqb = [[self.bf(self.EX_O + h * HQ + i * TQ, TQ) for i in range(NH1)]
               for h in range(2)]
        g_s = [self.bf(self.QT_O + i * 4096, 4096) for i in range(2)]
        u_s = [self.bf(self.QT_O + 8192 + i * 4096, 4096) for i in range(2)]
        d_s = [self.bf(self.W1_O + i * 8192, 8192) for i in range(2)]
        QW = min(2048, NH1 * TQ)
        rbufs = [self.bf(self.RG_O, QW), self.bf(self.WVS_O + 2048, QW)]
        gbuf = self.bf(self.WVS_O, QW)
        s1w = self.bf(self.RG_O + 2048, TQ)
        cg = self.STAT[:, 0:TQ]
        cu = self.STAT[:, 256:512]
        cdb = [self.STAT[:, 512:768],
               self.MU[:, 256:512] if T >= 512 else self.STAT[:, 512:768]]
        rec_h = self.STAT[:, 768:768 + TQ]
        mxh = self.RS[:, 0:TQ]
        mrow = self.MU[0:1, 0:TQ]
        mrow2 = self.MU[0:1, 1024:1024 + TQ]
        rfrow = self.MU[0:1, 1280:1280 + TQ]
        scrq = self.SCR[:, 0:TQ]
        scr2 = self.SCR[:, 256:512]
        sl_g = [self.SCR[:, 0:TQ], self.MU[:, 512:768]]
        sl_s = [self.SCR[:, 256:512], self.MU[:, 768:1024]]
        vab = self.MU[:, 1536:1536 + TQ]
        ug2 = [self.MU[:, 1792:1792 + TQ], self.RS[:, TQ:2 * TQ]]
        cu0 = self.STAT[0:1, 256:256 + TQ]
        NS = L1_HID // 512
        g_cons = [0, 0]
        d_cons = [0, 0]
        hq_done = [0, 0]
        dmm_done = [0, 0]
        rb_free = [0, 0]
        hq_part = {}
        pa, pb = getattr(self, "p5_store_done", (0, 0))
        self.wait("sync", "oA", pa)
        self.wait("sync", "oB", pb)

        def emit_down_strip(tqp, ds, hq_wait=None):
            par = ds % 2
            smp = "eA" if par == 0 else "eB"
            hqp = hqb[tqp % 2]
            if d_cons[par] > 0:
                self.wait("sync", "te", d_cons[par])
            ldd = self.dma("sync",
                           d_s[par].rearrange("p (k m) -> p k m", k=NCH),
                           self.wdq_sc[ds * 1024:(ds + 1) * 1024, :].rearrange(
                               "(k p) m -> p k m", p=P), smp)
            if hq_wait is not None:
                self.wait("tensor", "se", hq_wait)
            elif ds == 0:
                self.wait("tensor", "se", hq_done[tqp % 2])
            self.wait("tensor", smp, ldd)
            for dc in range(NCH):
                b = 4 + dc // 2
                off = (dc % 2) * TQ
                if ds == 0 and dc % 2 == 0:
                    self.use_bank(b)
                for k in range(NCH):
                    hk = ds * NCH + k
                    first = (ds == 0 and k == 0 and dc % 2 == 0)
                    last = (ds == 3 and k == NCH - 1 and dc % 2 == 1)
                    self.emit("tensor", lambda e, k=k, dc=dc, b=b, off=off,
                              par=par, hk=hk, first=first, last=last,
                              hqp=hqp: e.matmul(
                                  self.bank(b, TQ, off=off),
                                  d_s[par].rearrange(
                                      "p (k m) -> p k m", k=NCH)[:, k, :]
                                  [:, dc * 128:(dc + 1) * 128],
                                  hqp[hk],
                                  start=first, stop=last),
                              inc="te" if k == NCH - 1 else None)
            d_cons[par] = self.cnt["te"]
            if ds == 3:
                dmm_done[tqp % 2] = self.cnt["te"]

        def emit_down_evac(tqp):
            c0 = tqp * TQ
            cdq = cdb[tqp % 2]
            te_t = self.cnt["te"]
            prev_fin = self.cnt["ve"]
            for dc in range(NCH):
                b = 4 + dc // 2
                off = (dc % 2) * TQ
                self.wait("sync", "ve", prev_fin)
                lx = self.dma("sync", scr2,
                              self.x1_sp[dc * P:(dc + 1) * P, c0:c0 + TQ],
                              "sy")
                ls = self.dma("sync", s1w,
                              self.s1_sp[dc * P:(dc + 1) * P, c0:c0 + TQ],
                              "sy")
                self.wait("vector", "te", te_t)
                self.emit("vector", lambda e, b=b, off=off, cdq=cdq:
                          e.tensor_tensor(
                              scrq, self.bank(b, TQ, off=off), cdq, ALU.mult),
                          inc="ve")
                v_t = self.milestone("vector", "ve")
                if dc % 2 == 1:
                    self.release_bank(b, "ve", v_t)
                self.wait("scalar", "ve", v_t)
                self.wait("scalar", "so", self.cnt["so"])
                s_tt = self.emit("scalar", lambda e: e.activation(
                    scrq, scrq, AF.Silu), inc="se")
                self.wait("vector", "se", s_tt)
                self.wait("vector", "sy", ls)
                self.emit("vector", lambda e: e.tensor_tensor(
                    scrq, scrq, s1w, ALU.mult), inc="ve")
                self.drain("vector")
                if "upd" in self.dbg_names:
                    v_t = self.milestone("vector", "ve")
                    self.wait("sync", "ve", v_t)
                    dt_ = self.dma("sync",
                                   self.dbg_upd[dc * P:(dc + 1) * P,
                                                c0:c0 + TQ], scrq, "dbg2")
                    self.wait("vector", "dbg2", dt_)
                v_t = self.emit("vector", lambda e: e.tensor_tensor(
                    scrq, scrq, scr2, ALU.add), inc="ve")
                self.drain("vector")
                prev_fin = v_t
                self.wait("sync", "ve", v_t)
                self.dma("sync", self.out[dc * P:(dc + 1) * P, c0:c0 + TQ],
                         scrq, "so")
                self.wait("vector", "so", self.cnt["so"])

        self._p9_vp = [0, 0]
        for tq in range(NTQ):
            c0 = tq * TQ
            self.wait("sync", "te", self.cnt["te"])
            self.emit("vector", lambda e, c0=c0: e.tensor_scalar(
                cg, self.SXR[:, c0:c0 + TQ], self.mclip[:, 0:1], 1.0 / 127.0,
                ALU.mult, ALU.mult), inc="ve")
            self.emit("vector", lambda e, c0=c0: e.tensor_scalar(
                cu0, self.SXR[0:1, c0:c0 + TQ], self.mclip[0:1, 1:2],
                1.0 / 127.0, ALU.mult, ALU.mult), inc="ve")
            self.drain("vector")
            self.emit("vector", lambda e: e.memset(mxh, 0.0), inc="ve")
            self.drain("vector")
            pend = {}

            def flush_h(hh):
                s_tt = pend.pop(hh)
                self.drain("vector")
                self.wait("vector", "se", s_tt)
                vp = self.emit("vector", lambda e, hh=hh:
                               e.tensor_tensor(
                                   hpt[hh], ug2[hh % 2],
                                   sl_s[hh % 2], ALU.mult), inc="ve")
                self._p9_vp[hh % 2] = vp
                self.drain("vector")
                self.emit("vector", lambda e, hh=hh: e.scalar_tensor_tensor(
                    vab, hpt[hh], -1.0, hpt[hh], ALU.mult, ALU.max),
                    inc="ve")
                self.drain("vector")
                self.emit("vector", lambda e: e.tensor_tensor(
                    mxh, mxh, vab, ALU.max), inc="ve")
                self.drain("vector")
            for hs in range(NS):
                par = hs % 2
                smp = "sA" if par == 0 else "sB"
                if g_cons[par] > 0:
                    self.wait("sync", "te", g_cons[par])
                ldg = self.dma("sync",
                               g_s[par].rearrange("p (k m) -> p k m", k=NCH),
                               self.wgq_sc[:, hs * 512:(hs + 1) * 512].rearrange(
                                   "(k p) m -> p k m", p=P), smp)
                ldu = self.dma("sync",
                               u_s[par].rearrange("p (k m) -> p k m", k=NCH),
                               self.wuq_sc[:, hs * 512:(hs + 1) * 512].rearrange(
                                   "(k p) m -> p k m", p=P), smp)
                self.wait("tensor", smp, ldu)
                for mh in range(4):
                    hh = hs * 4 + mh
                    bg, bu = [0, 1][mh % 2], [2, 3][mh % 2]
                    self.use_bank(bg)
                    self.use_bank(bu)
                    for k in range(NCH):
                        last = (k == NCH - 1)
                        self.emit("tensor", lambda e, k=k, mh=mh, bg=bg,
                                  par=par, c0=c0, last=last: e.matmul(
                                      self.bank(bg, TQ),
                                      g_s[par][:, k * 512 + mh * 128:
                                               k * 512 + (mh + 1) * 128],
                                      xq[k][:, c0:c0 + TQ],
                                      start=(k == 0), stop=last),
                                  inc="te" if last else None)
                        self.emit("tensor", lambda e, k=k, mh=mh, bu=bu,
                                  par=par, c0=c0, last=last: e.matmul(
                                      self.bank(bu, TQ),
                                      u_s[par][:, k * 512 + mh * 128:
                                               k * 512 + (mh + 1) * 128],
                                      xq[k][:, c0:c0 + TQ],
                                      start=(k == 0), stop=last),
                                  inc="te" if last else None)
                    te_t = self.cnt["te"]
                    self.wait("scalar", "te", te_t)
                    if self._p9_vp[hh % 2] > 0:
                        self.wait("scalar", "ve", self._p9_vp[hh % 2])
                    cp = self.emit("scalar", lambda e, hh=hh, bu=bu: e.copy(
                        ug2[hh % 2], self.bank(bu, TQ)), inc="se")
                    self.release_bank(bu, "se", cp)
                    self.wait("vector", "te", te_t)
                    vg = self.emit("vector", lambda e, bg=bg, hh=hh:
                                   e.tensor_tensor(
                                       sl_g[hh % 2], self.bank(bg, TQ),
                                       cg, ALU.mult), inc="ve")
                    self.release_bank(bg, "ve", vg)
                    self.wait("scalar", "ve", vg)
                    s_tt = self.emit("scalar", lambda e, hh=hh: e.activation(
                        sl_s[hh % 2], sl_g[hh % 2], AF.Silu), inc="se")
                    pend[hh] = s_tt
                    if (hh - 1) in pend:
                        flush_h(hh - 1)
                g_cons[par] = self.cnt["te"]
                if tq > 0 and hs % 2 == 1:
                    emit_down_strip(tq - 1, hs // 2)
            for hh in sorted(pend):
                flush_h(hh)
            v_t = self.milestone("vector", "ve")
            self.wait("tensor", "ve", v_t)
            for g in range(2):
                self.use_bank(0)
                te_t = self.emit("tensor", lambda e, g=g: e.transpose(
                    self.bank(0, 128), mxh[:, g * P:(g + 1) * P], self.IDF[:]),
                    inc="te")
                self.wait("vector", "te", te_t)
                v_t = self.emit("vector", lambda e, g=g: e.reduce_max(
                    self.SMALL[:, g:g + 1], self.bank(0, 128), axis=AX.X),
                    inc="ve")
                self.drain("vector")
                self.release_bank(0, "ve", v_t)
            v_t = self.milestone("vector", "ve")
            self.wait("tensor", "ve", v_t)
            self.use_bank(1)
            te_t = self.emit("tensor", lambda e: e.transpose(
                self.psum[0:2, 512:512 + P], self.SMALL[:, 0:2],
                self.IDF[:]), inc="te")
            self.wait("vector", "te", te_t)
            v_t = self.emit("vector", lambda e: e.tensor_copy(
                self.SCR[0:2, 0:P], self.psum[0:2, 512:512 + P]),
                inc="ve")
            self.drain("vector")
            self.release_bank(1, "ve", v_t)
            self.wait("sync", "ve", v_t)
            d_t = self.dma("sync",
                           mrow.rearrange("o (g f) -> o g f", g=2),
                           self.SCR[0:2, 0:P], "sb2")
            self.wait("vector", "sb2", d_t)
            self.emit("vector", lambda e: e.tensor_tensor(
                mrow2, mrow, cu0, ALU.mult), inc="ve")
            self.drain("vector")
            self.emit("vector", lambda e: e.tensor_single_scalar(
                mrow2, mrow2, 1e-5, ALU.max), inc="ve")
            self.drain("vector")
            self.emit("vector", lambda e: e.reciprocal(rfrow, mrow2),
                      inc="ve")
            self.drain("vector")
            self.emit("vector", lambda e: e.tensor_tensor(
                rfrow, rfrow, cu0, ALU.mult), inc="ve")
            v_t = self.milestone("vector", "ve")
            self.wait("tensor", "ve", v_t)
            self.use_bank(0)
            te_t = self.emit("tensor", lambda e: e.matmul(
                self.bank(0, 2 * TQ), self.ONEC[0:1, :],
                self.MU[0:1, 1024:1024 + 2 * TQ],
                start=True, stop=True), inc="te")
            self.wait("vector", "te", te_t)
            cdq = cdb[tq % 2]
            self.emit("vector", lambda e, cdq=cdq: e.tensor_scalar(
                cdq, self.bank(0, TQ), self.mclip[:, 2:3], 1.0 / 127.0,
                ALU.mult, ALU.mult), inc="ve")
            self.drain("vector")
            v_t = self.emit("vector", lambda e: e.tensor_copy(
                rec_h, self.bank(0, TQ, off=TQ)), inc="ve")
            self.drain("vector")
            self.release_bank(0, "ve", v_t)
            if tq > 0:
                emit_down_evac(tq - 1)
            for i in range(NH1):
                self.emit("vector", lambda e, i=i: e.scalar_tensor_tensor(
                    hpt[i], hpt[i], 127.0, rec_h, ALU.mult, ALU.mult),
                    inc="ve")
            self.drain("vector")
            NQC = NH1 * TQ // QW
            self.wait("scalar", "ve", self.cnt["ve"])
            s_add = {}
            for q in range(NQC):
                s_add[q] = self.emit("scalar", lambda e, q=q: e.activation(
                    self.FW[:, q * QW:(q + 1) * QW],
                    self.FW[:, q * QW:(q + 1) * QW],
                    AF.Identity, bias=128.5), inc="se")
            if dmm_done[tq % 2] > 0:
                self.wait("scalar", "te", dmm_done[tq % 2])
            for q in range(NQC):
                u = self.FW[:, q * QW:(q + 1) * QW]
                hqq = self.bf(self.EX_O + (tq % 2) * HQ + q * QW, QW)
                rb = rbufs[q % 2]
                if rb_free[q % 2] > 0:
                    self.wait("vector", "se", rb_free[q % 2])
                self.wait("vector", "se", s_add[q])
                self.emit("vector", lambda e, u=u, rb=rb: e.tensor_scalar(
                    rb, u, 128.0, 128.0, ALU.is_ge, ALU.mult), inc="ve")
                self.drain("vector")
                for sbit in (64.0, 32.0, 16.0, 8.0, 4.0, 2.0, 1.0):
                    self.emit("vector", lambda e, sbit=sbit, u=u, rb=rb:
                              e.scalar_tensor_tensor(
                                  gbuf, rb, sbit, u, ALU.add, ALU.is_le),
                              inc="ve")
                    self.drain("vector")
                    self.emit("vector", lambda e, sbit=sbit, rb=rb:
                              e.scalar_tensor_tensor(
                                  rb, gbuf, sbit, rb, ALU.mult, ALU.add),
                              inc="ve")
                    self.drain("vector")
                self.wait("scalar", "ve", self.cnt["ve"])
                s_f = self.emit("scalar", lambda e, hqq=hqq, rb=rb:
                                e.activation(hqq, rb, AF.Identity,
                                             bias=-128.0), inc="se")
                rb_free[q % 2] = s_f
                hq_part[q] = s_f
            hq_done[tq % 2] = self.cnt["se"]
            if tq == 2:
                self.dbg_dump("hq0", [h[:, 0:TQ] for h in hqb[0]],
                              src_sems=("se",))
        for ds in range(4):
            emit_down_strip(NTQ - 1, ds,
                            hq_wait=hq_part[min(ds, len(hq_part) - 1)])
        emit_down_evac(NTQ - 1)
        self.drain("vector")



def _prep_inputs(inputs):
    x = np.asarray(inputs["L0_attn_query"])
    n_cores = x.shape[0]
    bias6 = np.stack([
        np.asarray(inputs["L0_attn_bq"]),
        np.asarray(inputs["L0_attn_bk"]),
        np.asarray(inputs["L0_attn_bv"]),
        np.asarray(inputs["L0_attn_bo"]),
        np.asarray(inputs["L0_attn_ln_g"]),
        np.asarray(inputs["L0_attn_ln_b"]),
    ], axis=1).astype(np.float32)
    tr = lambda a: np.ascontiguousarray(np.asarray(a).T)
    shared = {
        "bias6": np.ascontiguousarray(bias6),
        "wqT": tr(inputs["L0_attn_Wq"]),
        "wkT": tr(inputs["L0_attn_Wk"]),
        "wvT": tr(inputs["L0_attn_Wv"]),
        "woT": tr(inputs["L0_attn_Wo"]),
        "wg0T": tr(inputs["L0_ffn_w_gate"]),
        "wu0T": tr(inputs["L0_ffn_w_up"]),
        "wd0T": tr(inputs["L0_ffn_w_down"]),
        "wqkvT": tr(inputs["L1_attn_w_qkv"]),
        "woutT": tr(inputs["L1_attn_w_out"]),
        "wg1T": tr(inputs["L1_ffn_w_gate"]),
        "wu1T": tr(inputs["L1_ffn_w_up"]),
        "wd1T": tr(inputs["L1_ffn_w_down"]),
    }
    in_maps = []
    for c in range(n_cores):
        m = dict(shared)
        m["x"] = np.ascontiguousarray(x[c].T)
        in_maps.append(m)
    return in_maps


def kernel(**inputs):
    in_maps = _prep_inputs(inputs)
    b = Builder(T=2048, last_stage=99)
    nc = b.build()
    res = run_bass_kernel_spmd(nc, in_maps, core_ids=list(range(8)))
    outs = [np.ascontiguousarray(r["out"].T) for r in res.results]
    return np.stack(outs, axis=0).astype(np.float32)


# version 20
# speedup vs baseline: 1.0174x; 1.0056x over previous
import numpy as np
from contextlib import ExitStack

import concourse.bass as bass
import concourse.mybir as mybir
from concourse.bass_utils import run_bass_kernel_spmd


F32 = mybir.dt.float32
BF16 = mybir.dt.bfloat16
I32 = mybir.dt.int32
AF = mybir.ActivationFunctionType
ALU = mybir.AluOpType
AX = mybir.AxisListType

P = 128
D = 1024
NCH = D // P
H = 16
HD = 64
L0_HID = 3072
L1_HID = 4096
LN_EPS = 1e-5

ENGS = ("tensor", "vector", "scalar", "gpsimd", "sync")


class Builder:
    def __init__(self, T=2048, last_stage=99, dbg=(), p2_sub=9):
        self.T = T
        self.last_stage = last_stage
        self.p2_sub = p2_sub
        self.dbg_names = set(dbg)
        self.nc = bass.Bass("TRN2", target_bir_lowering=False, debug=False)
        self.streams = {e: [] for e in ENGS}
        self.cnt = {}
        self.sem = {}
        self.stack = ExitStack()

    def mksem(self, name):
        self.sem[name] = self.stack.enter_context(self.nc.semaphore(name))
        self.cnt[name] = 0

    def sbuf(self, name, shape, dtype):
        return self.nc.alloc_sbuf_tensor("sb_" + name, shape, dtype)

    def const(self, val):
        if val in self.consts:
            return
        i = len(self.consts)
        assert i < 16
        ap = self.cpool[:, i:i + 1]
        self.consts[val] = ap
        self.emit("vector", lambda e, ap=ap, val=val: e.memset(ap, val),
                  inc="ve")
        self.nc.const_aps.register(F32, val, ap)

    OWN_SEM = {"vector": "ve", "scalar": "se", "gpsimd": "ge"}

    def emit(self, eng, fn, inc=None, amt=1):
        if inc is None:
            inc = self.OWN_SEM.get(eng)
        if inc is not None:
            self.cnt[inc] += amt
            sem = self.sem[inc]

            def wrapped(e, fn=fn, sem=sem, amt=amt):
                fn(e).then_inc(sem, amt)
            self.streams[eng].append(wrapped)
            return self.cnt[inc]
        self.streams[eng].append(fn)
        return None

    def wait(self, eng, sname, val):
        sem = self.sem[sname]
        self.streams[eng].append(lambda e, sem=sem, val=val: e.wait_ge(sem, val))

    def wait_all(self, eng, sname):
        self.wait(eng, sname, self.cnt[sname])

    def drain(self, eng):
        own = self.OWN_SEM.get(eng)
        if own is not None:
            self.wait(eng, own, self.cnt[own])
        else:
            self.streams[eng].append(lambda e: e.drain())

    def milestone(self, eng, sname):
        if self.OWN_SEM.get(eng) == sname and self.cnt[sname] > 0:
            self.wait(eng, sname, self.cnt[sname])
            return self.cnt[sname]
        return self.emit(eng, lambda e: e.drain(), inc=sname)

    def dma(self, queue, out_ap, in_ap, sname, slow_ok=False):
        self.cnt[sname] += 16
        sem = self.sem[sname]

        def fn(e, out_ap=out_ap, in_ap=in_ap, sem=sem):
            e.dma_start(out_ap, in_ap,
                        allow_slow_non_contiguous=slow_ok).then_inc(sem, 16)
        self.streams[queue].append(fn)
        return self.cnt[sname]

    def finalize(self):
        nc = self.nc
        with nc.Block() as block:
            for eng in ENGS:
                dec = getattr(block, eng)
                stream = self.streams[eng]

                def body(e, stream=stream):
                    for fn in stream:
                        fn(e)
                dec(body)
        self.stack.close()
        return nc

    def dbg_dump(self, name, tiles, src_sems=()):
        if name not in self.dbg_names:
            return
        aps = [(t[:] if not isinstance(t, bass.AP) else t) for t in tiles]
        n = len(aps)
        cols = aps[0].shape[-1]
        d = self.nc.dram_tensor("dbg_" + name, [n * P, cols], F32,
                                kind="ExternalOutput").ap()
        for s in src_sems:
            self.wait_all("sync", s)
            self.wait_all("gpsimd", s)
        for i, ap in enumerate(aps):
            if ap.dtype == F32:
                self.dma("sync", d[i * P:(i + 1) * P, :], ap, "dbg2")
            else:
                self.dma("gpsimd", d[i * P:(i + 1) * P, :], ap, "dbg")
        for en in ("vector", "scalar", "tensor", "sync", "gpsimd"):
            self.wait(en, "dbg", self.cnt["dbg"])
            self.wait(en, "dbg2", self.cnt["dbg2"])

    def bank(self, i, cols=512, off=0, p0=0, p1=P):
        return self.psum[p0:p1, 512 * i + off: 512 * i + off + cols]

    def use_bank(self, i):
        sname, cntv = self.bank_free[i]
        if cntv > 0:
            self.wait("tensor", sname, cntv)

    def release_bank(self, i, sname, cntv):
        self.bank_free[i] = (sname, cntv)

    def build(self):
        nc, T = self.nc, self.T
        TT = min(512, T)
        NTT = T // TT
        self.NTT, self.TT = NTT, TT
        self.NTC = T // P

        for s in ("sy", "gd", "gv", "bro", "oA", "oB", "wA", "wB",
                  "sA", "sB", "dA", "dB", "eA", "eB", "gA", "gB", "sb2",
                  "dbg", "dbg2", "te", "ve", "se", "ge", "so", "xq",
                  "pA", "pB"):
            self.mksem(s)

        dt = {}
        dt["x"] = nc.dram_tensor("x", [D, T], F32, kind="ExternalInput").ap()
        dt["bias6"] = nc.dram_tensor("bias6", [D, 6], F32, kind="ExternalInput").ap()
        for w in ("wqT", "wkT", "wvT", "woT"):
            dt[w] = nc.dram_tensor(w, [D, D], F32, kind="ExternalInput").ap()
        dt["wg0T"] = nc.dram_tensor("wg0T", [D, L0_HID], I32, kind="ExternalInput").ap()
        dt["wu0T"] = nc.dram_tensor("wu0T", [D, L0_HID], I32, kind="ExternalInput").ap()
        dt["wd0T"] = nc.dram_tensor("wd0T", [L0_HID, D], I32, kind="ExternalInput").ap()
        dt["wqkvT"] = nc.dram_tensor("wqkvT", [D, 3 * D], F32, kind="ExternalInput").ap()
        dt["woutT"] = nc.dram_tensor("woutT", [D, D], F32, kind="ExternalInput").ap()
        dt["wg1T"] = nc.dram_tensor("wg1T", [D, L1_HID], F32, kind="ExternalInput").ap()
        dt["wu1T"] = nc.dram_tensor("wu1T", [D, L1_HID], F32, kind="ExternalInput").ap()
        dt["wd1T"] = nc.dram_tensor("wd1T", [L1_HID, D], F32, kind="ExternalInput").ap()
        self.dt = dt
        self.out = nc.dram_tensor("out", [D, T], F32, kind="ExternalOutput").ap()
        self.x1_sp = nc.dram_tensor("x1_sp", [D, T], F32, kind="Internal").ap()
        self.xq_sp = nc.dram_tensor("xq_sp", [D, T], BF16, kind="Internal").ap()
        self.s1_sp = nc.dram_tensor("s1_sp", [D, T], BF16, kind="Internal").ap()
        self.wgq_sc = nc.dram_tensor("wgq_sc", [D, L1_HID], BF16, kind="Internal").ap()
        self.wuq_sc = nc.dram_tensor("wuq_sc", [D, L1_HID], BF16, kind="Internal").ap()
        self.wdq_sc = nc.dram_tensor("wdq_sc", [L1_HID, D], BF16, kind="Internal").ap()

        sb = self.sbuf
        self.XN_O = 0
        self.W1_O = 16384
        self.QT_O = 32768
        self.EX_O = 49152
        self.WVS_O = 65536
        self.RG_O = 69632
        self.BF = sb("BF", [P, 71936], BF16)
        self.FW = sb("FW", [P, 8192], F32)
        self.MU = sb("MU", [P, max(T, 2048)], F32)
        self.RS = sb("RS", [P, max(T, 512)], F32)
        self.SXR = sb("SXR", [P, T], F32)
        self.SCR = sb("SCR", [P, 512], F32)
        self.STAT = sb("STAT", [P, 1024], F32)
        self.SMALL = sb("SMALL", [P, 16], F32)
        self.cpool = sb("cpool", [P, 16], F32)
        self.consts = {}
        self.ones = sb("ones", [P, 128], BF16)
        self.maskT2 = sb("maskT2", [P, 2 * P], BF16)
        self.bias6 = sb("bias6c", [P, 6 * NCH], F32)
        self.Sst = [sb(f"Sst{i}", [P, 8 * HD], BF16) for i in range(2)]
        self.IDF = sb("IDF", [P, P], F32)
        self.ONEC = sb("ONEC", [1, P], F32)
        self.psum = self.stack.enter_context(nc.psum_tensor("ps", [P, 4096], F32))
        self.bank_free = [("te", 0)] * 8

        self.xn = [self.bf(self.XN_O + i * T, T) for i in range(NCH)]
        if "upd" in self.dbg_names:
            self.dbg_upd = nc.dram_tensor("dbg_upd", [D, T], F32,
                                          kind="ExternalOutput").ap()
        self.win_te = {}
        self.win_ve = {}

        self.marks = {}

        def mark(nm):
            self.marks[nm] = {e: len(self.streams[e]) for e in ENGS}
        self.P0_setup()
        mark("P1")
        self.P1_preln0()
        if self.last_stage >= 2:
            mark("P2")
            self.P2_l0_attn()
        if self.last_stage >= 3:
            mark("P3")
            self.P3_innerln_wo()
        if self.last_stage >= 4:
            mark("P4")
            self.P4_l0_ffn()
        self.p5_tasks = []
        if self.last_stage >= 6:
            mark("P6")
            self.P6_preln1()
        if self.last_stage >= 7:
            if self.last_stage >= 5:
                self.p5_tasks = self.make_p5_tasks()
            mark("P7")
            self.P7_qkv()
        if self.last_stage >= 8:
            mark("P8")
            self.P8_l1_attn()
            mark("P8b")
            self.P8b_wout()
        if self.last_stage >= 5:
            if self.last_stage < 7:
                self.p5_tasks = self.make_p5_tasks()
            mark("P5")
            self.P5_wquant()
        if self.last_stage >= 9:
            mark("P9")
            self.P9_l1_ffn()
        mark("END")
        if self.last_stage < 9 and self.last_stage >= 4:
            for s in ("sy", "ve", "se", "so"):
                self.wait_all("sync", s)
            for i in range(NCH):
                self.dma("sync", self.out[i * P:(i + 1) * P, :],
                         self.x1_sp[i * P:(i + 1) * P, :], "so")
        elif self.last_stage < 4:
            self.wait_all("sync", "ve")
            for i in range(NCH):
                self.dma("sync", self.out[i * P:(i + 1) * P, :],
                         self.MU[:, 0:T], "so")
        self.wait_all("sync", "so")
        self.wait_all("sync", "dbg")
        self.wait_all("sync", "dbg2")
        self.wait_all("sync", "sy")
        self.wait_all("sync", "oA")
        self.wait_all("sync", "oB")
        return self.finalize()

    def bf(self, off, n):
        return self.BF[:, off:off + n]

    def bias_ap(self, j, m):
        return self.bias6[:, j * NCH + m: j * NCH + m + 1]

    def P0_setup(self):
        dt = self.dt
        self.const(LN_EPS)
        self.const(0.125)
        self.const(1.0 / 127.0)
        self.const(128.5)
        self.const(-128.0)
        self.emit("vector", lambda e: e.memset(self.ones[:], 1.0))
        self.emit("gpsimd", lambda e: e.memset(self.maskT2[:], 1.0))
        self.emit("gpsimd", lambda e: e.memset(self.Sst[0][:], 0.0))
        self.emit("gpsimd", lambda e: e.memset(self.Sst[1][:], 0.0))
        self.drain("gpsimd")
        self.emit("gpsimd", lambda e: e.affine_select(
            self.maskT2[:], self.maskT2[:], pattern=[[0, 2], [1, 128]],
            compare_op=ALU.is_ge, fill=0.0, base=0, channel_multiplier=-1),
            inc="ge")
        ld = self.dma("sync", self.bias6[:].rearrange("p (j m) -> p j m", m=NCH),
                      dt["bias6"].rearrange("(m p) j -> p j m", p=P), "sy",
                      slow_ok=True)
        self.wait("vector", "sy", ld)
        idf = self.IDF[:]
        self.emit("gpsimd", lambda e: e.memset(idf, 1.0))
        self.drain("gpsimd")
        self.emit("gpsimd", lambda e: e.affine_select(
            idf, idf, pattern=[[-1, 128]], compare_op=ALU.is_equal,
            fill=0.0, base=0, channel_multiplier=1), inc="ge")
        self.emit("vector", lambda e: e.memset(self.ONEC[:], 1.0), inc="ve")
        v_t = self.milestone("vector", "ve")
        self.wait("tensor", "ve", v_t)
        self.wait("tensor", "ge", self.cnt["ge"])
        te_t = self.emit("tensor", lambda e: e.transpose(
            self.psum[0:16, 0:128], self.bias6[:, NCH:3 * NCH], idf),
            inc="te")
        self.wait("vector", "te", te_t)
        bT = self.STAT[0:16, 0:128]
        v_t = self.emit("vector", lambda e, bT=bT: e.tensor_copy(
            bT, self.psum[0:16, 0:128]), inc="ve")
        self.wait("gpsimd", "ve", v_t)
        self.brow = self.bf(self.RG_O, 2048)
        for j in range(2):
            self.dma("gpsimd",
                     self.brow[0:1, j * D:(j + 1) * D].rearrange(
                         "o (m f) -> o m f", m=NCH),
                     bT[j * NCH:(j + 1) * NCH, :], "bro")
        self.wait_all("vector", "bro")
        self.wait_all("scalar", "bro")

    def ln_stats_from_windows(self, src_dram, write_xn):
        T, TT, NTT = self.T, self.TT, self.NTT
        sq = self.bf(self.QT_O, 512)[:, 0:self.TT]
        wins = [[self.FW[:, i * TT:(i + 1) * TT] for i in range(NCH)],
                [self.FW[:, 4096 + i * TT:4096 + (i + 1) * TT] for i in range(NCH)]]
        ld_t = {}
        self.wait_all("sync", "ve")
        self.wait_all("sync", "te")

        def issue(tt):
            w = wins[tt % 2]
            sm = "wA" if tt % 2 == 0 else "wB"
            for i in range(NCH):
                t = self.dma("sync", w[i],
                             src_dram[i * P:(i + 1) * P, tt * TT:(tt + 1) * TT],
                             sm)
            ld_t[tt] = (sm, t)
        for tt in range(min(2, NTT)):
            issue(tt)
        sq8 = [self.bf(self.QT_O + k * 512, 512)[:, 0:TT] for k in range(NCH)]
        for tt in range(NTT):
            w = wins[tt % 2]
            c0 = tt * TT
            self.wait("vector", ld_t[tt][0], ld_t[tt][1])
            if tt > 0:
                self.wait("vector", "te", self.win_te[tt - 1])
            self.use_bank(6)
            self.use_bank(7)
            for k in range(NCH):
                if write_xn:
                    self.emit("vector", lambda e, k=k, c0=c0, w=w: e.tensor_copy(
                        self.xn[k][:, c0:c0 + TT], w[k]), inc="ve")
                self.emit("vector", lambda e, k=k, w=w: e.tensor_tensor(
                    sq8[k], w[k], w[k], ALU.mult), inc="ve")
            v_t = self.milestone("vector", "ve")
            self.wait("tensor", "ve", v_t)
            for k in range(NCH):
                last = (k == NCH - 1)
                self.emit("tensor", lambda e, k=k, c0=c0, last=last: e.matmul(
                    self.bank(6, TT), self.ones[:, 0:128],
                    self.xn[k][:, c0:c0 + TT],
                    start=(k == 0), stop=last), inc="te" if last else None)
                self.emit("tensor", lambda e, k=k, last=last: e.matmul(
                    self.bank(7, TT), self.ones[:, 0:128], sq8[k],
                    start=(k == 0), stop=last), inc="te" if last else None)
            te_t = self.cnt["te"]
            self.win_te[tt] = te_t
            if tt + 2 < NTT:
                self.wait("sync", "te", te_t)
                issue(tt + 2)
            self._ln_tail(te_t, c0, TT)
        self.drain("vector")

    def ln_stats_from_tiles(self, tiles):
        T, TT, NTT = self.T, self.TT, self.NTT
        sq8 = [self.bf(self.QT_O + k * 512, 512)[:, 0:self.TT]
               for k in range(NCH)]
        prev_te = 0
        for tt in range(NTT):
            c0 = tt * TT
            self.use_bank(6)
            self.use_bank(7)
            if prev_te > 0:
                self.wait("vector", "te", prev_te)
            for k in range(NCH):
                self.emit("vector", lambda e, k=k, c0=c0: e.tensor_tensor(
                    sq8[k], tiles[k][:, c0:c0 + TT], tiles[k][:, c0:c0 + TT],
                    ALU.mult), inc="ve")
            v_t = self.milestone("vector", "ve")
            self.wait("tensor", "ve", v_t)
            for k in range(NCH):
                last = (k == NCH - 1)
                self.emit("tensor", lambda e, k=k, c0=c0, last=last: e.matmul(
                    self.bank(6, TT), self.ones[:, 0:128],
                    tiles[k][:, c0:c0 + TT],
                    start=(k == 0), stop=last), inc="te" if last else None)
                self.emit("tensor", lambda e, k=k, last=last: e.matmul(
                    self.bank(7, TT), self.ones[:, 0:128], sq8[k],
                    start=(k == 0), stop=last), inc="te" if last else None)
            prev_te = self.cnt["te"]
            self._ln_tail(prev_te, c0, TT)
        self.drain("vector")

    def _ln_tail(self, te_t, c0, TTc):
        st = self.STAT
        mu = self.MU[:, c0:c0 + TTc]
        rstd = self.RS[:, c0:c0 + TTc]
        self.wait("vector", "te", te_t)
        self.emit("vector", lambda e, mu=mu: e.tensor_scalar_mul(
            mu, self.bank(6, TTc), 1.0 / D), inc="ve")
        self.emit("vector", lambda e: e.tensor_scalar_mul(
            st[:, 0:TTc], self.bank(7, TTc), 1.0 / D), inc="ve")
        self.drain("vector")
        self.emit("vector", lambda e, mu=mu: e.tensor_tensor(
            st[:, TTc:2 * TTc], mu, mu, ALU.mult), inc="ve")
        self.drain("vector")
        self.emit("vector", lambda e: e.tensor_tensor(
            st[:, 0:TTc], st[:, 0:TTc], st[:, TTc:2 * TTc], ALU.subtract),
            inc="ve")
        v_t = self.milestone("vector", "ve")
        self.release_bank(6, "ve", v_t)
        self.release_bank(7, "ve", v_t)
        self.wait("scalar", "ve", v_t)
        s_t = self.emit("scalar", lambda e: e.activation(
            st[:, TTc:2 * TTc], st[:, 0:TTc], AF.Sqrt, bias=LN_EPS), inc="se")
        self.wait("vector", "se", s_t)
        self.emit("vector", lambda e, rstd=rstd: e.reciprocal(
            rstd, st[:, TTc:2 * TTc]), inc="ve")
        self.drain("vector")

    def ln_apply_windows(self, src_dram, dst_tiles, absmax=False):
        T, TT, NTT = self.T, self.TT, self.NTT
        wins = [[self.FW[:, i * TT:(i + 1) * TT] for i in range(NCH)],
                [self.FW[:, 4096 + i * TT:4096 + (i + 1) * TT] for i in range(NCH)]]
        scr = self.SCR[:, 0:TT]
        if absmax:
            self.emit("vector", lambda e: e.memset(self.SXR[:], 0.0), inc="ve")
            self.drain("vector")
        ld_t = {}
        self.wait_all("sync", "ve")
        self.wait_all("sync", "te")

        def issue(tt):
            w = wins[tt % 2]
            sm = "wA" if tt % 2 == 0 else "wB"
            for i in range(NCH):
                t = self.dma("sync", w[i],
                             src_dram[i * P:(i + 1) * P, tt * TT:(tt + 1) * TT],
                             sm)
            ld_t[tt] = (sm, t)
        for tt in range(min(2, NTT)):
            issue(tt)
        for tt in range(NTT):
            w = wins[tt % 2]
            c0 = tt * TT
            self.wait("vector", ld_t[tt][0], ld_t[tt][1])
            for i in range(NCH):
                self.emit("vector", lambda e, i=i, w=w, c0=c0: e.tensor_tensor(
                    scr, w[i], self.MU[:, c0:c0 + TT], ALU.subtract), inc="ve")
                self.drain("vector")
                self.emit("vector", lambda e, c0=c0: e.tensor_tensor(
                    scr, scr, self.RS[:, c0:c0 + TT], ALU.mult), inc="ve")
                self.drain("vector")
                self.emit("vector", lambda e, i=i, c0=c0: e.tensor_copy(
                    dst_tiles[i][:, c0:c0 + TT], scr), inc="ve")
                if absmax:
                    self.drain("vector")
                    self.emit("vector", lambda e: e.scalar_tensor_tensor(
                        scr, scr, -1.0, scr, ALU.mult, ALU.max), inc="ve")
                    self.drain("vector")
                    self.emit("vector", lambda e, c0=c0: e.tensor_tensor(
                        self.SXR[:, c0:c0 + TT], self.SXR[:, c0:c0 + TT],
                        scr, ALU.max), inc="ve")
                self.drain("vector")
            self.win_ve[tt] = self.cnt["ve"]
            if tt + 2 < NTT:
                self.wait("sync", "ve", self.cnt["ve"])
                issue(tt + 2)
        if absmax:
            self.bcast_ptok_max()
        self.drain("vector")

    def bcast_ptok_max(self):
        T, TT, NTT = self.T, self.TT, self.NTT
        NG = T // P
        v_t = self.milestone("vector", "ve")
        self.wait("tensor", "ve", v_t)
        for g in range(NG):
            self.use_bank(6)
            te_t = self.emit("tensor", lambda e, g=g: e.transpose(
                self.bank(6, 128), self.SXR[:, g * P:(g + 1) * P], self.IDF[:]),
                inc="te")
            self.wait("vector", "te", te_t)
            v_t = self.emit("vector", lambda e, g=g: e.reduce_max(
                self.STAT[:, g:g + 1], self.bank(6, 128), axis=AX.X), inc="ve")
            self.drain("vector")
            self.release_bank(6, "ve", v_t)
        v_t = self.milestone("vector", "ve")
        self.wait("tensor", "ve", v_t)
        self.use_bank(6)
        te_t = self.emit("tensor", lambda e: e.transpose(
            self.psum[0:NG, 512 * 6:512 * 6 + P],
            self.STAT[:, 0:NG], self.IDF[:]), inc="te")
        self.wait("vector", "te", te_t)
        v_t = self.emit("vector", lambda e: e.tensor_copy(
            self.STAT[0:NG, 256:256 + P],
            self.psum[0:NG, 512 * 6:512 * 6 + P]), inc="ve")
        self.drain("vector")
        self.release_bank(6, "ve", v_t)
        self.wait("sync", "ve", v_t)
        d_t = self.dma("sync",
                       self.SXR[0:1, 0:T].rearrange("o (g f) -> o g f", g=NG),
                       self.STAT[0:NG, 256:256 + P], "sb2")
        self.wait("vector", "sb2", d_t)
        v_t = self.emit("vector", lambda e: e.tensor_single_scalar(
            self.SXR[0:1, 0:T], self.SXR[0:1, 0:T], 1e-5, ALU.max), inc="ve")
        self.drain("vector")
        v_t = self.milestone("vector", "ve")
        self.wait("tensor", "ve", v_t)
        for tt in range(NTT):
            c0 = tt * TT
            self.use_bank(7)
            te_t = self.emit("tensor", lambda e, c0=c0: e.matmul(
                self.bank(7, TT), self.ONEC[0:1, :],
                self.SXR[0:1, c0:c0 + TT], start=True, stop=True), inc="te")
            self.wait("vector", "te", te_t)
            v_t = self.emit("vector", lambda e, c0=c0: e.tensor_copy(
                self.SXR[:, c0:c0 + TT], self.bank(7, TT)), inc="ve")
            self.drain("vector")
            self.release_bank(7, "ve", v_t)

    def P1_preln0(self):
        self.ln_stats_from_windows(self.dt["x"], write_xn=True)
        self.ln_apply_windows(self.dt["x"], self.xn)
        self.dbg_dump("xn", list(self.xn), src_sems=("ve",))

    def proj_fm(self, lhsT_fn, rhs_tiles, rhs_c0, N, out_writer, banks,
                nk=NCH, m_range=None, bias_row=None):
        if m_range is None:
            m_range = range(NCH)
        bi = 0
        for m in m_range:
            b = banks[bi % len(banks)]
            bi += 1
            self.use_bank(b)
            for k in range(nk):
                last = (k == nk - 1) and bias_row is None
                self.emit("tensor",
                          lambda e, k=k, m=m, b=b, last=last: e.matmul(
                              self.bank(b, N), lhsT_fn(k, m),
                              rhs_tiles[k][:, rhs_c0:rhs_c0 + N],
                              start=(k == 0), stop=last),
                          inc="te" if last else None)
            if bias_row is not None:
                self.emit("tensor", lambda e, m=m, b=b: e.matmul(
                    self.bank(b, N), bias_row(m), self.ones[0:1, 0:N],
                    start=False, stop=True), inc="te")
            te_t = self.cnt["te"]
            sem, cnt = out_writer(m, self.bank(b, N), te_t)
            self.release_bank(b, sem, cnt)

    def elu1_writer(self, dst_tile_fn, bias_j):
        scr = self.SCR[:, 0:512]

        def w(m, ps, te_t):
            TTc = ps.shape[-1]
            self.wait("vector", "te", te_t)
            pv = getattr(self, "_elu_prev_v", 0)
            if pv > 0:
                self.wait("vector", "se", self.cnt["se"])
            bap = self.bias_ap(bias_j, m)
            self.emit("vector", lambda e, bap=bap: e.tensor_scalar(
                scr[:, 0:TTc], ps, bap, 0.0, ALU.add, ALU.min), inc="ve")
            self.emit("vector", lambda e, bap=bap: e.tensor_scalar(
                self.STAT[:, 0:TTc], ps, bap, 0.0, ALU.add, ALU.max),
                inc="ve")
            v_m = self.milestone("vector", "ve")
            self.wait("scalar", "ve", v_m)
            s_t = self.emit("scalar", lambda e: e.activation(
                scr[:, 0:TTc], scr[:, 0:TTc], AF.Exp), inc="se")
            self.wait("vector", "se", s_t)
            v_t = self.emit("vector", lambda e: e.tensor_tensor(
                dst_tile_fn(m)[:, 0:TTc], scr[:, 0:TTc], self.STAT[:, 0:TTc],
                ALU.add), inc="ve")
            self.drain("vector")
            self._elu_prev_v = v_t
            return ("ve", v_t)
        return w

    def P2_l0_attn(self):
        self._elu_prev_v = self.cnt["ve"]
        self.wait("tensor", "ve", self.cnt["ve"])
        self.wait("tensor", "se", self.cnt["se"])
        for q_ in ("gpsimd", "sync"):
            self.wait(q_, "te", self.cnt["te"])
            self.wait(q_, "ve", self.cnt["ve"])
            self.wait(q_, "se", self.cnt["se"])
        T, TT, NTT = self.T, self.TT, self.NTT
        dt = self.dt
        wq = self.bf(self.W1_O, 8192)
        wk = self.bf(self.W1_O + 8192, 8192)
        for name, dst in (("wqT", wq), ("wkT", wk)):
            ldw = self.dma("gpsimd", dst.rearrange("p (k m) -> p k m", k=NCH),
                           dt[name].rearrange("(k p) m -> p k m", p=P), "gd")
        wvs = self.bf(self.WVS_O, 4096)
        q2 = self.bf(self.QT_O, 4096)
        k2f = self.bf(self.QT_O + 4096, 4096)
        k2t = self.bf(self.QT_O + 8192, 4096)
        v2 = self.bf(self.QT_O + 12288, 4096)
        ab = [self.bf(self.RG_O + 2048, 256), self.bf(self.RG_O + 2048, 256)]
        o_t = [self.bf(self.EX_O + p * T, T) for p in range(NCH)]
        wt = lambda w, k, m: w[:, k * 1024 + m * 128: k * 1024 + (m + 1) * 128]

        self.wait("tensor", "gd", self.cnt["gd"])
        self.wait_all("tensor", "bro")
        nchunks = TT // P
        wvs_cons = 0
        for tt in range(NTT):
            c0 = tt * TT
            self.proj_fm(lambda k, m: wt(wq, k, m), self.xn, c0, TT,
                         self.elu1_writer(
                             lambda m: q2[:, m * TT:(m + 1) * TT], 0),
                         banks=[4, 5])
            self.proj_fm(lambda k, m: wt(wk, k, m), self.xn, c0, TT,
                         self.elu1_writer(
                             lambda m: k2f[:, m * TT:(m + 1) * TT], 1),
                         banks=[4, 5])
            for tj in range(nchunks):
                for n in range(2):
                    b = [4, 5][(tj * 2 + n) % 2]
                    self.use_bank(b)
                    for k in range(NCH):
                        self.emit("tensor", lambda e, k=k, tj=tj, n=n, b=b,
                                  c0=c0:
                                  e.matmul(self.bank(b, 512),
                                           self.xn[k][:, c0 + tj * P:c0 + (tj + 1) * P],
                                           wk[:, k * 1024 + n * 512:
                                              k * 1024 + (n + 1) * 512],
                                           start=(k == 0), stop=False))
                    self.emit("tensor", lambda e, tj=tj, n=n, b=b: e.matmul(
                        self.bank(b, 512), self.ones[0:1, 0:128],
                        self.brow[0:1, n * 512:(n + 1) * 512],
                        start=False, stop=True), inc="te")
                    te_t = self.cnt["te"]
                    scr = self.SCR[:, 0:512]
                    self.wait("vector", "te", te_t)
                    self.wait("vector", "se", self.cnt["se"])
                    self.emit("vector", lambda e, b=b: e.tensor_single_scalar(
                        scr, self.bank(b, 512), 0.0, ALU.min), inc="ve")
                    self.emit("vector", lambda e, b=b: e.tensor_single_scalar(
                        self.STAT[:, 0:512], self.bank(b, 512), 0.0, ALU.max),
                        inc="ve")
                    v_m = self.milestone("vector", "ve")
                    self.wait("scalar", "ve", v_m)
                    s_t = self.emit("scalar", lambda e: e.activation(
                        scr, scr, AF.Exp), inc="se")
                    self.wait("vector", "se", s_t)
                    v_t = self.emit("vector", lambda e, tj=tj, n=n: e.tensor_tensor(
                        k2t[:, tj * 1024 + n * 512: tj * 1024 + (n + 1) * 512],
                        scr, self.STAT[:, 0:512], ALU.add), inc="ve")
                    self.drain("vector")
                    self._elu_prev_v = v_t
                    self.release_bank(b, "ve", v_t)
            for n in range(2):
                if wvs_cons > 0:
                    self.wait("gpsimd", "te", wvs_cons)
                ldv = self.dma("gpsimd",
                               wvs.rearrange("p (k m) -> p k m", k=NCH),
                               dt["wvT"][:, n * 512:(n + 1) * 512].rearrange(
                                   "(k p) m -> p k m", p=P), "gv")
                self.wait("tensor", "gv", ldv)
                for tj in range(nchunks):
                    b = [4, 5][tj % 2]
                    self.use_bank(b)
                    for k in range(NCH):
                        self.emit("tensor", lambda e, k=k, tj=tj, b=b, c0=c0:
                                  e.matmul(self.bank(b, 512),
                                           self.xn[k][:, c0 + tj * P:c0 + (tj + 1) * P],
                                           wvs[:, k * 512:(k + 1) * 512],
                                           start=(k == 0), stop=False))
                    self.emit("tensor", lambda e, n=n, b=b: e.matmul(
                        self.bank(b, 512), self.ones[0:1, 0:128],
                        self.brow[0:1, 1024 + n * 512:1024 + (n + 1) * 512],
                        start=False, stop=True), inc="te")
                    te_t = self.cnt["te"]
                    if tj == nchunks - 1:
                        wvs_cons = te_t
                    self.wait("vector", "te", te_t)
                    v_t = self.emit("vector", lambda e, tj=tj, n=n, b=b:
                                    e.tensor_copy(
                                        v2[:, tj * 1024 + n * 512:
                                           tj * 1024 + (n + 1) * 512],
                                        self.bank(b, 512)), inc="ve")
                    self.drain("vector")
                    self.release_bank(b, "ve", v_t)
            if tt == 0:
                self.dbg_dump("q0f", [q2[:, m * TT:(m + 1) * TT]
                                      for m in range(NCH)], src_sems=("ve",))
                self.dbg_dump("k0f", [k2f[:, m * TT:(m + 1) * TT]
                                      for m in range(NCH)], src_sems=("ve",))
                self.dbg_dump("k0t", [k2t[:, j * 1024:(j + 1) * 1024]
                                      for j in range(nchunks)],
                              src_sems=("ve",))
                self.dbg_dump("v0t", [v2[:, j * 1024:(j + 1) * 1024]
                                      for j in range(nchunks)],
                              src_sems=("ve",))
            if self.p2_sub < 2:
                continue
            for cl in range(nchunks):
                c = tt * nchunks + cl
                tl = cl * P
                Sold = self.Sst[c % 2]
                Snew = self.Sst[(c + 1) % 2]
                self.use_bank(6)
                for p in range(NCH):
                    sb0 = [0, 1][p % 2]
                    sb1 = [2, 3][p % 2]
                    self.use_bank(sb0)
                    self.use_bank(sb1)
                    for lane in range(2):
                        r0, r1 = lane * 64, lane * 64 + 64
                        bsl = sb0 if lane == 0 else sb1
                        self.emit("tensor", lambda e, p=p, tl=tl, bsl=bsl,
                                  r0=r0, r1=r1: e.matmul(
                                      self.bank(bsl, 128),
                                      k2f[r0:r1, p * TT + tl:p * TT + tl + P],
                                      q2[r0:r1, p * TT + tl:p * TT + tl + P],
                                      start=True, stop=True,
                                      tile_position=(r0, 0)), inc="te")
                    te_t = self.cnt["te"]
                    abt = ab[p % 2]
                    self.wait("vector", "te", te_t)
                    self.emit("vector", lambda e, sb0=sb0, abt=abt:
                              e.tensor_tensor(abt[:, 0:128],
                                              self.bank(sb0, 128),
                                              self.maskT2[:, 0:128],
                                              ALU.mult), inc="ve")
                    self.drain("vector")
                    v_t = self.emit("vector", lambda e, sb1=sb1, abt=abt:
                                    e.tensor_tensor(abt[:, 128:256],
                                                    self.bank(sb1, 128),
                                                    self.maskT2[:, 128:256],
                                                    ALU.mult), inc="ve")
                    self.drain("vector")
                    self.release_bank(sb0, "ve", v_t)
                    self.release_bank(sb1, "ve", v_t)
                    if self.p2_sub < 3:
                        continue
                    ob = 4 + p // 4
                    oc = (p % 4) * 128
                    if oc == 0:
                        self.use_bank(ob)
                    self.wait("tensor", "ve", v_t)
                    for lane in range(2):
                        r0 = lane * 64
                        self.emit("tensor", lambda e, p=p, tl=tl, ob=ob, oc=oc,
                                  abt=abt, r0=r0, lane=lane: e.matmul(
                                      self.bank(ob, 128, off=oc,
                                                p0=r0, p1=r0 + 64),
                                      v2[:, (tl // P) * 1024 + p * 128 + r0:
                                         (tl // P) * 1024 + p * 128 + r0 + 64],
                                      abt[:, lane * 128:lane * 128 + 128],
                                      start=True, stop=False,
                                      tile_position=(0, r0)))
                        self.emit("tensor", lambda e, p=p, tl=tl, ob=ob, oc=oc,
                                  r0=r0, Sold=Sold: e.matmul(
                                      self.bank(ob, 128, off=oc,
                                                p0=r0, p1=r0 + 64),
                                      Sold[r0:r0 + 64, p * 64:p * 64 + 64],
                                      q2[r0:r0 + 64,
                                         p * TT + tl:p * TT + tl + P],
                                      start=False, stop=True,
                                      tile_position=(r0, r0)), inc="te")
                    if self.p2_sub < 4:
                        continue
                    for lane in range(2):
                        r0 = lane * 64
                        self.emit("tensor", lambda e, p=p, tl=tl, r0=r0, c=c:
                                  e.matmul(
                                      self.bank(6, 64, off=p * 64,
                                                p0=r0, p1=r0 + 64),
                                      k2t[:, (tl // P) * 1024 + p * 128 + r0:
                                          (tl // P) * 1024 + p * 128 + r0 + 64],
                                      v2[:, (tl // P) * 1024 + p * 128 + r0:
                                         (tl // P) * 1024 + p * 128 + r0 + 64],
                                      start=(p == 0),
                                      stop=(p == NCH - 1),
                                      tile_position=(0, r0)), inc="te")
                te_t = self.cnt["te"]
                if self.p2_sub < 3:
                    continue
                self.wait("vector", "te", te_t)
                for p in range(NCH):
                    ob = 4 + p // 4
                    oc = (p % 4) * 128
                    v_t = self.emit("vector", lambda e, p=p, ob=ob, oc=oc, c=c:
                                    e.tensor_copy(
                                        o_t[p][:, c * P:(c + 1) * P],
                                        self.bank(ob, 128, off=oc)), inc="ve")
                v_t = self.milestone("vector", "ve")
                self.release_bank(4, "ve", v_t)
                self.release_bank(5, "ve", v_t)
                if self.p2_sub < 4:
                    continue
                self.emit("vector", lambda e, Snew=Snew, Sold=Sold:
                          e.tensor_tensor(Snew[:], Sold[:],
                                          self.bank(6, 512), ALU.add),
                          inc="ve")
                v_t = self.milestone("vector", "ve")
                self.release_bank(6, "ve", v_t)
                self.wait("tensor", "ve", v_t)
        self.o_t = o_t
        self.dbg_dump("o0", list(o_t), src_sems=("ve",))

    def P3_innerln_wo(self):
        self._elu_prev_v = self.cnt["ve"]
        self.wait("tensor", "ve", self.cnt["ve"])
        self.wait("tensor", "se", self.cnt["se"])
        for q_ in ("gpsimd", "sync"):
            self.wait(q_, "te", self.cnt["te"])
            self.wait(q_, "ve", self.cnt["ve"])
            self.wait(q_, "se", self.cnt["se"])
        T, TT, NTT = self.T, self.TT, self.NTT
        dt = self.dt
        o_t = self.o_t
        self.ln_stats_from_tiles(o_t)
        oln = [self.bf(self.W1_O + i * T, T) for i in range(NCH)]
        scr = self.SCR[:, 0:TT]
        for tt in range(NTT):
            c0 = tt * TT
            for i in range(NCH):
                self.emit("vector", lambda e, i=i, c0=c0: e.tensor_tensor(
                    scr, o_t[i][:, c0:c0 + TT], self.MU[:, c0:c0 + TT],
                    ALU.subtract), inc="ve")
                self.drain("vector")
                self.emit("vector", lambda e, c0=c0: e.tensor_tensor(
                    scr, scr, self.RS[:, c0:c0 + TT], ALU.mult), inc="ve")
                self.drain("vector")
                self.emit("vector", lambda e, i=i, c0=c0: e.tensor_scalar(
                    oln[i][:, c0:c0 + TT], scr, self.bias_ap(4, i),
                    self.bias_ap(5, i), ALU.mult, ALU.add), inc="ve")
                self.drain("vector")
        v_t = self.milestone("vector", "ve")
        self.wait("gpsimd", "te", self.cnt["te"])
        wo = self.bf(self.QT_O, 8192)
        ldw = self.dma("gpsimd", wo.rearrange("p (k m) -> p k m", k=NCH),
                       dt["woT"].rearrange("(k p) m -> p k m", p=P), "gd")
        self.wait("tensor", "gd", ldw)
        self.wait("tensor", "ve", v_t)
        s_t = [self.bf(self.EX_O + i * T, T) for i in range(NCH)]

        def s_writer_tt(c0):
            def w(m, ps, te_t):
                self.wait("vector", "te", te_t)
                v_t = self.emit("vector", lambda e, m=m: e.tensor_scalar_add(
                    s_t[m][:, c0:c0 + TT], ps, self.bias_ap(3, m)), inc="ve")
                self.drain("vector")
                return ("ve", v_t)
            return w
        for tt in range(NTT):
            c0 = tt * TT
            self.proj_fm(lambda k, m: wo[:, k * 1024 + m * 128:
                                         k * 1024 + (m + 1) * 128],
                         oln, c0, TT, s_writer_tt(c0), banks=[4, 5])
        self.s_t = s_t
        self.dbg_dump("s0", list(s_t), src_sems=("ve",))

    def P4_l0_ffn(self):
        self._elu_prev_v = self.cnt["ve"]
        self.wait("tensor", "ve", self.cnt["ve"])
        self.wait("tensor", "se", self.cnt["se"])
        for q_ in ("gpsimd", "sync"):
            self.wait(q_, "te", self.cnt["te"])
            self.wait(q_, "ve", self.cnt["ve"])
            self.wait(q_, "se", self.cnt["se"])
        T, TT, NTT = self.T, self.TT, self.NTT
        dt = self.dt
        s_t = self.s_t
        h_t = [self.bf(self.W1_O + i * TT, TT) for i in range(L0_HID // P)]
        g_s = [self.bf(self.QT_O + i * 4096, 4096) for i in range(2)]
        u_s = [self.bf(self.QT_O + 8192 + i * 4096, 4096) for i in range(2)]
        wd_s = [self.bf(self.W1_O + 12288, 4096), self.bf(self.WVS_O, 4096)]
        xw = self.FW[:, 0:TT]
        NS = L0_HID // 512
        g_cons = [0, 0]
        u_cons = [0, 0]
        wd_cons = [0, 0]
        last_down_te = 0
        prev_evac_v = 0
        for tt in range(NTT):
            c0 = tt * TT
            if last_down_te > 0:
                self.wait("vector", "te", last_down_te)
            for hs in range(NS):
                par = hs % 2
                if g_cons[par] > 0:
                    self.wait("gpsimd", "se", g_cons[par])
                    self.wait("gpsimd", "ve", u_cons[par])
                smp = "gA" if par == 0 else "gB"
                ldg = self.dma("gpsimd",
                               g_s[par].rearrange("p (k m) -> p k m", k=NCH),
                               dt["wg0T"][:, hs * 512:(hs + 1) * 512].rearrange(
                                   "(k p) m -> p k m", p=P), smp)
                ldu = self.dma("gpsimd",
                               u_s[par].rearrange("p (k m) -> p k m", k=NCH),
                               dt["wu0T"][:, hs * 512:(hs + 1) * 512].rearrange(
                                   "(k p) m -> p k m", p=P), smp)
                self.wait("tensor", smp, ldu)
                for mh in range(4):
                    hh = hs * 4 + mh
                    bg, bu = [0, 1][mh % 2], [2, 3][mh % 2]
                    self.use_bank(bg)
                    self.use_bank(bu)
                    for k in range(NCH):
                        last = (k == NCH - 1)
                        self.emit("tensor", lambda e, k=k, mh=mh, bg=bg,
                                  par=par, c0=c0, last=last: e.matmul(
                                      self.bank(bg, TT),
                                      g_s[par][:, k * 512 + mh * 128:
                                               k * 512 + (mh + 1) * 128],
                                      self.xn[k][:, c0:c0 + TT],
                                      start=(k == 0), stop=last),
                                  inc="te" if last else None)
                        self.emit("tensor", lambda e, k=k, mh=mh, bu=bu,
                                  par=par, c0=c0, last=last: e.matmul(
                                      self.bank(bu, TT),
                                      u_s[par][:, k * 512 + mh * 128:
                                               k * 512 + (mh + 1) * 128],
                                      self.xn[k][:, c0:c0 + TT],
                                      start=(k == 0), stop=last),
                                  inc="te" if last else None)
                    te_t = self.cnt["te"]
                    scr = self.SCR[:, 0:TT]
                    self.wait("scalar", "te", te_t)
                    if prev_evac_v > 0:
                        self.wait("scalar", "ve", prev_evac_v)
                    s_tt = self.emit("scalar", lambda e, bg=bg: e.activation(
                        scr, self.bank(bg, TT), AF.Silu, scale=1.0 / 127.0),
                        inc="se")
                    g_cons[par] = s_tt
                    self.wait("vector", "se", s_tt)
                    v_t = self.emit("vector", lambda e, hh=hh, bu=bu:
                                    e.scalar_tensor_tensor(
                                        h_t[hh], self.bank(bu, TT),
                                        1.0 / 127.0, scr,
                                        ALU.mult, ALU.mult), inc="ve")
                    u_cons[par] = v_t
                    prev_evac_v = v_t
                    self.drain("vector")
                    self.release_bank(bg, "se", s_tt)
                    self.release_bank(bu, "ve", v_t)
            h_done = self.cnt["ve"]
            self.wait("tensor", "ve", h_done)
            NSD = 6
            dc_te = [0] * NCH
            for ds in range(NSD):
                par = ds % 2
                if wd_cons[par] > 0:
                    self.wait("gpsimd", "te", wd_cons[par])
                smp = "dA" if par == 0 else "dB"
                ldd = self.dma("gpsimd",
                               wd_s[par].rearrange("p (k m) -> p k m", k=4),
                               dt["wd0T"][ds * 512:(ds + 1) * 512, :].rearrange(
                                   "(k p) m -> p k m", p=P), smp)
                self.wait("tensor", smp, ldd)
                for dc in range(NCH):
                    b = dc
                    if ds == 0:
                        self.use_bank(b)
                    for k in range(4):
                        hk = ds * 4 + k
                        last = (ds == NSD - 1 and k == 3)
                        strip_tail = (dc == NCH - 1 and k == 3)
                        self.emit("tensor", lambda e, k=k, dc=dc, b=b, par=par,
                                  hk=hk, ds=ds, last=last: e.matmul(
                                      self.bank(b, TT),
                                      wd_s[par][:, k * 1024 + dc * 128:
                                                k * 1024 + (dc + 1) * 128],
                                      h_t[hk],
                                      start=(ds == 0 and k == 0), stop=last),
                                  inc="te" if (last or strip_tail) else None)
                    if ds == NSD - 1:
                        dc_te[dc] = self.cnt["te"]
                wd_cons[par] = self.cnt["te"]
            last_down_te = self.cnt["te"]
            prev_v = getattr(self, "_p4_prev_v", 0)
            prev_st = getattr(self, "_p4_prev_st", 0)
            for dc in range(NCH):
                b = dc
                scr = self.SCR[:, 0:TT]
                if prev_v > 0:
                    self.wait("sync", "ve", prev_v)
                lx = self.dma("sync", xw,
                              dt["x"][dc * P:(dc + 1) * P, c0:c0 + TT], "sy")
                self.wait("scalar", "te", dc_te[dc])
                if prev_v > 0:
                    self.wait("scalar", "ve", prev_v)
                s_tt = self.emit("scalar", lambda e, b=b: e.activation(
                    scr, self.bank(b, TT), AF.Silu, scale=1.0 / 127.0),
                    inc="se")
                self.wait("vector", "se", s_tt)
                if prev_st > 0:
                    self.wait("vector", "so", prev_st)
                self.emit("vector", lambda e, dc=dc, c0=c0: e.tensor_tensor(
                    self.STAT[:, 0:TT], scr, s_t[dc][:, c0:c0 + TT], ALU.mult),
                    inc="ve")
                self.wait("vector", "sy", lx)
                self.drain("vector")
                v_t = self.emit("vector", lambda e: e.tensor_tensor(
                    self.STAT[:, 0:TT], self.STAT[:, 0:TT], xw, ALU.add),
                    inc="ve")
                self.drain("vector")
                prev_v = v_t
                self.release_bank(b, "ve", v_t)
                self.wait("sync", "ve", v_t)
                prev_st = self.dma("sync",
                                   self.x1_sp[dc * P:(dc + 1) * P, c0:c0 + TT],
                                   self.STAT[:, 0:TT], "so")
                self._p4_prev_v = prev_v
                self._p4_prev_st = prev_st
            prev_evac_v = prev_v
        self.drain("vector")

    def make_p5_tasks(self):
        dt = self.dt
        specs = [("wg1T", D, L1_HID), ("wu1T", D, L1_HID),
                 ("wd1T", L1_HID, D)]
        fw = [self.FW[:, 5632:6656], self.FW[:, 6656:7680]]
        self.p5_acc = [self.SMALL[:, 2 + j:3 + j] for j in range(3)]
        red = self.SMALL[:, 5:6]
        tl = []
        for j, (name, A, B) in enumerate(specs):
            srcd = dt[name]
            for a in range(A // P):
                for u in range(B // 1024):
                    tl.append((j, srcd[a * P:(a + 1) * P,
                                       u * 1024:(u + 1) * 1024]))
        for j in range(3):
            self.emit("vector", lambda e, j=j: e.memset(self.p5_acc[j], 0.0),
                      inc="ve")
        self.drain("vector")
        st = {}

        def issue(i):
            self.wait("sync", "ve", self.cnt["ve"])
            sm = "pA" if i % 2 == 0 else "pB"
            self.wait("sync", sm, self.cnt[sm])
            st[i] = (sm, self.dma("sync", fw[i % 2], tl[i][1], sm))
        issue(0)
        issue(1)

        def task(i):
            j = tl[i][0]
            self.wait("vector", *st.pop(i))
            self.emit("vector", lambda e, i=i: e.reduce_sum(
                red, fw[i % 2], axis=AX.X, apply_absolute_value=True),
                inc="ve")
            self.drain("vector")
            self.emit("vector", lambda e, j=j: e.tensor_tensor(
                self.p5_acc[j], self.p5_acc[j], red, ALU.add), inc="ve")
            self.drain("vector")
            if i + 2 < len(tl):
                issue(i + 2)
        return [lambda i=i: task(i) for i in range(len(tl))]

    def P5_wquant(self):
        for en in ("vector", "scalar"):
            self.wait(en, "so", self.cnt["so"])
        self._elu_prev_v = self.cnt["ve"]
        self.wait("tensor", "ve", self.cnt["ve"])
        self.wait("tensor", "se", self.cnt["se"])
        for q_ in ("gpsimd", "sync"):
            self.wait(q_, "te", self.cnt["te"])
            self.wait(q_, "ve", self.cnt["ve"])
            self.wait(q_, "se", self.cnt["se"])
        self.wait("sync", "xq", self.cnt["xq"])
        if self.last_stage >= 9:
            t = 0
            for i in range(NCH):
                t = self.dma("sync", self.bf(self.XN_O + i * self.T, self.T),
                             self.xq_sp[i * P:(i + 1) * P, :], "wA")
            self._xq_reload_t = t
        while self.p5_tasks:
            self.p5_tasks.pop(0)()
        dt = self.dt
        specs = [("wg1T", self.wgq_sc, D, L1_HID, 0),
                 ("wu1T", self.wuq_sc, D, L1_HID, 1),
                 ("wd1T", self.wdq_sc, L1_HID, D, 2)]
        fw = [self.FW[:, 0:2048], self.FW[:, 2048:4096]]
        self.mclip = self.SMALL[:, 8:11]
        self.thr = self.SMALL[:, 11:14]
        ub = [self.bf(self.QT_O + i * 2048, 2048) for i in range(2)]
        db = [self.bf(self.QT_O + 4096 + i * 2048, 2048) for i in range(2)]
        qb = [self.bf(self.QT_O + 8192 + i * 2048, 2048) for i in range(2)]
        self.wait_all("sync", "ve")
        self.wait_all("sync", "te")
        for name, dst, A, B, j in specs:
            srcd = dt[name]
            tiles = []
            if B >= 2048:
                for a in range(A // P):
                    for u in range(B // 2048):
                        tiles.append((src_ := srcd[a * P:(a + 1) * P,
                                                   u * 2048:(u + 1) * 2048],
                                      dst[a * P:(a + 1) * P,
                                          u * 2048:(u + 1) * 2048], 2048))
            else:
                for a in range(A // P):
                    tiles.append((srcd[a * P:(a + 1) * P, :],
                                  dst[a * P:(a + 1) * P, :], B))
            n = len(tiles)
            lds = {}
            cons = {}
            v_t = self.milestone("vector", "ve")
            self.wait("tensor", "ve", v_t)
            self.use_bank(6)
            te_t = self.emit("tensor", lambda e, j=j: e.transpose(
                self.psum[0:1, 3072:3072 + P], self.p5_acc[j],
                self.IDF[:]), inc="te")
            self.wait("vector", "te", te_t)
            self.emit("vector", lambda e: e.reduce_sum(
                self.STAT[0:1, 8:9], self.psum[0:1, 3072:3072 + P],
                axis=AX.X), inc="ve")
            self.drain("vector")
            self.emit("vector", lambda e, A=A, B=B: e.tensor_scalar(
                self.STAT[0:1, 8:9], self.STAT[0:1, 8:9], 1.0 / (A * B), 1e-5,
                ALU.mult, ALU.max), inc="ve")
            self.drain("vector")
            v_t = self.emit("vector", lambda e: e.tensor_scalar_mul(
                self.STAT[0:1, 9:10], self.STAT[0:1, 8:9], 0.5), inc="ve")
            self.drain("vector")
            self.release_bank(6, "ve", v_t)
            v_t = self.milestone("vector", "ve")
            self.wait("tensor", "ve", v_t)
            self.use_bank(7)
            te_t = self.emit("tensor", lambda e: e.matmul(
                self.psum[:, 3584:3586], self.ONEC[0:1, :],
                self.STAT[0:1, 8:10], start=True, stop=True), inc="te")
            mc = self.mclip[:, j:j + 1]
            th = self.thr[:, j:j + 1]
            self.wait("vector", "te", te_t)
            self.emit("vector", lambda e, mc=mc: e.tensor_copy(
                mc, self.psum[:, 3584:3585]), inc="ve")
            self.drain("vector")
            v_t = self.emit("vector", lambda e, th=th: e.tensor_copy(
                th, self.psum[:, 3585:3586]), inc="ve")
            self.drain("vector")
            self.release_bank(7, "ve", v_t)
            lds = {}
            sts = {}
            self.wait("sync", "ve", self.cnt["ve"])
            self.wait("sync", "se", self.cnt["se"])
            self.wait("vector", "oA", self.cnt["oA"])
            self.wait("vector", "oB", self.cnt["oB"])
            self.wait("scalar", "oA", self.cnt["oA"])
            self.wait("scalar", "oB", self.cnt["oB"])
            self.wait("sync", "wA", self.cnt["wA"])
            self.wait("sync", "wB", self.cnt["wB"])
            for i in range(min(2, n)):
                sm = "wA" if i % 2 == 0 else "wB"
                lds[i] = (sm, self.dma("sync", fw[i % 2][:, 0:tiles[i][2]],
                                       tiles[i][0], sm))
            for i in range(n):
                par = i % 2
                cc = tiles[i][2]
                self.wait("scalar", *lds[i])
                if i >= 2:
                    self.wait("scalar", "ve", cons[i - 2])
                s_tt = self.emit("scalar", lambda e, par=par, cc=cc:
                                 e.activation(ub[par][:, 0:cc],
                                              fw[par][:, 0:cc], AF.Sign),
                                 inc="se")
                self.wait("vector", *lds[i])
                self.emit("vector", lambda e, par=par, cc=cc:
                          e.scalar_tensor_tensor(db[par][:, 0:cc],
                                                 fw[par][:, 0:cc], -1.0,
                                                 fw[par][:, 0:cc],
                                                 ALU.mult, ALU.max), inc="ve")
                self.drain("vector")
                self.emit("vector", lambda e, par=par, cc=cc, th=th:
                          e.tensor_single_scalar(db[par][:, 0:cc],
                                                 db[par][:, 0:cc], th,
                                                 ALU.is_ge), inc="ve")
                self.wait("vector", "se", s_tt)
                if i >= 2:
                    self.wait("vector", sts[i - 2][0], sts[i - 2][1])
                self.drain("vector")
                v_t = self.emit("vector", lambda e, par=par, cc=cc:
                                e.tensor_tensor(qb[par][:, 0:cc],
                                                ub[par][:, 0:cc],
                                                db[par][:, 0:cc], ALU.mult),
                                inc="ve")
                cons[i] = v_t
                self.drain("vector")
                self.wait("gpsimd", "ve", v_t)
                smo = "oA" if par == 0 else "oB"
                sts[i] = (smo, self.dma("gpsimd", tiles[i][1],
                                        qb[par][:, 0:cc], smo))
                if i + 2 < n:
                    sm = "wA" if i % 2 == 0 else "wB"
                    self.wait("sync", "ve", v_t)
                    lds[i + 2] = (sm, self.dma(
                        "sync", fw[par][:, 0:tiles[i + 2][2]],
                        tiles[i + 2][0], sm))
        self.drain("vector")
        self.p5_store_done = (self.cnt["oA"], self.cnt["oB"])

    def P6_preln1(self):
        self.wait_all("sync", "so")
        self._elu_prev_v = self.cnt["ve"]
        self.wait("tensor", "ve", self.cnt["ve"])
        self.wait("tensor", "se", self.cnt["se"])
        for q_ in ("gpsimd", "sync"):
            self.wait(q_, "te", self.cnt["te"])
            self.wait(q_, "ve", self.cnt["ve"])
            self.wait(q_, "se", self.cnt["se"])
        self.win_te = {}
        self.win_ve = {}
        self.ln_stats_from_windows(self.x1_sp, write_xn=True)
        self.ln_apply_windows(self.x1_sp, self.xn, absmax=True)
        self.dbg_dump("xn1", list(self.xn), src_sems=("ve",))

    def P7_qkv(self):
        for en in ("vector", "scalar", "tensor"):
            self.wait(en, "oA", self.cnt["oA"])
            self.wait(en, "oB", self.cnt["oB"])
        self._elu_prev_v = self.cnt["ve"]
        self.wait("tensor", "ve", self.cnt["ve"])
        self.wait("tensor", "se", self.cnt["se"])
        for q_ in ("gpsimd", "sync"):
            self.wait(q_, "te", self.cnt["te"])
            self.wait(q_, "ve", self.cnt["ve"])
            self.wait(q_, "se", self.cnt["se"])
        T, TT, NTT = self.T, self.TT, self.NTT
        self.xq_tasks = (self.make_xq_tasks()
                         if self.last_stage >= 9 else [])
        dt = self.dt
        Q = [self.bf(self.QT_O + i * T, T) for i in range(NCH)]
        K = [self.bf(self.W1_O + i * T, T) for i in range(NCH)]
        Vp = self.bf(self.EX_O, 16640)
        ws = self.bf(self.RG_O, 2048)
        NSC = 256
        nst = 3 * D // NSC
        st_cons = 0
        for sidx in range(nst):
            cbase = sidx * NSC
            for _ in range(6):
                if self.p5_tasks:
                    self.p5_tasks.pop(0)()
            if st_cons > 0:
                self.wait("gpsimd", "te", st_cons)
            lds = self.dma("gpsimd", ws.rearrange("p (k m) -> p k m", k=NCH),
                           dt["wqkvT"][:, cbase:cbase + NSC].rearrange(
                               "(k p) m -> p k m", p=P), "gv")
            self.wait("tensor", "gv", lds)
            kind = cbase // D
            moff = (cbase % D)
            if kind < 2:
                dstt = Q if kind == 0 else K
                for mm in range(NSC // P):
                    m = moff // P + mm
                    b = [4, 5][mm % 2]
                    for tt in range(NTT):
                        c0 = tt * TT
                        self.use_bank(b)
                        for k in range(NCH):
                            last = (k == NCH - 1)
                            self.emit("tensor", lambda e, k=k, mm=mm, b=b,
                                      c0=c0, last=last: e.matmul(
                                          self.bank(b, TT),
                                          ws[:, k * NSC + mm * P:
                                             k * NSC + (mm + 1) * P],
                                          self.xn[k][:, c0:c0 + TT],
                                          start=(k == 0), stop=last),
                                      inc="te" if last else None)
                        te_t = self.cnt["te"]
                        if tt == NTT - 1:
                            st_cons = te_t
                        self.wait("vector", "te", te_t)
                        v_t = self.emit("vector", lambda e, m=m, c0=c0,
                                        dstt=dstt, b=b: e.tensor_copy(
                                            dstt[m][:, c0:c0 + TT],
                                            self.bank(b, TT)), inc="ve")
                        self.drain("vector")
                        self.release_bank(b, "ve", v_t)
            else:
                for tj in range(self.NTC):
                    b = [4, 5][tj % 2]
                    self.use_bank(b)
                    for k in range(NCH):
                        last = (k == NCH - 1)
                        self.emit("tensor", lambda e, k=k, tj=tj, b=b,
                                  last=last: e.matmul(
                                      self.bank(b, NSC),
                                      self.xn[k][:, tj * P:(tj + 1) * P],
                                      ws[:, k * NSC:(k + 1) * NSC],
                                      start=(k == 0), stop=last),
                                  inc="te" if last else None)
                    te_t = self.cnt["te"]
                    if tj == self.NTC - 1:
                        st_cons = te_t
                    self.wait("vector", "te", te_t)
                    h0 = moff // HD
                    v_t = self.emit("vector", lambda e, tj=tj, h0=h0, b=b:
                                    e.tensor_copy(
                                        Vp[:, tj * 1040 + h0 * 65:]
                                        [:, 0:4 * 65].rearrange(
                                            "p (h d) -> p h d", h=4)
                                        [:, :, 0:64],
                                        self.bank(b, NSC).rearrange(
                                            "p (h d) -> p h d", h=4)),
                                    inc="ve")
                    self.drain("vector")
                    self.release_bank(b, "ve", v_t)
        self.emit("vector", lambda e: e.memset(
            Vp.rearrange("p (t h d) -> p t h d", t=16, h=16)[:, :, :, 64:65],
            1.0), inc="ve")
        self.drain("vector")
        self.Q, self.K, self.Vp = Q, K, Vp
        self.dbg_dump("q1", list(Q), src_sems=("ve",))
        self.dbg_dump("k1", list(K), src_sems=("ve",))

    def make_xq_tasks(self):
        T, TT, NTT = self.T, self.TT, self.NTT
        win = [self.FW[:, i * TT:(i + 1) * TT] for i in range(NCH)]
        scr = self.FW[:, 4096:4096 + TT]
        tq_ = self.FW[:, 4608:4608 + TT]
        rec = self.FW[:, 5120:5120 + TT]
        xr = self.bf(self.WVS_O + 768, TT)
        xg = self.bf(self.WVS_O + 1280, TT)
        st = {"w": 0}

        def issue(tt):
            self.wait("sync", "ve", self.cnt["ve"])
            self.wait("sync", "xq", self.cnt["xq"])
            t = 0
            for i in range(NCH):
                t = self.dma("sync", win[i],
                             self.x1_sp[i * P:(i + 1) * P,
                                        tt * TT:(tt + 1) * TT], "wA")
            st["w"] = t

        issue(0)

        def task(tt, i):
            c0 = tt * TT
            w = win[i]
            if i == 0:
                self.wait("vector", "wA", st["w"])
                self.emit("vector", lambda e: e.reciprocal(
                    rec, self.SXR[:, c0:c0 + TT]), inc="ve")
                self.drain("vector")
            self.emit("vector", lambda e: e.tensor_tensor(
                scr, w, self.MU[:, c0:c0 + TT], ALU.subtract), inc="ve")
            self.drain("vector")
            self.emit("vector", lambda e: e.tensor_tensor(
                scr, scr, self.RS[:, c0:c0 + TT], ALU.mult), inc="ve")
            self.drain("vector")
            self.emit("vector", lambda e: e.scalar_tensor_tensor(
                tq_, scr, 127.0, rec, ALU.mult, ALU.mult), inc="ve")
            self.drain("vector")
            self.emit("vector", lambda e: e.tensor_scalar_add(
                tq_, tq_, 128.5), inc="ve")
            self.drain("vector")
            self.emit("vector", lambda e: e.tensor_scalar(
                xr, tq_, 128.0, 128.0, ALU.is_ge, ALU.mult), inc="ve")
            self.drain("vector")
            for sbit in (64.0, 32.0, 16.0, 8.0, 4.0, 2.0, 1.0):
                self.emit("vector", lambda e, sbit=sbit:
                          e.scalar_tensor_tensor(
                              xg, xr, sbit, tq_, ALU.add, ALU.is_le),
                          inc="ve")
                self.drain("vector")
                self.emit("vector", lambda e, sbit=sbit:
                          e.scalar_tensor_tensor(
                              xr, xg, sbit, xr, ALU.mult, ALU.add),
                          inc="ve")
                self.drain("vector")
            self.emit("vector", lambda e: e.tensor_scalar_add(
                w, xr, -128.0), inc="ve")
            v_t = self.milestone("vector", "ve")
            self.wait("gpsimd", "ve", v_t)
            self.dma("gpsimd", self.xq_sp[i * P:(i + 1) * P, c0:c0 + TT],
                     w, "xq")
            if i == NCH - 1 and tt + 1 < NTT:
                issue(tt + 1)

        return [lambda tt=tt, i=i: task(tt, i)
                for tt in range(NTT) for i in range(NCH)]

    def P8_l1_attn(self):
        self._elu_prev_v = self.cnt["ve"]
        self.wait("tensor", "ve", self.cnt["ve"])
        self.wait("tensor", "se", self.cnt["se"])
        for q_ in ("gpsimd", "sync"):
            self.wait(q_, "te", self.cnt["te"])
            self.wait(q_, "ve", self.cnt["ve"])
            self.wait(q_, "se", self.cnt["se"])
        T, TT, NTT = self.T, self.TT, self.NTT
        Q, K, Vp = self.Q, self.K, self.Vp
        o1 = [self.bf(self.XN_O + i * T, T) for i in range(NCH)]
        ring = [self.bf(self.RG_O + r * 512, 512)[:, 0:TT] for r in range(4)]
        o1tmp = self.bf(self.WVS_O + 256, 512)
        zrow = self.STAT[0:1, 0:TT]
        o1u = self.SCR[0:64, 0:TT]
        NJC = self.NTC
        for ic in range(NTT):
            i0 = ic * TT
            for p in range(NCH):
                avb = [4, 5]
                self.use_bank(4)
                self.use_bank(5)
                ring_v = [0, 0, 0, 0]
                av_info = {}

                def emit_av(j, p=p):
                    for lane in range(2):
                        rsl, s_tt = av_info.pop((j, lane))
                        self.wait("tensor", "se", s_tt)
                        self.emit("tensor", lambda e, j=j, rsl=rsl,
                                  lane=lane, p=p: e.matmul(
                                      self.bank(avb[lane], TT,
                                                p0=0, p1=65),
                                      Vp[:, j * 1040 + (2 * p + lane) * 65:
                                         j * 1040 + (2 * p + lane) * 65 + 65],
                                      ring[rsl],
                                      start=(j == 0), stop=(j == NJC - 1)),
                                  inc="te")
                        ring_v[rsl] = self.cnt["te"]
                for jc in range(NJC):
                    for lane in range(2):
                        bsl = [0, 1][jc % 2] if lane == 0 else [2, 3][jc % 2]
                        self.use_bank(bsl)
                        r0 = lane * 64
                        self.emit("tensor", lambda e, p=p, jc=jc, r0=r0,
                                  bsl=bsl, i0=i0: e.matmul(
                                      self.bank(bsl, TT),
                                      K[p][r0:r0 + 64, jc * P:(jc + 1) * P],
                                      Q[p][r0:r0 + 64, i0:i0 + TT],
                                      start=True, stop=True,
                                      tile_position=(r0, 0)), inc="te")
                        te_t = self.cnt["te"]
                        rsl = (jc % 2) * 2 + lane
                        if ring_v[rsl] > 0:
                            self.wait("scalar", "te", ring_v[rsl])
                        self.wait("scalar", "te", te_t)
                        s_tt = self.emit("scalar", lambda e, bsl=bsl, rsl=rsl:
                                         e.activation(ring[rsl],
                                                      self.bank(bsl, TT),
                                                      AF.Exp, scale=0.125),
                                         inc="se")
                        self.release_bank(bsl, "se", s_tt)
                        av_info[(jc, lane)] = (rsl, s_tt)
                    if jc > 0:
                        emit_av(jc - 1)
                emit_av(NJC - 1)
                te_t = self.cnt["te"]
                for lane in range(2):
                    b = avb[lane]
                    zbank = 6 + lane
                    self.wait("vector", "te", te_t)
                    self.emit("vector", lambda e, b=b: e.reciprocal(
                        zrow, self.bank(b, TT, p0=64, p1=65)), inc="ve")
                    v_t = self.milestone("vector", "ve")
                    self.wait("tensor", "ve", v_t)
                    self.use_bank(zbank)
                    tz = self.emit("tensor", lambda e, zbank=zbank: e.matmul(
                        self.bank(zbank, TT, p0=0, p1=64),
                        self.ONEC[0:1, 0:64], zrow,
                        start=True, stop=True), inc="te")
                    pv = getattr(self, "_p8_prev_o1u", 0)
                    self.wait("scalar", "te", te_t)
                    if pv > 0:
                        self.wait("scalar", "ve", pv)
                    s_t = self.emit("scalar", lambda e, b=b: e.copy(
                        o1u, self.bank(b, TT, p0=0, p1=64)), inc="se")
                    if lane == 0:
                        dst = o1[p][0:64, i0:i0 + TT]
                    else:
                        dst = o1tmp[0:64, 0:TT]
                        self.wait("vector", "sb2",
                                  getattr(self, "p8_o1dma", 0))
                    self.wait("vector", "te", tz)
                    self.wait("vector", "se", s_t)
                    v_t = self.emit("vector", lambda e, dst=dst, zbank=zbank:
                                    e.tensor_tensor(
                                        dst, o1u,
                                        self.bank(zbank, TT, p0=0, p1=64),
                                        ALU.mult), inc="ve")
                    self.drain("vector")
                    self._p8_prev_o1u = v_t
                    self.release_bank(b, "ve", v_t)
                    self.release_bank(zbank, "ve", v_t)
                    if lane == 1:
                        self.wait("sync", "ve", v_t)
                        self.p8_o1dma = self.dma(
                            "sync", o1[p][64:128, i0:i0 + TT],
                            o1tmp[0:64, 0:TT], "sb2")
                if self.xq_tasks:
                    self.xq_tasks.pop(0)()
                if self.p5_tasks:
                    self.p5_tasks.pop(0)()
        while self.xq_tasks:
            self.xq_tasks.pop(0)()
        self.o1 = o1
        self.dbg_dump("o1", list(o1), src_sems=("ve", "sy"))

    def P8b_wout(self):
        self._elu_prev_v = self.cnt["ve"]
        self.wait("tensor", "ve", self.cnt["ve"])
        self.wait("tensor", "se", self.cnt["se"])
        for q_ in ("gpsimd", "sync"):
            self.wait(q_, "te", self.cnt["te"])
            self.wait(q_, "ve", self.cnt["ve"])
            self.wait(q_, "se", self.cnt["se"])
        T, TT, NTT = self.T, self.TT, self.NTT
        dt = self.dt
        o1 = self.o1
        wo = self.bf(self.QT_O, 8192)
        ldw = self.dma("gpsimd", wo.rearrange("p (k m) -> p k m", k=NCH),
                       dt["woutT"].rearrange("(k p) m -> p k m", p=P), "gd")
        self.wait("tensor", "gd", ldw)
        self.wait_all("tensor", "sb2")
        s1 = [self.bf(self.W1_O + i * T, T) for i in range(NCH)]

        def w_writer(c0):
            def w(m, ps, te_t):
                self.wait("vector", "te", te_t)
                v_t = self.emit("vector", lambda e, m=m: e.tensor_copy(
                    s1[m][:, c0:c0 + TT], ps), inc="ve")
                self.drain("vector")
                self.wait("sync", "ve", v_t)
                self.dma("sync", self.s1_sp[m * P:(m + 1) * P, c0:c0 + TT],
                         s1[m][:, c0:c0 + TT], "wB")
                return ("ve", v_t)
            return w
        for tt in range(NTT):
            c0 = tt * TT
            self.proj_fm(lambda k, m: wo[:, k * 1024 + m * 128:
                                         k * 1024 + (m + 1) * 128],
                         o1, c0, TT, w_writer(c0), banks=[4, 5])
        self.s1 = s1
        self.dbg_dump("s1", list(s1), src_sems=("ve",))

    def P9_l1_ffn(self):
        self.wait_all("sync", "so")
        self._elu_prev_v = self.cnt["ve"]
        self.wait("tensor", "ve", self.cnt["ve"])
        self.wait("tensor", "se", self.cnt["se"])
        for q_ in ("gpsimd", "sync"):
            self.wait(q_, "te", self.cnt["te"])
            self.wait(q_, "ve", self.cnt["ve"])
            self.wait(q_, "se", self.cnt["se"])
        T, TT, NTT = self.T, self.TT, self.NTT
        dt = self.dt
        xq = [self.bf(self.XN_O + i * T, T) for i in range(NCH)]
        self.wait("sync", "wB", self.cnt["wB"])
        t = self._xq_reload_t
        self.wait("tensor", "wA", t)
        self.wait("gpsimd", "wA", t)
        self.dbg_dump("xq", list(xq), src_sems=("wA",))
        TQ = 256
        NTQ = T // TQ
        NH1 = L1_HID // P
        HQ = NH1 * TQ
        hpt = [self.FW[:, i * TQ:(i + 1) * TQ] for i in range(NH1)]
        hqb = [[self.bf(self.EX_O + h * HQ + i * TQ, TQ) for i in range(NH1)]
               for h in range(2)]
        g_s = [self.bf(self.QT_O + i * 4096, 4096) for i in range(2)]
        u_s = [self.bf(self.QT_O + 8192 + i * 4096, 4096) for i in range(2)]
        d_s = [self.bf(self.W1_O + i * 8192, 8192) for i in range(2)]
        QW = min(2048, NH1 * TQ)
        rbufs = [self.bf(self.RG_O, QW), self.bf(self.WVS_O + 2048, QW)]
        gbuf = self.bf(self.WVS_O, QW)
        s1w = self.bf(self.RG_O + 2048, TQ)
        cg = self.STAT[:, 0:TQ]
        cu = self.STAT[:, 256:512]
        cdb = [self.STAT[:, 512:768],
               self.MU[:, 256:512] if T >= 512 else self.STAT[:, 512:768]]
        rec_h = self.STAT[:, 768:768 + TQ]
        mxh = self.RS[:, 0:TQ]
        mrow = self.MU[0:1, 0:TQ]
        mrow2 = self.MU[0:1, 1024:1024 + TQ]
        rfrow = self.MU[0:1, 1280:1280 + TQ]
        scrq = self.SCR[:, 0:TQ]
        scr2 = self.SCR[:, 256:512]
        sl_g = [self.SCR[:, 0:TQ], self.MU[:, 512:768]]
        sl_s = [self.SCR[:, 256:512], self.MU[:, 768:1024]]
        vab = self.MU[:, 1536:1536 + TQ]
        ug2 = [self.MU[:, 1792:1792 + TQ], self.RS[:, TQ:2 * TQ]]
        cu0 = self.STAT[0:1, 256:256 + TQ]
        NS = L1_HID // 512
        g_cons = [0, 0]
        d_cons = [0, 0]
        hq_done = [0, 0]
        dmm_done = [0, 0]
        rb_free = [0, 0]
        hq_part = {}
        pa, pb = getattr(self, "p5_store_done", (0, 0))
        self.wait("sync", "oA", pa)
        self.wait("sync", "oB", pb)

        def emit_down_strip(tqp, ds, hq_wait=None):
            par = ds % 2
            smp = "eA" if par == 0 else "eB"
            hqp = hqb[tqp % 2]
            if d_cons[par] > 0:
                self.wait("sync", "te", d_cons[par])
            ldd = self.dma("sync",
                           d_s[par].rearrange("p (k m) -> p k m", k=NCH),
                           self.wdq_sc[ds * 1024:(ds + 1) * 1024, :].rearrange(
                               "(k p) m -> p k m", p=P), smp)
            if hq_wait is not None:
                self.wait("tensor", "se", hq_wait)
            elif ds == 0:
                self.wait("tensor", "se", hq_done[tqp % 2])
            self.wait("tensor", smp, ldd)
            for dc in range(NCH):
                b = 4 + dc // 2
                off = (dc % 2) * TQ
                if ds == 0 and dc % 2 == 0:
                    self.use_bank(b)
                for k in range(NCH):
                    hk = ds * NCH + k
                    first = (ds == 0 and k == 0 and dc % 2 == 0)
                    last = (ds == 3 and k == NCH - 1 and dc % 2 == 1)
                    self.emit("tensor", lambda e, k=k, dc=dc, b=b, off=off,
                              par=par, hk=hk, first=first, last=last,
                              hqp=hqp: e.matmul(
                                  self.bank(b, TQ, off=off),
                                  d_s[par].rearrange(
                                      "p (k m) -> p k m", k=NCH)[:, k, :]
                                  [:, dc * 128:(dc + 1) * 128],
                                  hqp[hk],
                                  start=first, stop=last),
                              inc="te" if k == NCH - 1 else None)
            d_cons[par] = self.cnt["te"]
            if ds == 3:
                dmm_done[tqp % 2] = self.cnt["te"]

        def emit_down_evac(tqp):
            c0 = tqp * TQ
            cdq = cdb[tqp % 2]
            te_t = self.cnt["te"]
            prev_fin = self.cnt["ve"]
            for dc in range(NCH):
                b = 4 + dc // 2
                off = (dc % 2) * TQ
                self.wait("sync", "ve", prev_fin)
                lx = self.dma("sync", scr2,
                              self.x1_sp[dc * P:(dc + 1) * P, c0:c0 + TQ],
                              "sy")
                ls = self.dma("sync", s1w,
                              self.s1_sp[dc * P:(dc + 1) * P, c0:c0 + TQ],
                              "sy")
                self.wait("vector", "te", te_t)
                self.emit("vector", lambda e, b=b, off=off, cdq=cdq:
                          e.tensor_tensor(
                              scrq, self.bank(b, TQ, off=off), cdq, ALU.mult),
                          inc="ve")
                v_t = self.milestone("vector", "ve")
                if dc % 2 == 1:
                    self.release_bank(b, "ve", v_t)
                self.wait("scalar", "ve", v_t)
                self.wait("scalar", "so", self.cnt["so"])
                s_tt = self.emit("scalar", lambda e: e.activation(
                    scrq, scrq, AF.Silu), inc="se")
                self.wait("vector", "se", s_tt)
                self.wait("vector", "sy", ls)
                self.emit("vector", lambda e: e.tensor_tensor(
                    scrq, scrq, s1w, ALU.mult), inc="ve")
                self.drain("vector")
                if "upd" in self.dbg_names:
                    v_t = self.milestone("vector", "ve")
                    self.wait("sync", "ve", v_t)
                    dt_ = self.dma("sync",
                                   self.dbg_upd[dc * P:(dc + 1) * P,
                                                c0:c0 + TQ], scrq, "dbg2")
                    self.wait("vector", "dbg2", dt_)
                v_t = self.emit("vector", lambda e: e.tensor_tensor(
                    scrq, scrq, scr2, ALU.add), inc="ve")
                self.drain("vector")
                prev_fin = v_t
                self.wait("sync", "ve", v_t)
                self.dma("sync", self.out[dc * P:(dc + 1) * P, c0:c0 + TQ],
                         scrq, "so")
                self.wait("vector", "so", self.cnt["so"])

        self._p9_vp = [0, 0]
        for tq in range(NTQ):
            c0 = tq * TQ
            if tq == 0:
                self.wait("sync", "te", self.cnt["te"])
            self.emit("vector", lambda e, c0=c0: e.tensor_scalar(
                cg, self.SXR[:, c0:c0 + TQ], self.mclip[:, 0:1], 1.0 / 127.0,
                ALU.mult, ALU.mult), inc="ve")
            self.emit("vector", lambda e, c0=c0: e.tensor_scalar(
                cu0, self.SXR[0:1, c0:c0 + TQ], self.mclip[0:1, 1:2],
                1.0 / 127.0, ALU.mult, ALU.mult), inc="ve")
            self.drain("vector")
            self.emit("vector", lambda e: e.memset(mxh, 0.0), inc="ve")
            self.drain("vector")
            pend = {}

            def flush_h(hh):
                s_tt = pend.pop(hh)
                self.drain("vector")
                self.wait("vector", "se", s_tt)
                vp = self.emit("vector", lambda e, hh=hh:
                               e.tensor_tensor(
                                   hpt[hh], ug2[hh % 2],
                                   sl_s[hh % 2], ALU.mult), inc="ve")
                self._p9_vp[hh % 2] = vp
                self.drain("vector")
                self.emit("vector", lambda e, hh=hh: e.scalar_tensor_tensor(
                    vab, hpt[hh], -1.0, hpt[hh], ALU.mult, ALU.max),
                    inc="ve")
                self.drain("vector")
                self.emit("vector", lambda e: e.tensor_tensor(
                    mxh, mxh, vab, ALU.max), inc="ve")
                self.drain("vector")
            for hs in range(NS):
                par = hs % 2
                smp = "sA" if par == 0 else "sB"
                if g_cons[par] > 0:
                    self.wait("sync", "te", g_cons[par])
                ldg = self.dma("sync",
                               g_s[par].rearrange("p (k m) -> p k m", k=NCH),
                               self.wgq_sc[:, hs * 512:(hs + 1) * 512].rearrange(
                                   "(k p) m -> p k m", p=P), smp)
                ldu = self.dma("sync",
                               u_s[par].rearrange("p (k m) -> p k m", k=NCH),
                               self.wuq_sc[:, hs * 512:(hs + 1) * 512].rearrange(
                                   "(k p) m -> p k m", p=P), smp)
                self.wait("tensor", smp, ldu)
                for mh in range(4):
                    hh = hs * 4 + mh
                    bg, bu = [0, 1][mh % 2], [2, 3][mh % 2]
                    self.use_bank(bg)
                    self.use_bank(bu)
                    for k in range(NCH):
                        last = (k == NCH - 1)
                        self.emit("tensor", lambda e, k=k, mh=mh, bg=bg,
                                  par=par, c0=c0, last=last: e.matmul(
                                      self.bank(bg, TQ),
                                      g_s[par][:, k * 512 + mh * 128:
                                               k * 512 + (mh + 1) * 128],
                                      xq[k][:, c0:c0 + TQ],
                                      start=(k == 0), stop=last),
                                  inc="te" if last else None)
                        self.emit("tensor", lambda e, k=k, mh=mh, bu=bu,
                                  par=par, c0=c0, last=last: e.matmul(
                                      self.bank(bu, TQ),
                                      u_s[par][:, k * 512 + mh * 128:
                                               k * 512 + (mh + 1) * 128],
                                      xq[k][:, c0:c0 + TQ],
                                      start=(k == 0), stop=last),
                                  inc="te" if last else None)
                    te_t = self.cnt["te"]
                    self.wait("scalar", "te", te_t)
                    if self._p9_vp[hh % 2] > 0:
                        self.wait("scalar", "ve", self._p9_vp[hh % 2])
                    cp = self.emit("scalar", lambda e, hh=hh, bu=bu: e.copy(
                        ug2[hh % 2], self.bank(bu, TQ)), inc="se")
                    self.release_bank(bu, "se", cp)
                    self.wait("vector", "te", te_t)
                    vg = self.emit("vector", lambda e, bg=bg, hh=hh:
                                   e.tensor_tensor(
                                       sl_g[hh % 2], self.bank(bg, TQ),
                                       cg, ALU.mult), inc="ve")
                    self.release_bank(bg, "ve", vg)
                    self.wait("scalar", "ve", vg)
                    s_tt = self.emit("scalar", lambda e, hh=hh: e.activation(
                        sl_s[hh % 2], sl_g[hh % 2], AF.Silu), inc="se")
                    pend[hh] = s_tt
                    if (hh - 1) in pend:
                        flush_h(hh - 1)
                g_cons[par] = self.cnt["te"]
                if tq > 0 and hs % 2 == 1:
                    emit_down_strip(tq - 1, hs // 2)
            for hh in sorted(pend):
                flush_h(hh)
            v_t = self.milestone("vector", "ve")
            self.wait("tensor", "ve", v_t)
            for g in range(2):
                self.use_bank(0)
                te_t = self.emit("tensor", lambda e, g=g: e.transpose(
                    self.bank(0, 128), mxh[:, g * P:(g + 1) * P], self.IDF[:]),
                    inc="te")
                self.wait("vector", "te", te_t)
                v_t = self.emit("vector", lambda e, g=g: e.reduce_max(
                    self.SMALL[:, g:g + 1], self.bank(0, 128), axis=AX.X),
                    inc="ve")
                self.drain("vector")
                self.release_bank(0, "ve", v_t)
            v_t = self.milestone("vector", "ve")
            self.wait("tensor", "ve", v_t)
            self.use_bank(1)
            te_t = self.emit("tensor", lambda e: e.transpose(
                self.psum[0:2, 512:512 + P], self.SMALL[:, 0:2],
                self.IDF[:]), inc="te")
            self.wait("vector", "te", te_t)
            v_t = self.emit("vector", lambda e: e.tensor_copy(
                self.SCR[0:2, 0:P], self.psum[0:2, 512:512 + P]),
                inc="ve")
            self.drain("vector")
            self.release_bank(1, "ve", v_t)
            self.wait("sync", "ve", v_t)
            d_t = self.dma("sync",
                           mrow.rearrange("o (g f) -> o g f", g=2),
                           self.SCR[0:2, 0:P], "sb2")
            self.wait("vector", "sb2", d_t)
            self.emit("vector", lambda e: e.tensor_tensor(
                mrow2, mrow, cu0, ALU.mult), inc="ve")
            self.drain("vector")
            self.emit("vector", lambda e: e.tensor_single_scalar(
                mrow2, mrow2, 1e-5, ALU.max), inc="ve")
            self.drain("vector")
            self.emit("vector", lambda e: e.reciprocal(rfrow, mrow2),
                      inc="ve")
            self.drain("vector")
            self.emit("vector", lambda e: e.tensor_tensor(
                rfrow, rfrow, cu0, ALU.mult), inc="ve")
            v_t = self.milestone("vector", "ve")
            self.wait("tensor", "ve", v_t)
            self.use_bank(0)
            te_t = self.emit("tensor", lambda e: e.matmul(
                self.bank(0, 2 * TQ), self.ONEC[0:1, :],
                self.MU[0:1, 1024:1024 + 2 * TQ],
                start=True, stop=True), inc="te")
            self.wait("vector", "te", te_t)
            cdq = cdb[tq % 2]
            self.emit("vector", lambda e, cdq=cdq: e.tensor_scalar(
                cdq, self.bank(0, TQ), self.mclip[:, 2:3], 1.0 / 127.0,
                ALU.mult, ALU.mult), inc="ve")
            self.drain("vector")
            v_t = self.emit("vector", lambda e: e.tensor_copy(
                rec_h, self.bank(0, TQ, off=TQ)), inc="ve")
            self.drain("vector")
            self.release_bank(0, "ve", v_t)
            if tq > 0:
                emit_down_evac(tq - 1)
            for i in range(NH1):
                self.emit("vector", lambda e, i=i: e.scalar_tensor_tensor(
                    hpt[i], hpt[i], 127.0, rec_h, ALU.mult, ALU.mult),
                    inc="ve")
            self.drain("vector")
            NQC = NH1 * TQ // QW
            self.wait("scalar", "ve", self.cnt["ve"])
            s_add = {}
            for q in range(NQC):
                s_add[q] = self.emit("scalar", lambda e, q=q: e.activation(
                    self.FW[:, q * QW:(q + 1) * QW],
                    self.FW[:, q * QW:(q + 1) * QW],
                    AF.Identity, bias=128.5), inc="se")
            if dmm_done[tq % 2] > 0:
                self.wait("scalar", "te", dmm_done[tq % 2])
            for q in range(NQC):
                u = self.FW[:, q * QW:(q + 1) * QW]
                hqq = self.bf(self.EX_O + (tq % 2) * HQ + q * QW, QW)
                rb = rbufs[q % 2]
                if rb_free[q % 2] > 0:
                    self.wait("vector", "se", rb_free[q % 2])
                self.wait("vector", "se", s_add[q])
                self.emit("vector", lambda e, u=u, rb=rb: e.tensor_scalar(
                    rb, u, 128.0, 128.0, ALU.is_ge, ALU.mult), inc="ve")
                self.drain("vector")
                for sbit in (64.0, 32.0, 16.0, 8.0, 4.0, 2.0, 1.0):
                    self.emit("vector", lambda e, sbit=sbit, u=u, rb=rb:
                              e.scalar_tensor_tensor(
                                  gbuf, rb, sbit, u, ALU.add, ALU.is_le),
                              inc="ve")
                    self.drain("vector")
                    self.emit("vector", lambda e, sbit=sbit, rb=rb:
                              e.scalar_tensor_tensor(
                                  rb, gbuf, sbit, rb, ALU.mult, ALU.add),
                              inc="ve")
                    self.drain("vector")
                self.wait("scalar", "ve", self.cnt["ve"])
                s_f = self.emit("scalar", lambda e, hqq=hqq, rb=rb:
                                e.activation(hqq, rb, AF.Identity,
                                             bias=-128.0), inc="se")
                rb_free[q % 2] = s_f
                hq_part[q] = s_f
            hq_done[tq % 2] = self.cnt["se"]
            if tq == 2:
                self.dbg_dump("hq0", [h[:, 0:TQ] for h in hqb[0]],
                              src_sems=("se",))
        for ds in range(4):
            emit_down_strip(NTQ - 1, ds,
                            hq_wait=hq_part[min(ds, len(hq_part) - 1)])
        emit_down_evac(NTQ - 1)
        self.drain("vector")



def _prep_inputs(inputs):
    x = np.asarray(inputs["L0_attn_query"])
    n_cores = x.shape[0]
    bias6 = np.stack([
        np.asarray(inputs["L0_attn_bq"]),
        np.asarray(inputs["L0_attn_bk"]),
        np.asarray(inputs["L0_attn_bv"]),
        np.asarray(inputs["L0_attn_bo"]),
        np.asarray(inputs["L0_attn_ln_g"]),
        np.asarray(inputs["L0_attn_ln_b"]),
    ], axis=1).astype(np.float32)
    tr = lambda a: np.ascontiguousarray(np.asarray(a).T)
    shared = {
        "bias6": np.ascontiguousarray(bias6),
        "wqT": tr(inputs["L0_attn_Wq"]),
        "wkT": tr(inputs["L0_attn_Wk"]),
        "wvT": tr(inputs["L0_attn_Wv"]),
        "woT": tr(inputs["L0_attn_Wo"]),
        "wg0T": tr(inputs["L0_ffn_w_gate"]),
        "wu0T": tr(inputs["L0_ffn_w_up"]),
        "wd0T": tr(inputs["L0_ffn_w_down"]),
        "wqkvT": tr(inputs["L1_attn_w_qkv"]),
        "woutT": tr(inputs["L1_attn_w_out"]),
        "wg1T": tr(inputs["L1_ffn_w_gate"]),
        "wu1T": tr(inputs["L1_ffn_w_up"]),
        "wd1T": tr(inputs["L1_ffn_w_down"]),
    }
    in_maps = []
    for c in range(n_cores):
        m = dict(shared)
        m["x"] = np.ascontiguousarray(x[c].T)
        in_maps.append(m)
    return in_maps


def kernel(**inputs):
    in_maps = _prep_inputs(inputs)
    b = Builder(T=2048, last_stage=99)
    nc = b.build()
    res = run_bass_kernel_spmd(nc, in_maps, core_ids=list(range(8)))
    outs = [np.ascontiguousarray(r["out"].T) for r in res.results]
    return np.stack(outs, axis=0).astype(np.float32)
